# version 17
# baseline (speedup 1.0000x reference)
import numpy as np

import concourse.bass as bass
import concourse.mybir as mybir
from concourse.bass_utils import run_bass_kernel_spmd

F32 = mybir.dt.float32
BF16 = mybir.dt.bfloat16
AF = mybir.ActivationFunctionType
ALU = mybir.AluOpType
AX = mybir.AxisListType

NT = 1024
NROWS = 1152
DM = 2048
PROJ = 8512
INTER = 4096
H = 64
CH = 128
NCH = 8
DE = 8192
KD = 16
EPS_LN = 1e-5
EPS_RMS = 1e-5
EPS_Q = 1e-8

(PH_P0, PH_P1, PH_P1B, PH_P2A, PH_CC, PH_PG, PH_E0, PH_P2B, PH_P3M, PH_P3T,
 PH_P4A, PH_P4B, PH_END) = range(13)

SBUF_BUDGET = 212600
ENGINES = ("pe", "act", "dve", "pool", "sp")


def _bf():
    import ml_dtypes
    return ml_dtypes.bfloat16


class Buf:
    __slots__ = ("name", "last_w", "readers", "first_w_op")

    def __init__(self, name):
        self.name = name
        self.last_w = None
        self.readers = []
        self.first_w_op = None


class Op:
    __slots__ = ("eng", "fn", "deps", "inc", "inc_idx", "kind", "tag", "fam")

    def __init__(self, eng, fn, deps, kind, tag):
        self.eng = eng
        self.fn = fn
        self.deps = deps
        self.inc = False
        self.inc_idx = 0
        self.kind = kind
        self.tag = tag


class Builder:
    CHAINED = {"ldc0", "ldc1", "spall", "ccin", "tap"}

    def __init__(self):
        self.ops = {e: [] for e in ENGINES}
        self.nfam = {}
        self.fam_last = {}
        self.wait_all_fams = set()
        self.ncc = 0

    def _deps(self, reads, writes, extra):
        deps = set(extra)
        for b in reads:
            if b.last_w is not None:
                deps.add(b.last_w)
        for b in writes:
            if b.last_w is not None:
                deps.add(b.last_w)
            deps.update(b.readers)
        return deps

    def _rec(self, eng, fn, reads, writes, extra, kind, tag, fam=None):
        deps = self._deps(reads, writes, extra)
        idx = len(self.ops[eng])
        o = Op(eng, fn, deps, kind, tag)
        if kind == "dma":
            o.inc_idx = self.nfam.setdefault(fam, 0)
            self.nfam[fam] += 1
            o.fam = fam
            ev = ("d", fam, o.inc_idx)
        elif kind == "cc":
            o.inc_idx = self.ncc
            self.ncc += 1
            ev = ("c", 0, o.inc_idx)
        else:
            ev = ("e", eng, idx)
        self.ops[eng].append(o)
        for b in reads:
            b.readers.append(ev)
        for b in writes:
            if b.first_w_op is None:
                b.first_w_op = (eng, idx)
            b.last_w = ev
            b.readers = []
        return ev

    def op(self, eng, fn, reads=(), writes=(), extra=(), tag=""):
        return self._rec(eng, fn, reads, writes, extra, "op", tag)

    def dma(self, q, fn, reads=(), writes=(), extra=(), tag="", fam=None):
        extra = set(extra)
        if fam in self.CHAINED and fam in self.fam_last:
            extra.add(self.fam_last[fam])
        ev = self._rec(q, fn, reads, writes, extra, "dma", tag, fam=fam)
        self.fam_last[fam] = ev
        return ev

    def cc(self, fn, reads=(), writes=(), extra=(), tag="cc"):
        return self._rec("pool", fn, reads, writes, extra, "cc", tag)

    def finalize(self, block, sems, dma_sems, cc_sem):
        for eng in ENGINES:
            for o in self.ops[eng]:
                for (k, src, idx) in o.deps:
                    if k == "e":
                        if src == "pe" and eng == "pe":
                            continue
                        self.ops[src][idx].inc = True
        for eng in ENGINES:
            n = 0
            for o in self.ops[eng]:
                if o.kind == "op" and o.inc:
                    n += 1
                    o.inc_idx = n

        ops = self.ops

        nfam = self.nfam
        wa = self.wait_all_fams

        def wait_value(ev):
            k, src, idx = ev
            if k == "e":
                o = ops[src][idx]
                assert o.inc, f"dep on non-inc op {src}:{idx}:{o.tag}"
                return ("e", src), o.inc_idx
            if k == "d":
                n = nfam[src] if src in wa else (idx + 1)
                return ("d", src), 16 * n
            return ("c", 0), idx + 1

        def make_body(eng):
            def body(engobj):
                hw = {}
                for o in ops[eng]:
                    for ev in sorted(o.deps, key=lambda t: (t[0], str(t[1]), t[2])):
                        if ev[0] == "e" and ev[1] == eng == "pe":
                            continue
                        if (o.kind == "dma" and ev[0] == "d"
                                and ev[1] == o.fam
                                and o.fam not in self.CHAINED):
                            continue
                        key, val = wait_value(ev)
                        if hw.get(key, 0) >= val:
                            continue
                        if key[0] == "e":
                            engobj.wait_ge(sems[key[1]], val)
                        elif key[0] == "d":
                            engobj.wait_ge(dma_sems[key[1]], val)
                        else:
                            engobj.wait_ge(cc_sem, val)
                        hw[key] = val
                    inst = o.fn(engobj)
                    if o.kind == "dma":
                        inst.then_inc(dma_sems[o.fam], 16)
                    elif o.kind == "cc":
                        inst.then_inc(cc_sem)
                    elif o.inc:
                        inst.then_inc(sems[eng], 1)
            return body

        block.tensor(make_body("pe"))
        block.scalar(make_body("act"))
        block.vector(make_body("dve"))
        block.gpsimd(make_body("pool"))
        block.sync(make_body("sp"))


class Arena:

    def __init__(self, budget):
        self.budget = budget
        self.reqs = []

    def place(self, size, ph0, ph1, name, mloc):
        size = (size + 31) // 32 * 32
        self.reqs.append((size, ph0, ph1, name, mloc))

    def commit(self, base_addr):
        items = []
        for (size, ph0, ph1, name, mloc) in sorted(
                self.reqs, key=lambda r: (-r[0], r[3])):
            taken = sorted((off, off + sz) for (off, sz, a, b, _n) in items
                           if not (b < ph0 or a > ph1))
            off = 0
            for s, e in taken:
                if off + size <= s:
                    break
                off = max(off, e)
            assert off + size <= self.budget, \
                f"SBUF overflow: {name} ({size}B, ph {ph0}-{ph1}) at {off}"
            items.append((off, size, ph0, ph1, name))
            mloc.addr = base_addr + off
        return items


def build_program(taps=(), upto=PH_END):
    bf = _bf()
    nc = bass.Bass("TRN2", target_bir_lowering=False)
    bld = Builder()
    bld.wait_all_fams = set()

    d_x = nc.dram_tensor("x", [NROWS, DM], F32, kind="ExternalInput")
    d_wip = nc.dram_tensor("wip_t", [DM, PROJ], F32, kind="ExternalInput")
    d_wop = nc.dram_tensor("wop_t", [INTER, DM], F32, kind="ExternalInput")
    d_wu = nc.dram_tensor("wu_t", [DM, DE], F32, kind="ExternalInput")
    d_wg = nc.dram_tensor("wg_t", [DM, DE], F32, kind="ExternalInput")
    d_wd = nc.dram_tensor("wd_t", [DE, DM], F32, kind="ExternalInput")
    d_wus = nc.dram_tensor("wu_s", [256, DE], F32, kind="ExternalInput")
    d_wgs = nc.dram_tensor("wg_s", [256, DE], F32, kind="ExternalInput")
    d_wds = nc.dram_tensor("wd_s", [1024, DM], F32, kind="ExternalInput")
    d_cw = nc.dram_tensor("conv_w", [128, 136], F32, kind="ExternalInput")
    d_cb = nc.dram_tensor("conv_b", [128, 34], F32, kind="ExternalInput")
    d_alog = nc.dram_tensor("alog", [64, 1], F32, kind="ExternalInput")
    d_dtb = nc.dram_tensor("dtb", [64, 1], F32, kind="ExternalInput")
    d_drow = nc.dram_tensor("d_row", [1, 64], F32, kind="ExternalInput")
    d_nw = nc.dram_tensor("normw", [128, 32], F32, kind="ExternalInput")
    d_e0m = nc.dram_tensor("e0m", [1, 8], F32, kind="ExternalInput")
    d_out = nc.dram_tensor("out", [NT, DM], F32, kind="ExternalOutput")

    d_sg = nc.dram_tensor("sg_scr", [INTER, NT], BF16)
    d_x1 = nc.dram_tensor("x1_scr", [NT, DM], F32)
    d_da = nc.dram_tensor("dacc_scr", [NT, DM], F32)
    d_ccin = nc.dram_tensor("cc_in", [128, 4100], F32)
    d_ccout = nc.dram_tensor("cc_out", [1024, 4100], F32, addr_space="Shared")

    c_idf = nc.inline_tensor(np.eye(128, dtype=np.float32), "identf")
    c_idb = nc.inline_tensor(np.eye(128, dtype=bf), "identb")
    c_tri = nc.inline_tensor(np.triu(np.ones((128, 128), dtype=bf)), "tri")
    sel_np = np.zeros((64, 64 * 128), dtype=bf)
    for h in range(64):
        sel_np[h, h * 128:(h + 1) * 128] = 1
    c_sel = nc.inline_tensor(sel_np, "selc")
    c_ocb = nc.inline_tensor(np.ones((128, 1), dtype=bf), "onescb")
    c_orf = nc.inline_tensor(np.ones((1, 128), dtype=np.float32), "onesrf")
    c_ocf = nc.inline_tensor(np.ones((128, 1), dtype=np.float32), "onescf")
    c_kc = nc.inline_tensor(np.tile(np.array([[EPS_LN, 1.0]], np.float32), (128, 1)), "kconst")

    ar = Arena(SBUF_BUDGET)
    arena = nc.alloc_sbuf_tensor("arena", [128, SBUF_BUDGET], mybir.dt.uint8)
    arena_addr = nc.lookup_mloc(arena).addr

    S = {}

    def place(name, shape, dtype, ph0, ph1):
        per_part = int(np.prod(shape[1:])) * mybir.dt.size(dtype)
        h = nc.virtual_sbuf_tensor(name, list(shape), dtype)
        nc._tensor(h.name, list(h.shape), h.dtype, type="SB")
        mloc = nc.lookup_mloc(h)
        mloc.allocated = True
        ar.place(per_part, ph0, ph1, name, mloc)
        S[name] = h
        return h

    s_idf = place("s_idf", [128, 128], F32, PH_P0, PH_END)
    s_idb = place("s_idb", [128, 128], BF16, PH_P0, PH_END)
    s_tri = place("s_tri", [128, 128], BF16, PH_P0, PH_P2B)
    s_ocb = place("s_ocb", [128, 1], BF16, PH_P0, PH_END)
    s_orf = place("s_orf", [1, 128], F32, PH_P0, PH_END)
    s_ocf = place("s_ocf", [128, 1], F32, PH_P0, PH_END)
    s_cw = place("s_cw", [128, 136], F32, PH_P0, PH_P1B)
    s_cb = place("s_cb", [128, 34], F32, PH_P0, PH_P1B)
    s_alog = place("s_alog", [64, 1], F32, PH_P0, PH_P1)
    s_dtb = place("s_dtb", [64, 1], F32, PH_P0, PH_P1)
    s_drow = place("s_drow", [1, 64], F32, PH_P0, PH_P2B)
    s_dbc = place("s_dbc", [128, 64], F32, PH_P0, PH_P2B)
    s_nw = place("s_nw", [128, 32], F32, PH_P0, PH_P3M)
    s_e0m = place("s_e0m", [1, 8], F32, PH_P0, PH_E0)
    s_mcols = place("s_mcols", [128, 8], F32, PH_E0, PH_E0)
    s_col = place("s_col", [128, 16], F32, PH_P0, PH_END)
    s_kc = place("s_kc", [128, 2], F32, PH_P0, PH_END)

    s_xlnT = place("s_xlnT", [128, KD * NROWS], BF16, PH_P0, PH_PG)
    s_xs = place("s_xs", [128, NCH * INTER], BF16, PH_P1B, PH_P2B)
    s_yg = place("s_yg", [128, 32 * NT], BF16, PH_P2B, PH_P3M)
    s_E = place("s_E", [128, H * 64], F32, PH_P2A, PH_P2B)
    s_sel = place("s_sel", [64, 64 * 128], BF16, PH_P2B, PH_P2B)

    s_xtm = [place(f"s_xtm{i}", [128, DM], F32, PH_P0, PH_P0) for i in range(2)]
    s_xln = [place(f"s_xln{i}", [128, DM], BF16, PH_P0, PH_P0) for i in range(2)]
    s_sq0 = [place(f"s_sq0{i}", [128, DM], BF16, PH_P0, PH_P0) for i in range(2)]
    s_wbf = [place(f"s_wbf{i}", [128, 2 * DM], BF16, PH_P1, PH_PG) for i in range(2)]
    s_raw = [place(f"s_raw{i}", [128, NROWS], BF16, PH_P1, PH_P1B) for i in range(3)]
    s_cacc = [place(f"s_cacc{i}", [128, NT], F32, PH_P1, PH_P1B) for i in range(2)]
    s_ca4 = [place(f"s_ca4{i}", [128, NT], F32, PH_P1, PH_P1B) for i in range(2)]
    s_hso = [place(f"s_hso{i}", [128, NT], BF16, PH_P1B, PH_P1B) for i in range(3)]
    s_Bfm = place("s_Bfm", [128, NT], BF16, PH_P1, PH_P2B)
    s_Cfm = place("s_Cfm", [128, NT], BF16, PH_P1, PH_P2B)
    s_sgb = [place(f"s_sgb{i}", [128, NT], BF16, PH_PG, PH_P2B) for i in range(2)]

    s_dtfm = place("s_dtfm", [64, NT], F32, PH_P1, PH_P1)
    s_sp1 = place("s_sp1", [64, 512], F32, PH_P1, PH_P1)
    s_sp2 = place("s_sp2", [64, 512], F32, PH_P1, PH_P1)
    s_sp3 = place("s_sp3", [64, 512], F32, PH_P1, PH_P1)
    s_negA = place("s_negA", [64, 1], F32, PH_P1, PH_P1)
    s_adt = place("s_adt", [64, NT], F32, PH_P1, PH_P1)
    s_acum = place("s_acum", [64, NT], F32, PH_P1, PH_P1)
    s_z64 = place("s_z64", [64, 128], F32, PH_P0, PH_P1)
    s_dsf = place("s_dsf", [64, 128], F32, PH_P1, PH_P1)
    s_dsf2 = place("s_dsf2", [64, 128], F32, PH_P1, PH_P1)
    s_lrow = place("s_lrow", [1, 64], F32, PH_P1, PH_P1)
    s_actm = place("s_actm", [128, NCH * 64], F32, PH_P1, PH_P2B)
    s_dttm = place("s_dttm", [128, NCH * 64], F32, PH_P1, PH_P2B)
    s_dsdt = place("s_dsdt", [128, NCH * 64], F32, PH_P1, PH_P2B)
    s_lam = place("s_lam", [128, NCH * 64], F32, PH_P1, PH_P2B)
    s_btm = place("s_btm", [128, NCH * 128], BF16, PH_P1, PH_P2B)
    s_gmt = place("s_gmt", [128, NCH * 128], BF16, PH_P1, PH_P2B)
    s_hi = place("s_hi", [64, NCH * 128], BF16, PH_P1, PH_P2B)
    s_lo = place("s_lo", [64, NCH * 128], BF16, PH_P1, PH_P2B)

    s_bd = [place(f"s_bd{i}", [128, 128], BF16, PH_P2A, PH_P2B) for i in range(4)]
    s_xc = [place(f"s_xc{i}", [128, 64], BF16, PH_P2A, PH_P2B) for i in range(4)]
    s_scr = [place(f"s_scr{i}", [128, 128], F32, PH_P2B, PH_P2B) for i in range(4)]
    s_et = [place(f"s_et{i}", [128, 128], BF16, PH_P2B, PH_P2B) for i in range(4)]
    s_sdt = [place(f"s_sdt{i}", [128, 128], BF16, PH_P2B, PH_P2B) for i in range(4)]
    s_mt = [place(f"s_mt{i}", [128, 128], BF16, PH_P2B, PH_P2B) for i in range(4)]
    s_cd = [place(f"s_cd{i}", [128, 128], BF16, PH_P2B, PH_P2B) for i in range(4)]
    s_ebf = [place(f"s_ebf{i}", [128, 64], BF16, PH_P2B, PH_P2B) for i in range(4)]
    s_di = [place(f"s_di{i}", [128, 128], BF16, PH_P2B, PH_P2B) for i in range(4)]
    s_slot = [place(f"s_slot{i}", [128, 4096], F32, PH_E0, PH_E0) for i in range(2)]

    s_spc = [place(f"s_spc{i}", [128, 2048], F32, PH_P1, PH_P2A) for i in range(2)]
    s_part = place("s_part", [128, 4], F32, PH_P0, PH_CC)
    s_spall = place("s_spall", [128, 24], F32, PH_E0, PH_E0)
    s_sp3r = place("s_sp3r", [128, 3], F32, PH_E0, PH_E0)
    s_srow = place("s_srow", [1, 8], F32, PH_E0, PH_P4B)
    s_bcols = place("s_bcols", [128, 12], F32, PH_E0, PH_P4B)

    s_wopg = [place(f"s_wopg{i}", [128, 4 * DM], BF16, PH_P3M, PH_P3M)
              for i in range(2)]
    s_oacc = place("s_oacc", [128, 8 * DM], F32, PH_P3M, PH_P3T)
    s_rb = place("s_rb", [128, NT], F32, PH_P3M, PH_P3M)
    s_r1 = place("s_r1", [1, 2 * NT], F32, PH_P3M, PH_P3M)
    s_rsum = place("s_rsum", [1, NT], F32, PH_P2B, PH_P3M)
    s_ys = [place(f"s_ys{i}", [128, NT], BF16, PH_P2B, PH_P3M) for i in range(2)]
    s_wops = [place(f"s_wops{i}", [128, DM], F32, PH_P3M, PH_P3M) for i in range(2)]
    s_xrd = [place(f"s_xrd{i}", [128, DM], F32, PH_P3T, PH_P3T) for i in range(2)]
    s_t1 = [place(f"s_t1{i}", [128, DM], F32, PH_P3T, PH_P3T) for i in range(2)]
    s_x1b = [place(f"s_x1b{i}", [128, DM], F32, PH_P3T, PH_P3T) for i in range(2)]
    s_xl2 = [place(f"s_xl2{i}", [128, DM], BF16, PH_P3T, PH_P3T) for i in range(2)]
    s_sqb = [place(f"s_sqb{i}", [128, DM], BF16, PH_P3T, PH_P4B) for i in range(2)]

    s_xl2T = place("s_xl2T", [128, KD * NT], BF16, PH_P3T, PH_P4A)
    s_hid = place("s_hid", [128, 64 * NT], BF16, PH_P4A, PH_P4B)
    s_ust = [place(f"s_ust{i}", [128, 1024], F32, PH_P4A, PH_P4A) for i in range(2)]
    s_gst = [place(f"s_gst{i}", [128, 1024], F32, PH_P4A, PH_P4A) for i in range(2)]
    s_q2 = place("s_q2", [128, 1024], BF16, PH_P4A, PH_P4A)
    s_qu = [place(f"s_qu{i}", [128, 2048], BF16, PH_P4A, PH_P4A) for i in range(2)]
    s_qg = [place(f"s_qg{i}", [128, 2048], BF16, PH_P4A, PH_P4A) for i in range(2)]
    s_hs2 = [place(f"s_hs2{i}", [128, 512], BF16, PH_P4A, PH_P4A) for i in range(2)]
    s_dst = [place(f"s_dst{i}", [128, 512], F32, PH_P4B, PH_P4B) for i in range(3)]
    s_q2d = [place(f"s_q2d{i}", [128, 512], BF16, PH_P4B, PH_P4B) for i in range(2)]
    s_qd = [place(f"s_qd{i}", [128, 512], BF16, PH_P4B, PH_P4B) for i in range(3)]
    s_dev = [place(f"s_dev{i}", [128, 512], F32, PH_P4B, PH_P4B) for i in range(2)]
    s_l3s = place("s_l3s", [128, 16], F32, PH_P4B, PH_P4B)
    s_l3q = place("s_l3q", [128, 16], F32, PH_P4B, PH_P4B)
    s_dsq = [place(f"s_dsq{i}", [128, 512], BF16, PH_P4B, PH_P4B) for i in range(2)]
    s_dar = [place(f"s_dar{i}", [128, DM], F32, PH_P4B, PH_P4B) for i in range(2)]
    s_x1r = [place(f"s_x1r{i}", [128, DM], F32, PH_P4B, PH_P4B) for i in range(2)]
    s_fin = [place(f"s_fin{i}", [128, DM], F32, PH_P4B, PH_P4B) for i in range(2)]

    pb = [nc.alloc_psum_tensor(f"pb{i}", [128, 512], F32) for i in range(8)]
    pbb = [t.ap().bitcast(BF16) for t in pb]

    B = {}

    def g(name):
        if name not in B:
            B[name] = Buf(name)
        return B[name]

    def E(eng, fn, reads=(), writes=(), extra=(), tag=""):
        return bld.op(eng, fn, [g(r) for r in reads], [g(w) for w in writes], extra, tag)

    def D(q, fn, reads=(), writes=(), extra=(), tag="", fam=None):
        return bld.dma(q, fn, [g(r) for r in reads], [g(w) for w in writes],
                       extra, tag, fam=fam or tag)

    tap_t = {}

    def add_tap(name, shape, dtype):
        tap_t[name] = nc.dram_tensor("tap_" + name, list(shape), dtype,
                                     kind="ExternalOutput")

    def tap(name, ap_fn, reads):
        if name in taps:
            D("sp", lambda e, t=tap_t[name], f=ap_fn: e.dma_start(out=t.ap(), in_=f()),
              reads=reads, tag=f"tap_{name}", fam="tap")

    if taps:
        add_tap("dacc", [NT, DM], F32)
        add_tap("l3s", [128, 16], F32)
        add_tap("l3q", [128, 16], F32)
        add_tap("xlnT", [128, KD * NROWS], BF16)
        add_tap("dtfm", [64, NT], F32)
        add_tap("acum", [64, NT], F32)
        add_tap("Bfm", [128, NT], BF16)
        add_tap("Cfm", [128, NT], BF16)
        add_tap("xs", [128, NCH * INTER], BF16)
        add_tap("E", [128, H * 64], F32)
        add_tap("yg", [128, 32 * NT], BF16)
        add_tap("oacc", [128, 8 * DM], F32)
        add_tap("hid", [128, 64 * NT], BF16)
        add_tap("srow", [1, 8], F32)
        add_tap("gmt", [128, NCH * 128], BF16)
        add_tap("hilo", [64, NCH * 128], BF16)
        add_tap("actm", [128, NCH * 64], F32)
        add_tap("dsdt", [128, NCH * 64], F32)
        add_tap("lam", [128, NCH * 64], F32)

    for nm, dd in [("s_idf", c_idf), ("s_idb", c_idb), ("s_tri", c_tri),
                   ("s_ocb", c_ocb), ("s_orf", c_orf), ("s_ocf", c_ocf),
                   ("s_cw", d_cw), ("s_cb", d_cb), ("s_alog", d_alog),
                   ("s_dtb", d_dtb), ("s_drow", d_drow), ("s_nw", d_nw),
                   ("s_e0m", d_e0m), ("s_kc", c_kc)]:
        D("sp", lambda e, s=S[nm], dd=dd: e.dma_start(out=s.ap(), in_=dd.ap()),
          [], [nm], tag="ldc", fam="ldc" + str(len(B) % 2))
    E("dve", lambda e: e.memset(s_z64.ap(), 0.0), [], ["s_z64"], tag="z64")
    E("dve", lambda e: e.memset(s_part[:, 0:3], 0.0),
      [], ["s_part#0", "s_part#1", "s_part#2"], tag="partz")

    E("pe", lambda e: e.matmul(pb[7][:, 0:64], s_orf.ap(), s_drow.ap(),
                               start=True, stop=True),
      ["s_orf", "s_drow"], ["pb7"], tag="dbc_mm")
    E("dve", lambda e: e.tensor_copy(s_dbc.ap(), pb[7][:, 0:64]),
      ["pb7"], ["s_dbc"], tag="dbc_ev")

    def ln_stats(src_ap, src_bufs, sqbuf, sqname, tagp, cb=0):
        Fdim = src_ap.shape[-1]

        def cc_(i):
            return s_col[:, cb + i:cb + i + 1]

        def cn_(i):
            return f"s_col#{cb + i}"

        E("dve", lambda e: e.tensor_reduce(cc_(0), src_ap, AX.X, ALU.add),
          src_bufs, [cn_(0)], tag=tagp + "_s")
        E("act", lambda e: e.activation(sqbuf, src_ap, AF.Square,
                                        accum_out=cc_(1)),
          src_bufs, [sqname, cn_(1)], tag=tagp + "_q")
        E("dve", lambda e: e.tensor_scalar(cc_(2), cc_(0),
                                           1.0 / Fdim, None, ALU.mult),
          [cn_(0)], [cn_(2)], tag=tagp + "_m")
        E("dve", lambda e: e.tensor_tensor(cc_(4), cc_(2), cc_(2), ALU.mult),
          [cn_(2)], [cn_(4)], tag=tagp + "_m2")
        E("dve", lambda e: e.tensor_scalar(cc_(5), cc_(1),
                                           1.0 / Fdim, None, ALU.mult),
          [cn_(1)], [cn_(5)], tag=tagp + "_v1")
        E("dve", lambda e: e.tensor_tensor(cc_(5), cc_(5), cc_(4), ALU.subtract),
          [cn_(4), cn_(5)], [cn_(5)], tag=tagp + "_v2")
        E("act", lambda e: e.activation(cc_(6), cc_(5), AF.Sqrt,
                                        bias=s_kc[:, 0:1]),
          [cn_(5), "s_kc"], [cn_(6)], tag=tagp + "_sd")
        E("dve", lambda e: e.reciprocal(cc_(7), cc_(6)), [cn_(6)], [cn_(7)],
          tag=tagp + "_r")

    for ti in range(9):
        i2 = ti % 2
        xb, xl, sq = s_xtm[i2], s_xln[i2], s_sq0[i2]
        xbn, xln_n, sqn = f"s_xtm{i2}", f"s_xln{i2}", f"s_sq0{i2}"
        D("sp", lambda e, ti=ti, xb=xb: e.dma_start(
            out=xb.ap(), in_=d_x[ti * 128:(ti + 1) * 128, :]), [], [xbn],
          tag=f"ldx{ti}", fam=xbn)
        ln_stats(xb.ap(), [xbn], sq.ap(), sqn, f"ln1_{ti}")
        E("dve", lambda e, xb=xb, xl=xl: e.tensor_scalar(
            xl.ap(), xb.ap(), s_col[:, 2:3], s_col[:, 7:8], ALU.subtract, ALU.mult),
          [xbn, "s_col#2", "s_col#7"], [xln_n], tag=f"ln1a_{ti}")
        for kc in range(KD):
            bk = 6 + (kc % 2)
            E("pe", lambda e, xl=xl, kc=kc, bk=bk: e.transpose(
                pbb[bk][:, 0:128], xl[:, kc * 128:(kc + 1) * 128], s_idb.ap()),
              [xln_n, "s_idb"], [f"pb{bk}"], tag="xt")
            E("dve", lambda e, kc=kc, ti=ti, bk=bk: e.tensor_copy(
                s_xlnT[:, kc * NROWS + ti * 128: kc * NROWS + (ti + 1) * 128],
                pbb[bk][:, 0:128]),
              [f"pb{bk}"], [f"s_xlnT#{kc}"], tag="xte")
    tap("xlnT", lambda: s_xlnT.ap(), [f"s_xlnT#{k}" for k in range(KD)])

    for wi, (dsrc, nrow) in enumerate([(d_wus, 256), (d_wgs, 256), (d_wds, 1024)]):
        ncol = DE if wi < 2 else DM
        rb_, cb_ = nrow // 128, ncol // 2048
        for pi in range(8):
            a, b2 = pi // cb_, pi % cb_
            buf = s_spc[pi % 2]
            bn = f"s_spc{pi % 2}"
            D("act", lambda e, dsrc=dsrc, a=a, b2=b2, buf=buf: e.dma_start(
                out=buf.ap(),
                in_=dsrc[a * 128:(a + 1) * 128, b2 * 2048:(b2 + 1) * 2048]),
              [], [bn], tag=f"scld{wi}_{pi}", fam=bn)
            E("dve", lambda e, buf=buf: e.tensor_reduce(
                s_part[:, 3:4], buf.ap(), AX.X, ALU.add, apply_absolute_value=True),
              [bn], ["s_part#3"], tag=f"scr{wi}_{pi}")
            E("dve", lambda e, wi=wi: e.tensor_tensor(
                s_part[:, wi:wi + 1], s_part[:, wi:wi + 1], s_part[:, 3:4], ALU.add),
              ["s_part#3", f"s_part#{wi}"], [f"s_part#{wi}"], tag=f"sca{wi}_{pi}")

    if upto < PH_P1:
        return _finish(nc, bld, tap_t, ar, arena_addr, B)

    lpair = {}

    def inproj_load(fidxs, pi):
        fws = [64 if f == 66 else 128 for f in fidxs]
        fw = sum(fws)
        f0 = fidxs[0] * 128
        wbf = s_wbf[pi]
        wbfn = f"s_wbf{pi}"
        src = d_wip.ap()[:, f0:f0 + fw].rearrange("(kc kp) f -> kp kc f", kp=128)
        D("pool", lambda e, src=src, wbf=wbf, fw=fw: e.dma_start(
            out=wbf[:, 0:KD * fw].rearrange("p (kc f) -> p kc f", f=fw), in_=src),
          [], [wbfn], tag=f"ipld{fidxs[0]}", fam=wbfn)
        for k, f in enumerate(fidxs):
            lpair[f] = (pi, k * fws[0], fw)

    def emit_inproj(fidx, kind, slab_i=0, tp_thunks=None):
        fw = 64 if fidx == 66 else 128
        pi, sub, pfw = lpair[fidx]
        wbf = s_wbf[pi]
        wbfn = f"s_wbf{pi}"
        halo = kind in ("B", "C", "hs")
        bs = ((fidx % 2) * 2) if kind == "gate" else (0 if fidx % 2 == 0 else 3)
        for kc in range(KD):
            if tp_thunks and kc % 2 == 1:
                tp_thunks.pop(0)()
            st, sp_ = (kc == 0), (kc == KD - 1)
            lsl = slice(kc * pfw + sub, kc * pfw + sub + fw)
            E("pe", lambda e, kc=kc, fw=fw, wbf=wbf, bs=bs, st=st, sp_=sp_,
              lsl=lsl: e.matmul(
                pb[bs][0:fw, 0:512], wbf[:, lsl],
                s_xlnT[:, kc * NROWS + 128: kc * NROWS + 640], start=st, stop=sp_),
              [wbfn, f"s_xlnT#{kc}"], [f"pb{bs}"], tag=f"ipA{fidx}")
            E("pe", lambda e, kc=kc, fw=fw, wbf=wbf, bs=bs, st=st, sp_=sp_,
              lsl=lsl: e.matmul(
                pb[bs + 1][0:fw, 0:512], wbf[:, lsl],
                s_xlnT[:, kc * NROWS + 640: kc * NROWS + 1152], start=st, stop=sp_),
              [wbfn, f"s_xlnT#{kc}"], [f"pb{bs + 1}"], tag=f"ipB{fidx}")
            if halo:
                E("pe", lambda e, kc=kc, fw=fw, wbf=wbf, bs=bs, st=st, sp_=sp_,
                  lsl=lsl: e.matmul(
                    pb[bs + 2][0:fw, 0:128], wbf[:, lsl],
                    s_xlnT[:, kc * NROWS: kc * NROWS + 128], start=st, stop=sp_),
                  [wbfn, f"s_xlnT#{kc}"], [f"pb{bs + 2}"], tag=f"ipH{fidx}")

        if kind == "dt":
            for tt in range(2):
                bk = bs + tt
                pp = pb[bk]
                E("act", lambda e, pp=pp: e.activation(
                    s_sp1.ap(), pp[0:64, 0:512], AF.Abs, bias=s_dtb.ap()),
                  [f"pb{bk}", "s_dtb"], ["s_sp1"], tag="dt_abs")
                E("act", lambda e: e.activation(s_sp2.ap(), s_sp1.ap(), AF.Exp,
                                                scale=-1.0),
                  ["s_sp1"], ["s_sp2"], tag="dt_exp")
                E("act", lambda e: e.activation(s_sp1.ap(), s_sp2.ap(), AF.Ln,
                                                bias=s_kc[0:64, 1:2]),
                  ["s_sp2", "s_kc"], ["s_sp1"], tag="dt_ln")
                E("act", lambda e, pp=pp: e.activation(
                    s_sp3.ap(), pp[0:64, 0:512], AF.Relu, bias=s_dtb.ap()),
                  [f"pb{bk}", "s_dtb"], ["s_sp3"], tag="dt_rel")
                E("dve", lambda e, tt=tt: e.tensor_tensor(
                    s_dtfm[:, tt * 512:(tt + 1) * 512], s_sp1.ap(), s_sp3.ap(),
                    ALU.add),
                  ["s_sp1", "s_sp3"], ["s_dtfm"], tag="dt_add")
            return

        if kind == "gate":
            j = fidx
            sgb = s_sgb[j % 2]
            sgn = f"s_sgb{j % 2}"
            E("act", lambda e, sgb=sgb, bs=bs: e.activation(
                sgb[:, 0:512], pb[bs][0:128, 0:512], AF.Silu),
              [f"pb{bs}"], [sgn], tag=f"g_s0_{j}")
            E("act", lambda e, sgb=sgb, bs=bs: e.activation(
                sgb[:, 512:1024], pb[bs + 1][0:128, 0:512], AF.Silu),
              [f"pb{bs + 1}"], [sgn], tag=f"g_s1_{j}")
            D("sp", lambda e, sgb=sgb, j=j: e.dma_start(
                out=d_sg[j * 128:(j + 1) * 128, :], in_=sgb.ap()),
              [sgn], [f"dsg{j % 2}"], tag=f"g_st_{j}", fam=f"dsg{j % 2}")
            return

        slab = s_raw[slab_i % 3]
        slabn = f"s_raw{slab_i % 3}"
        E("act", lambda e, slab=slab, bs=bs: e.copy(
            slab[:, 0:128], pb[bs + 2][0:128, 0:128]),
          [f"pb{bs + 2}"], [slabn], tag="cv_e0")
        E("dve", lambda e, slab=slab, bs=bs: e.tensor_copy(
            slab[:, 128:640], pb[bs][0:128, 0:512]),
          [f"pb{bs}"], [slabn], tag="cv_e1")
        E("dve", lambda e, slab=slab, bs=bs: e.tensor_copy(
            slab[:, 640:1152], pb[bs + 1][0:128, 0:512]),
          [f"pb{bs + 1}"], [slabn], tag="cv_e2")
        j34 = fidx - 32
        ca, cb2 = s_cacc[0], s_cacc[1]
        a0, a1 = s_ca4[0], s_ca4[1]
        E("act", lambda e, slab=slab, j=j34, a0=a0: e.activation(
            a0.ap(), slab[:, 125:1149], AF.Copy, scale=s_cw[:, j * 4: j * 4 + 1]),
          [slabn, "s_cw"], ["s_ca40"], tag="cv_t0")
        E("act", lambda e, slab=slab, j=j34, a1=a1: e.activation(
            a1.ap(), slab[:, 126:1150], AF.Copy,
            scale=s_cw[:, j * 4 + 1: j * 4 + 2]),
          [slabn, "s_cw"], ["s_ca41"], tag="cv_t1")
        E("pool", lambda e, a0=a0, a1=a1, ca=ca: e.tensor_tensor(
            ca.ap(), a0.ap(), a1.ap(), ALU.add),
          ["s_ca40", "s_ca41"], ["s_cacc0"], tag="cv_p01")
        E("dve", lambda e, slab=slab, j=j34, ca=ca, cb2=cb2: e.scalar_tensor_tensor(
            cb2.ap(), slab[:, 127:1151], s_cw[:, j * 4 + 2: j * 4 + 3], ca.ap(),
            ALU.mult, ALU.add),
          [slabn, "s_cacc0", "s_cw"], ["s_cacc1"], tag="cv_t2")
        E("dve", lambda e, slab=slab, j=j34, ca=ca, cb2=cb2: e.scalar_tensor_tensor(
            ca.ap(), slab[:, 128:1152], s_cw[:, j * 4 + 3: j * 4 + 4], cb2.ap(),
            ALU.mult, ALU.add),
          [slabn, "s_cacc1", "s_cw"], ["s_cacc0"], tag="cv_t3")
        if kind == "B":
            dst, dstn = s_Bfm, "s_Bfm"
        elif kind == "C":
            dst, dstn = s_Cfm, "s_Cfm"
        else:
            dst, dstn = s_hso[slab_i % 3], f"s_hso{slab_i % 3}"
        E("act", lambda e, dst=dst, j=j34, ca=ca: e.activation(
            dst.ap(), ca.ap(), AF.Silu, bias=s_cb[:, j:j + 1]),
          ["s_cacc0", "s_cb"], [dstn], tag=f"cv_si{fidx}")
        if kind == "hs":
            return (dst, dstn, fidx - 32)

    inproj_load([66], 0)
    emit_inproj(66, "dt")
    inproj_load([64, 65], 1)
    emit_inproj(64, "B", 0)
    emit_inproj(65, "C", 1)
    tap("dtfm", lambda: s_dtfm.ap(), ["s_dtfm"])
    tap("Bfm", lambda: s_Bfm.ap(), ["s_Bfm"])
    tap("Cfm", lambda: s_Cfm.ap(), ["s_Cfm"])

    E("act", lambda e: e.activation(s_sp1[:, 0:1], s_alog.ap(), AF.Exp),
      ["s_alog"], ["s_sp1"], tag="negA_e")
    E("dve", lambda e: e.tensor_scalar(s_negA.ap(), s_sp1[:, 0:1], -1.0, None,
                                       ALU.mult),
      ["s_sp1"], ["s_negA"], tag="negA_m")
    E("dve", lambda e: e.tensor_scalar(s_adt.ap(), s_dtfm.ap(), s_negA.ap(), None,
                                       ALU.mult),
      ["s_dtfm", "s_negA"], ["s_adt"], tag="adt")

    inproj_load([32, 33], 0)
    for c in range(NCH):
        cs = slice(c * 128, (c + 1) * 128)
        hc = slice(c * 64, (c + 1) * 64)
        E("dve", lambda e, cs=cs: e.tensor_tensor_scan(
            s_acum[:, cs], s_adt[:, cs], s_z64.ap(), 0.0, ALU.add, ALU.add),
          ["s_adt", "s_z64"], ["s_acum"], tag=f"scan{c}")
        E("dve", lambda e, c=c, cs=cs: e.tensor_scalar(
            s_dsf.ap(), s_acum[:, cs], s_acum[:, c * 128 + 127: c * 128 + 128],
            None, ALU.subtract),
          ["s_acum"], ["s_dsf"], tag=f"ds1_{c}")
        E("act", lambda e: e.activation(s_dsf2.ap(), s_dsf.ap(), AF.Exp, scale=-1.0),
          ["s_dsf"], ["s_dsf2"], tag=f"ds2_{c}")
        E("pe", lambda e, c=c: e.transpose(
            pb[5][0:1, 0:64], s_acum[:, c * 128 + 127: c * 128 + 128],
            s_idf[0:64, 0:64]),
          ["s_acum", "s_idf"], ["pb5"], tag=f"lamt{c}")
        E("dve", lambda e: e.tensor_copy(s_lrow.ap(), pb[5][0:1, 0:64]),
          ["pb5"], ["s_lrow"], tag=f"lame{c}")
        E("pe", lambda e: e.matmul(pb[5][:, 64:128], s_orf.ap(), s_lrow.ap(),
                                   start=True, stop=True),
          ["s_lrow", "s_orf"], ["pb5"], tag=f"lamb{c}")
        E("act", lambda e, hc=hc: e.activation(
            s_lam[:, hc], pb[5][:, 64:128], AF.Exp),
          ["pb5"], ["s_lam"], tag=f"lamx{c}")
        E("pe", lambda e, cs=cs: e.transpose(pb[4][:, 0:64], s_acum[:, cs],
                                             s_idf[0:64, 0:64]),
          ["s_acum", "s_idf"], ["pb4"], tag=f"at{c}")
        E("dve", lambda e, hc=hc: e.tensor_copy(s_actm[:, hc], pb[4][:, 0:64]),
          ["pb4"], ["s_actm"], tag=f"ate{c}")
        E("pe", lambda e, cs=cs: e.transpose(pb[4][:, 64:128], s_dtfm[:, cs],
                                             s_idf[0:64, 0:64]),
          ["s_dtfm", "s_idf"], ["pb4"], tag=f"dtt{c}")
        E("dve", lambda e, hc=hc: e.tensor_copy(s_dttm[:, hc], pb[4][:, 64:128]),
          ["pb4"], ["s_dttm"], tag=f"dtte{c}")
        E("pe", lambda e: e.transpose(pb[4][:, 128:192], s_dsf2.ap(),
                                      s_idf[0:64, 0:64]),
          ["s_dsf2", "s_idf"], ["pb4"], tag=f"dst{c}")
        E("dve", lambda e, hc=hc: e.tensor_tensor(
            s_dsdt[:, hc], pb[4][:, 128:192], s_dttm[:, hc], ALU.mult),
          ["pb4", "s_dttm"], ["s_dsdt"], tag=f"dsdt{c}")
        E("dve", lambda e, cs=cs: e.tensor_copy(s_hi[:, cs], s_acum[:, cs]),
          ["s_acum"], ["s_hi"], tag=f"hi{c}")
        E("dve", lambda e, cs=cs: e.tensor_tensor(
            s_lo[:, cs], s_acum[:, cs], s_hi[:, cs], ALU.subtract),
          ["s_acum", "s_hi"], ["s_lo"], tag=f"lo{c}")
        E("pe", lambda e, cs=cs: e.matmul(pb[6][0:128, 0:128], s_Bfm[:, cs],
                                          s_Cfm[:, cs], start=True, stop=True),
          ["s_Bfm", "s_Cfm"], ["pb6"], tag=f"gm{c}")
        E("dve", lambda e, cs=cs: e.tensor_tensor(
            s_gmt[:, cs], pb[6][0:128, 0:128], s_tri.ap(), ALU.mult),
          ["pb6", "s_tri"], ["s_gmt"], tag=f"gme{c}")
        E("pe", lambda e, cs=cs: e.transpose(pbb[7][:, 0:128], s_Bfm[:, cs],
                                             s_idb.ap()),
          ["s_Bfm", "s_idb"], ["pb7"], tag=f"bt{c}")
        E("dve", lambda e, cs=cs: e.tensor_copy(s_btm[:, cs], pbb[7][:, 0:128]),
          ["pb7"], ["s_btm"], tag=f"bte{c}")

    tap("acum", lambda: s_acum.ap(), ["s_acum"])
    tap("gmt", lambda: s_gmt.ap(), ["s_gmt"])
    tap("hilo", lambda: s_hi.ap(), ["s_hi"])
    tap("actm", lambda: s_actm.ap(), ["s_actm"])
    tap("dsdt", lambda: s_dsdt.ap(), ["s_dsdt"])
    tap("lam", lambda: s_lam.ap(), ["s_lam"])

    if upto < PH_P1B:
        return _finish(nc, bld, tap_t, ar, arena_addr, B)

    def hs_transpose_thunks(dst, dstn, jj):
        thunks = []
        for c in range(NCH):
            bk = 6 + (c % 2)

            def thunk(c=c, bk=bk, dst=dst, dstn=dstn, jj=jj):
                E("pe", lambda e, dst=dst, c=c, bk=bk: e.transpose(
                    pbb[bk][:, 0:128], dst[:, c * 128:(c + 1) * 128], s_idb.ap()),
                  [dstn, "s_idb"], [f"pb{bk}"], tag="hs_t")
                E("act", lambda e, c=c, jj=jj, bk=bk: e.copy(
                    s_xs[:, c * INTER + jj * 128: c * INTER + jj * 128 + 128],
                    pbb[bk][:, 0:128]),
                  [f"pb{bk}"], [f"s_xs#{c}"], tag="hs_te")
            thunks.append(thunk)
        return thunks

    hs_pend = []
    for j, fidx in enumerate(range(32, 64)):
        if fidx % 2 == 0 and fidx + 2 < 64:
            inproj_load([fidx + 2, fidx + 3], (fidx // 2 + 1) % 2)
        tps_ = hs_pend.pop(0) if len(hs_pend) > 2 else None
        hs_pend.append(hs_transpose_thunks(*emit_inproj(fidx, "hs", slab_i=j,
                                                        tp_thunks=tps_)))
    for item in hs_pend:
        for t_ in item:
            t_()
    tap("xs", lambda: s_xs.ap(), [f"s_xs#{c}" for c in range(NCH)])

    if upto < PH_P2A:
        return _finish(nc, bld, tap_t, ar, arena_addr, B)

    E("dve", lambda e: e.memset(s_E.ap(), 0.0), [],
      [f"s_E#{h}" for h in range(H)], tag="Ez")

    def p2a_s0(it):
        c, h = divmod(it, H)
        i2 = it % 4
        hcol = slice(c * 64 + h, c * 64 + h + 1)
        E("dve", lambda e, i2=i2, c=c, h=h, hcol=hcol: e.tensor_scalar(
            s_xc[i2].ap(), s_xs[:, c * INTER + h * 64: c * INTER + h * 64 + 64],
            s_dsdt[:, hcol], None, ALU.mult),
          [f"s_xs#{c}", "s_dsdt"], [f"s_xc{i2}"], tag=f"Axc{it}")

    def p2a_s1(it):
        c, h = divmod(it, H)
        sb_ = 4 + (it % 2)
        cs = slice(c * 128, (c + 1) * 128)
        E("pe", lambda e, sb_=sb_, cs=cs, i2=it % 4: e.matmul(
            pb[sb_][0:128, 0:64], s_btm[:, cs], s_xc[i2].ap(),
            start=True, stop=True),
          ["s_btm", f"s_xc{it % 4}"], [f"pb{sb_}"], tag=f"Amm{it}")

    def p2a_s2(it):
        c, h = divmod(it, H)
        sb_ = 4 + (it % 2)
        hcol = slice(c * 64 + h, c * 64 + h + 1)
        E("dve", lambda e, h=h, hcol=hcol, sb_=sb_: e.scalar_tensor_tensor(
            s_E[:, h * 64:(h + 1) * 64], s_E[:, h * 64:(h + 1) * 64],
            s_lam[:, hcol], pb[sb_][0:128, 0:64], ALU.mult, ALU.add),
          [f"s_E#{h}", "s_lam", f"pb{sb_}"], [f"s_E#{h}"], tag=f"Aeu{it}")

    NITA = NCH * H
    for k in range(NITA + 2):
        if k < NITA:
            p2a_s0(k)
        if 1 <= k <= NITA:
            p2a_s1(k - 1)
        if 2 <= k <= NITA + 1:
            p2a_s2(k - 2)
        if k % 40 == 0 and k // 40 < 12:
            g2 = (k // 40) * 2
            inproj_load([g2, g2 + 1], (k // 40) % 2)
        if k % 40 == 12 and k // 40 < 12:
            emit_inproj((k // 40) * 2, "gate")
        if k % 40 == 28 and k // 40 < 12:
            emit_inproj((k // 40) * 2 + 1, "gate")

    if upto < PH_CC:
        return _finish(nc, bld, tap_t, ar, arena_addr, B)

    D("act", lambda e: e.dma_start(out=d_ccin.ap()[:, 0:4096], in_=s_E.ap()),
      [f"s_E#{h}" for h in range(H)], ["dccin"], tag="ccs_e", fam="ccin")
    D("act", lambda e: e.dma_start(out=d_ccin.ap()[:, 4096:4100],
                                   in_=s_part[:, 0:4]),
      ["s_part#0", "s_part#1", "s_part#2", "s_part#3"], ["dccin"],
      tag="ccs_p", fam="ccin")
    bld.cc(lambda e: e.collective_compute(
        "AllGather", ALU.bypass, replica_groups=[list(range(8))],
        ins=[d_ccin.ap().opt()], outs=[d_ccout.ap().opt()]),
        reads=[g("dccin")], writes=[g("dccout")], tag="allgather")
    for g2 in range(12, 16):
        inproj_load([g2 * 2, g2 * 2 + 1], g2 % 2)
        emit_inproj(g2 * 2, "gate")
        emit_inproj(g2 * 2 + 1, "gate")

    if upto < PH_E0:
        return _finish(nc, bld, tap_t, ar, arena_addr, B)

    E("dve", lambda e: e.memset(s_E.ap(), 0.0), [],
      [f"s_E#{h}" for h in range(H)], tag="E0z")
    E("pe", lambda e: e.matmul(pb[5][:, 0:8], s_orf.ap(), s_e0m.ap(),
                               start=True, stop=True),
      ["s_orf", "s_e0m"], ["pb5"], tag="e0m_mm")
    E("dve", lambda e: e.tensor_copy(s_mcols.ap(), pb[5][:, 0:8]),
      ["pb5"], ["s_mcols"], tag="e0m_ev")
    allE = [f"s_E#{h}" for h in range(H)]
    for q in range(8):
        i2 = q % 2
        sl = s_slot[i2]
        sln = f"s_slot{i2}"
        D("sp", lambda e, q=q, sl=sl: e.dma_start(
            out=sl.ap(), in_=d_ccout[q * 128:(q + 1) * 128, 0:4096]),
          ["dccout"], [sln], tag=f"e0ld{q}", fam=sln)
        E("dve", lambda e, q=q, sl=sl: e.scalar_tensor_tensor(
            s_E.ap(), sl.ap(), s_mcols[:, q:q + 1], s_E.ap(),
            ALU.mult, ALU.add),
          [sln, "s_mcols"] + allE, allE, tag=f"e0a{q}")
    tap("E", lambda: s_E.ap(), [f"s_E#{h}" for h in range(H)])

    for q in range(8):
        D("pool", lambda e, q=q: e.dma_start(
            out=s_spall[:, q * 3:(q + 1) * 3],
            in_=d_ccout[q * 128:(q + 1) * 128, 4096:4099]),
          ["dccout"], ["s_spall"], tag=f"scg{q}", fam="spall")
    E("dve", lambda e: e.tensor_copy(s_sp3r.ap(), s_spall[:, 0:3]),
      ["s_spall"], ["s_sp3r"], tag="scc0")
    for q in range(1, 8):
        E("dve", lambda e, q=q: e.tensor_tensor(
            s_sp3r.ap(), s_sp3r.ap(), s_spall[:, q * 3:(q + 1) * 3], ALU.add),
          ["s_spall", "s_sp3r"], ["s_sp3r"], tag=f"scc{q}")
    E("pe", lambda e: e.matmul(pb[5][0:1, 8:11], s_ocf.ap(), s_sp3r.ap(),
                               start=True, stop=True),
      ["s_ocf", "s_sp3r"], ["pb5"], tag="sc_mm")
    E("dve", lambda e: e.tensor_copy(s_srow[0:1, 0:3], pb[5][0:1, 8:11]),
      ["pb5"], ["s_srow"], tag="sc_ev")
    E("dve", lambda e: e.tensor_scalar(
        s_srow[0:1, 0:3], s_srow[0:1, 0:3], 1.0 / (DE * DM), EPS_Q,
        ALU.mult, ALU.add),
      ["s_srow"], ["s_srow"], tag="sc_sc")
    E("dve", lambda e: e.tensor_scalar(
        s_srow[0:1, 3:6], s_srow[0:1, 0:3], 0.5, None, ALU.mult),
      ["s_srow"], ["s_srow"], tag="sc_thr")
    E("dve", lambda e: e.tensor_tensor(
        s_srow[0:1, 6:7], s_srow[0:1, 4:5], s_srow[0:1, 5:6], ALU.mult),
      ["s_srow"], ["s_srow"], tag="sc_fg")
    E("dve", lambda e: e.tensor_tensor(
        s_srow[0:1, 7:8], s_srow[0:1, 1:2], s_srow[0:1, 5:6], ALU.mult),
      ["s_srow"], ["s_srow"], tag="sc_fg2")
    tap("srow", lambda: s_srow.ap(), ["s_srow"])
    E("pe", lambda e: e.matmul(pb[5][:, 16:24], s_orf.ap(), s_srow.ap(),
                               start=True, stop=True),
      ["s_orf", "s_srow"], ["pb5"], tag="sc_bc")
    E("dve", lambda e: e.tensor_copy(s_bcols[:, 0:8], pb[5][:, 16:24]),
      ["pb5"], ["s_bcols"], tag="sc_bce")
    E("dve", lambda e: e.tensor_scalar(
        s_bcols[:, 8:11], s_bcols[:, 3:6], -1.0, None, ALU.mult),
      ["s_bcols"], ["s_bcols"], tag="sc_neg")

    if upto < PH_P2B:
        return _finish(nc, bld, tap_t, ar, arena_addr, B)

    D("sp", lambda e: e.dma_start(out=s_sel.ap(), in_=c_sel.ap()),
      [], ["s_sel"], tag="selld", fam="sel")

    def it2idx(it):
        hp, r = divmod(it, 16)
        c, hh = divmod(r, 2)
        return hp, c, hh

    def p2b_s0(it):
        hp, c, hh = it2idx(it)
        h = 2 * hp + hh
        i4 = it % 4
        cs = slice(c * 128, (c + 1) * 128)
        hcol = slice(c * 64 + h, c * 64 + h + 1)
        bcb = [2, 3, 7][it % 3]
        if c == 0 and hh == 0:
            sgb = s_sgb[hp % 2]
            D("sp", lambda e, hp=hp, sgb=sgb: e.dma_start(
                out=sgb.ap(), in_=d_sg[hp * 128:(hp + 1) * 128, :]),
              ["dsg0", "dsg1"], [f"s_sgb{hp % 2}"], tag=f"sgld{hp}",
              fam=f"s_sgb{hp % 2}")
        if c == 0:
            dii = (hp % 2) * 2 + hh
            E("dve", lambda e, dii=dii, h=h: e.tensor_scalar(
                s_di[dii].ap(), s_idb.ap(), s_dbc[:, h:h + 1], None, ALU.mult),
              ["s_idb", "s_dbc"], [f"s_di{dii}"], tag=f"dih{h}")
        E("pe", lambda e, h=h, cs=cs, bcb=bcb: e.matmul(
            pb[bcb][0:128, 0:128], s_sel[:, h * 128:(h + 1) * 128],
            s_hi[:, cs], start=True, stop=False),
          ["s_sel", "s_hi"], [f"pb{bcb}"], tag=f"bc1_{it}")
        E("pe", lambda e, h=h, cs=cs, bcb=bcb: e.matmul(
            pb[bcb][0:128, 0:128], s_sel[:, h * 128:(h + 1) * 128],
            s_lo[:, cs], start=False, stop=True),
          ["s_sel", "s_lo"], [f"pb{bcb}"], tag=f"bc2_{it}")
        E("dve", lambda e, i4=i4, c=c, h=h, hcol=hcol: e.tensor_scalar(
            s_xc[i4].ap(), s_xs[:, c * INTER + h * 64: c * INTER + h * 64 + 64],
            s_dttm[:, hcol], None, ALU.mult),
          [f"s_xs#{c}", "s_dttm"], [f"s_xc{i4}"], tag=f"xc_{it}")
        E("dve", lambda e, i4=i4, c=c, h=h, hcol=hcol: e.tensor_scalar(
            s_bd[i4][:, 0:64], s_xs[:, c * INTER + h * 64: c * INTER + h * 64 + 64],
            s_dsdt[:, hcol], None, ALU.mult),
          [f"s_xs#{c}", "s_dsdt"], [f"s_bd{i4}"], tag=f"xc2_{it}")

    def p2b_s1(it):
        hp, c, hh = it2idx(it)
        h = 2 * hp + hh
        i4 = it % 4
        cs = slice(c * 128, (c + 1) * 128)
        hcol = slice(c * 64 + h, c * 64 + h + 1)
        bcb = [2, 3, 7][it % 3]
        sb_ = it % 2
        E("act", lambda e, i4=i4, hcol=hcol, bcb=bcb: e.activation(
            s_scr[i4].ap(), pb[bcb][0:128, 0:128], AF.Relu, scale=-1.0,
            bias=s_actm[:, hcol]),
          [f"pb{bcb}", "s_actm"], [f"s_scr{i4}"], tag=f"rel_{it}")
        E("act", lambda e, i4=i4, bcb=bcb: e.activation(
            s_sdt[i4].ap(), pb[bcb][0:128, 0:128], AF.Exp),
          [f"pb{bcb}"], [f"s_sdt{i4}"], tag=f"sd_{it}")
        E("act", lambda e, i4=i4: e.activation(
            s_et[i4].ap(), s_scr[i4].ap(), AF.Exp, scale=-1.0),
          [f"s_scr{i4}"], [f"s_et{i4}"], tag=f"exp_{it}")
        E("pool", lambda e, i4=i4, cs=cs: e.tensor_tensor(
            s_mt[i4].ap(), s_et[i4].ap(), s_gmt[:, cs], ALU.mult),
          [f"s_et{i4}", "s_gmt"], [f"s_mt{i4}"], tag=f"mt_{it}")
        E("pool", lambda e, i4=i4, cs=cs: e.tensor_tensor(
            s_cd[i4].ap(), s_Cfm[:, cs], s_sdt[i4].ap(), ALU.mult),
          ["s_Cfm", f"s_sdt{i4}"], [f"s_cd{i4}"], tag=f"cd_{it}")
        E("pe", lambda e, i4=i4, sb_=sb_, cs=cs: e.matmul(
            pb[sb_][0:128, 0:64], s_btm[:, cs], s_bd[i4][:, 0:64],
            start=True, stop=True),
          ["s_btm", f"s_bd{i4}"], [f"pb{sb_}"], tag=f"smm_{it}")
        E("dve", lambda e, i4=i4, h=h: e.tensor_copy(
            s_ebf[i4].ap(), s_E[:, h * 64:(h + 1) * 64]),
          [f"s_E#{h}"], [f"s_ebf{i4}"], tag=f"ebf_{it}")
        E("dve", lambda e, h=h, hcol=hcol, sb_=sb_, i4=i4: e.scalar_tensor_tensor(
            s_E[:, h * 64:(h + 1) * 64], s_E[:, h * 64:(h + 1) * 64],
            s_lam[:, hcol], pb[sb_][0:128, 0:64], ALU.mult, ALU.add),
          [f"s_E#{h}", "s_lam", f"pb{sb_}", f"s_ebf{i4}"], [f"s_E#{h}"],
          tag=f"eu_{it}")

    def p2b_s2(it):
        hp, c, hh = it2idx(it)
        h = 2 * hp + hh
        i4 = it % 4
        dii = (hp % 2) * 2 + hh
        yb = 4 + ((hp * NCH + c) % 3)
        tp = None if hh == 0 else (0, 64)
        lo, hi2 = hh * 64, hh * 64 + 64
        E("pe", lambda e, yb=yb, lo=lo, hi2=hi2, tp=tp, i4=i4: e.matmul(
            pb[yb][lo:hi2, 0:128], s_xc[i4].ap(),
            s_mt[i4].ap(), start=True, stop=False, tile_position=tp),
          [f"s_xc{i4}", f"s_mt{i4}"], [f"pb{yb}"], tag=f"yd_{it}")
        E("pe", lambda e, h=h, c=c, yb=yb, lo=lo, hi2=hi2, tp=tp, dii=dii:
          e.matmul(
            pb[yb][lo:hi2, 0:128],
            s_xs[:, c * INTER + h * 64: c * INTER + h * 64 + 64],
            s_di[dii].ap(), start=False, stop=False, tile_position=tp),
          [f"s_xs#{c}", f"s_di{dii}"], [f"pb{yb}"], tag=f"yr_{it}")
        E("pe", lambda e, yb=yb, lo=lo, hi2=hi2, tp=tp, i4=i4: e.matmul(
            pb[yb][lo:hi2, 0:128], s_ebf[i4].ap(), s_cd[i4].ap(),
            start=False, stop=True, tile_position=tp),
          [f"s_ebf{i4}", f"s_cd{i4}"], [f"pb{yb}"], tag=f"yo_{it}")
        if hh == 1:
            sgb = s_sgb[hp % 2]
            E("dve", lambda e, hp=hp, c=c, yb=yb, sgb=sgb: e.tensor_tensor(
                s_yg[:, hp * NT + c * 128: hp * NT + (c + 1) * 128],
                pb[yb][0:128, 0:128], sgb[:, c * 128:(c + 1) * 128], ALU.mult),
              [f"pb{yb}", f"s_sgb{hp % 2}"], [f"s_yg#{hp}"], tag=f"yg_{it}")
            if c == NCH - 1:
                ys = s_ys[hp % 2]
                ysn = f"s_ys{hp % 2}"
                E("act", lambda e, hp=hp, ys=ys: e.activation(
                    ys.ap(), s_yg[:, hp * NT:(hp + 1) * NT], AF.Square),
                  [f"s_yg#{hp}"], [ysn], tag=f"sq_{hp}")
                sbk = 4 + (hp % 3)
                for half in range(2):
                    E("pe", lambda e, ys=ys, half=half, sbk=sbk: e.matmul(
                        pb[sbk][0:1, 0:512], s_ocb.ap(),
                        ys[:, half * 512:(half + 1) * 512],
                        start=True, stop=True),
                      [ysn, "s_ocb"], [f"pb{sbk}"], tag=f"st{half}_{hp}")
                    if hp == 0:
                        E("dve", lambda e, half=half, sbk=sbk: e.tensor_copy(
                            s_rsum[0:1, half * 512:(half + 1) * 512],
                            pb[sbk][0:1, 0:512]),
                          [f"pb{sbk}"], [f"s_rsum#{half}"], tag=f"sta{half}_{hp}")
                    else:
                        E("dve", lambda e, half=half, sbk=sbk: e.tensor_tensor(
                            s_rsum[0:1, half * 512:(half + 1) * 512],
                            s_rsum[0:1, half * 512:(half + 1) * 512],
                            pb[sbk][0:1, 0:512], ALU.add),
                          [f"pb{sbk}", f"s_rsum#{half}"], [f"s_rsum#{half}"],
                          tag=f"sta{half}_{hp}")

    NITB = 32 * 16
    for k in range(NITB + 2):
        if k < NITB:
            p2b_s0(k)
        if 1 <= k <= NITB:
            p2b_s1(k - 1)
        if 2 <= k <= NITB + 1:
            p2b_s2(k - 2)
    tap("yg", lambda: s_yg.ap(), [f"s_yg#{hp}" for hp in range(32)])

    if upto < PH_P3M:
        return _finish(nc, bld, tap_t, ar, arena_addr, B)

    for half in range(2):
        E("act", lambda e, half=half: e.activation(
            s_r1[0:1, half * 512:(half + 1) * 512],
            s_rsum[0:1, half * 512:(half + 1) * 512], AF.Sqrt,
            scale=1.0 / INTER, bias=s_kc[0:1, 0:1]),
          [f"s_rsum#{half}", "s_kc"], ["s_r1"], tag=f"rms_s{half}")
    E("dve", lambda e: e.reciprocal(s_r1[0:1, NT:2 * NT], s_r1[0:1, 0:NT]),
      ["s_r1"], ["s_r1"], tag="rms_r")
    for half, bk in ((0, 4), (1, 5)):
        E("pe", lambda e, half=half, bk=bk: e.matmul(
            pb[bk][:, 0:512], s_orf.ap(),
            s_r1[0:1, NT + half * 512: NT + (half + 1) * 512],
            start=True, stop=True),
          ["s_r1", "s_orf"], [f"pb{bk}"], tag=f"rb_{half}")
        E("dve", lambda e, half=half, bk=bk: e.tensor_copy(
            s_rb[:, half * 512:(half + 1) * 512], pb[bk][:, 0:512]),
          [f"pb{bk}"], ["s_rb"], tag=f"rbe_{half}")
    for j in range(32):
        E("dve", lambda e, j=j: e.tensor_tensor(
            s_yg[:, j * NT:(j + 1) * NT], s_yg[:, j * NT:(j + 1) * NT],
            s_rb.ap(), ALU.mult),
          [f"s_yg#{j}", "s_rb"], [f"s_yg#{j}"], tag=f"yn_{j}")

    def p3_load(gi):
        wg_ = s_wopg[gi % 2]
        for icg in range(4):
            ic = gi * 4 + icg
            ws = s_wops[ic % 2]
            wsn = f"s_wops{ic % 2}"
            D("sp", lambda e, ic=ic, ws=ws: e.dma_start(
                out=ws.ap(), in_=d_wop[ic * 128:(ic + 1) * 128, :]),
              [], [wsn], tag=f"wold{ic}", fam=wsn)
            E("dve", lambda e, ic=ic, icg=icg, ws=ws, wg_=wg_: e.tensor_scalar(
                wg_[:, icg * DM:(icg + 1) * DM], ws.ap(), s_nw[:, ic:ic + 1],
                None, ALU.mult),
              [wsn, "s_nw"], [f"s_wopg{gi % 2}"], tag=f"woc{ic}")

    def p3_mms(gi):
        wg_ = s_wopg[gi % 2]
        first, last = True, True
        for t in range(8):
            for eq in range(4):
                bk = (t * 4 + eq) % 4
                for icg in range(4):
                    ic = gi * 4 + icg
                    E("pe", lambda e, t=t, eq=eq, icg=icg, ic=ic, bk=bk, wg_=wg_,
                      first=first, last=last: e.matmul(
                        pb[bk][0:128, 0:512],
                        s_yg[:, ic * NT + t * 128: ic * NT + t * 128 + 128],
                        wg_[:, icg * DM + eq * 512: icg * DM + eq * 512 + 512],
                        start=(first and icg == 0), stop=(last and icg == 3)),
                      [f"s_yg#{ic}", f"s_wopg{gi % 2}"], [f"pb{bk}"],
                      tag=f"op{gi}_{t}_{eq}")
                if not last:
                    continue
                osl = slice(t * DM + eq * 512, t * DM + eq * 512 + 512)
                if gi == 0:
                    E("dve", lambda e, osl=osl, bk=bk: e.tensor_copy(
                        s_oacc[:, osl], pb[bk][0:128, 0:512]),
                      [f"pb{bk}"], [f"s_oacc#{osl.start // DM}"],
                      tag=f"oe{gi}_{t}_{eq}")
                else:
                    E("dve", lambda e, osl=osl, bk=bk: e.tensor_tensor(
                        s_oacc[:, osl], s_oacc[:, osl], pb[bk][0:128, 0:512],
                        ALU.add),
                      [f"pb{bk}", f"s_oacc#{osl.start // DM}"],
                      [f"s_oacc#{osl.start // DM}"], tag=f"oe{gi}_{t}_{eq}")

    for gi in range(9):
        if gi < 8:
            p3_load(gi)
        if gi >= 1:
            p3_mms(gi - 1)
    tap("oacc", lambda: s_oacc.ap(), [f"s_oacc#{t}" for t in range(8)])

    if upto < PH_P3T:
        return _finish(nc, bld, tap_t, ar, arena_addr, B)

    for t in range(8):
        i2 = t % 2
        oa = s_oacc[:, t * DM:(t + 1) * DM]
        oan = f"s_oacc#{t}"
        xr, t1, x1, xl2 = s_xrd[i2], s_t1[i2], s_x1b[i2], s_xl2[i2]
        xrn, x1n, xl2n = f"s_xrd{i2}", f"s_x1b{i2}", f"s_xl2{i2}"
        D("sp", lambda e, t=t, xr=xr: e.dma_start(
            out=xr.ap(), in_=d_x[128 + t * 128: 256 + t * 128, :]),
          [], [xrn], tag=f"xrd{t}", fam=xrn)
        cb = i2 * 8
        ln_stats(oa, [oan], s_sqb[i2].ap(), f"s_sqb{i2}", f"ln2_{t}", cb=cb)
        E("dve", lambda e, oa=oa, t1=t1, cb=cb: e.tensor_scalar(
            t1.ap(), oa, s_col[:, cb + 2:cb + 3], s_col[:, cb + 7:cb + 8],
            ALU.subtract, ALU.mult),
          [oan, f"s_col#{cb + 2}", f"s_col#{cb + 7}"], [f"s_t1{i2}"],
          tag=f"ln2a_{t}")
        E("dve", lambda e, t1=t1, xr=xr, x1=x1: e.tensor_tensor(
            x1.ap(), t1.ap(), xr.ap(), ALU.add),
          [f"s_t1{i2}", xrn], [x1n], tag=f"res1_{t}")
        D("sp", lambda e, t=t, x1=x1: e.dma_start(
            out=d_x1[t * 128:(t + 1) * 128, :], in_=x1.ap()),
          [x1n], [f"dx1_{t % 2}"], tag=f"x1st{t}", fam=f"dx1{t % 2}")
        ln_stats(x1.ap(), [x1n], s_sqb[i2].ap(), f"s_sqb{i2}", f"lnf_{t}", cb=cb)
        E("dve", lambda e, x1=x1, xl2=xl2, cb=cb: e.tensor_scalar(
            xl2.ap(), x1.ap(), s_col[:, cb + 2:cb + 3], s_col[:, cb + 7:cb + 8],
            ALU.subtract, ALU.mult),
          [x1n, f"s_col#{cb + 2}", f"s_col#{cb + 7}"], [xl2n], tag=f"lnfa_{t}")
        for kc in range(KD):
            bk = 6 + (kc % 2)
            E("pe", lambda e, xl2=xl2, kc=kc, bk=bk: e.transpose(
                pbb[bk][:, 0:128], xl2[:, kc * 128:(kc + 1) * 128], s_idb.ap()),
              [xl2n, "s_idb"], [f"pb{bk}"], tag="x2t")
            E("dve", lambda e, kc=kc, t=t, bk=bk: e.tensor_copy(
                s_xl2T[:, kc * NT + t * 128: kc * NT + (t + 1) * 128],
                pbb[bk][:, 0:128]),
              [f"pb{bk}"], [f"s_xl2T#{kc}"], tag="x2te")

    if upto < PH_P4A:
        return _finish(nc, bld, tap_t, ar, arena_addr, B)

    def p4a_quant(ei):
        i2 = ei % 2
        qu, qg = s_qu[i2], s_qg[i2]
        qun, qgn = f"s_qu{i2}", f"s_qg{i2}"
        for (dsrc, stg, stgn, qq, qqn, thr_c, nthr_c, on_act) in (
                (d_wu, s_ust, "s_ust", qu, qun, 3, 8, True),
                (d_wg, s_gst, "s_gst", qg, qgn, 4, 9, False)):
            src = dsrc.ap()[:, ei * 128:(ei + 1) * 128].rearrange(
                "(kc kp) f -> kp kc f", kp=128)
            for halfk in range(2):
                st = stg[halfk]
                stn = f"{stgn}{halfk}"
                hs_ = slice(halfk * 1024, (halfk + 1) * 1024)
                D("sp", lambda e, src=src, st=st, halfk=halfk: e.dma_start(
                    out=st.ap().rearrange("p (kc f) -> p kc f", f=128),
                    in_=src[:, halfk * 8:(halfk + 1) * 8, :]),
                  [], [stn], tag=f"uld{ei}", fam=stn)
                if on_act:
                    E("act", lambda e, st=st, qq=qq, hs_=hs_, nthr_c=nthr_c:
                      e.activation(
                        qq[:, hs_], st.ap(), AF.Sign,
                        bias=s_bcols[:, nthr_c:nthr_c + 1]),
                      [stn, "s_bcols"], [qqn], tag=f"sg1{ei}")
                    E("act", lambda e, st=st, thr_c=thr_c: e.activation(
                        s_q2[:, 0:1024], st.ap(), AF.Sign,
                        bias=s_bcols[:, thr_c:thr_c + 1]),
                      [stn, "s_bcols"], ["s_q2"], tag=f"sg2{ei}")
                    E("dve", lambda e, qq=qq, hs_=hs_: e.tensor_tensor(
                        qq[:, hs_], qq[:, hs_], s_q2[:, 0:1024], ALU.add),
                      [qqn, "s_q2"], [qqn], tag=f"sga{ei}")
                else:
                    E("dve", lambda e, st=st, qq=qq, hs_=hs_, thr_c=thr_c:
                      e.tensor_scalar(
                        qq[:, hs_], st.ap(), s_bcols[:, thr_c:thr_c + 1], None,
                        ALU.is_gt),
                      [stn, "s_bcols"], [qqn], tag=f"gg1{ei}")
                    E("dve", lambda e, st=st, nthr_c=nthr_c: e.tensor_scalar(
                        s_q2[:, 0:1024], st.ap(), s_bcols[:, nthr_c:nthr_c + 1],
                        None, ALU.is_lt),
                      [stn, "s_bcols"], ["s_q2"], tag=f"gg2{ei}")
                    E("dve", lambda e, qq=qq, hs_=hs_: e.tensor_tensor(
                        qq[:, hs_], qq[:, hs_], s_q2[:, 0:1024], ALU.subtract),
                      [qqn, "s_q2"], [qqn], tag=f"gga{ei}")

    def p4a_mm(ei):
        i2 = ei % 2
        qu, qg = s_qu[i2], s_qg[i2]
        qun, qgn = f"s_qu{i2}", f"s_qg{i2}"
        b0 = i2 * 4
        for kc in range(KD):
            st_, sp_ = (kc == 0), (kc == KD - 1)
            for tt in range(2):
                tsl = slice(kc * NT + tt * 512, kc * NT + tt * 512 + 512)
                E("pe", lambda e, qu=qu, kc=kc, tsl=tsl, b=b0 + tt, st_=st_,
                  sp_=sp_: e.matmul(
                    pb[b][0:128, 0:512], qu[:, kc * 128:(kc + 1) * 128],
                    s_xl2T[:, tsl], start=st_, stop=sp_),
                  [qun, f"s_xl2T#{kc}"], [f"pb{b0 + tt}"], tag=f"ump{ei}")
                E("pe", lambda e, qg=qg, kc=kc, tsl=tsl, b=b0 + 2 + tt, st_=st_,
                  sp_=sp_: e.matmul(
                    pb[b][0:128, 0:512], qg[:, kc * 128:(kc + 1) * 128],
                    s_xl2T[:, tsl], start=st_, stop=sp_),
                  [qgn, f"s_xl2T#{kc}"], [f"pb{b0 + 2 + tt}"], tag=f"gmp{ei}")
        for tt in range(2):
            hsb = s_hs2[tt]
            E("act", lambda e, hsb=hsb, b=b0 + tt: e.activation(
                hsb.ap(), pb[b][0:128, 0:512], AF.Silu, scale=s_bcols[:, 3:4]),
              [f"pb{b0 + tt}", "s_bcols"], [f"s_hs2{tt}"], tag=f"usi{ei}")
            E("dve", lambda e, hsb=hsb, ei=ei, tt=tt, b=b0 + 2 + tt:
              e.tensor_tensor(
                s_hid[:, ei * NT + tt * 512: ei * NT + tt * 512 + 512],
                hsb.ap(), pb[b][0:128, 0:512], ALU.mult),
              [f"s_hs2{tt}", f"pb{b0 + 2 + tt}"], [f"s_hid#{ei}"], tag=f"uhm{ei}")

    for ei in range(65):
        if ei < 64:
            p4a_quant(ei)
        if ei >= 1:
            p4a_mm(ei - 1)
    tap("hid", lambda: s_hid.ap(), [f"s_hid#{ei}" for ei in range(64)])

    if upto < PH_P4B:
        return _finish(nc, bld, tap_t, ar, arena_addr, B)

    def p4b_quant(dq, i):
        i3 = i % 3
        i2 = i % 2
        stg, qd, q2 = s_dst[i3], s_qd[i3], s_q2d[i2]
        stn, qdn, q2n = f"s_dst{i3}", f"s_qd{i3}", f"s_q2d{i2}"
        D("sp", lambda e, i=i, dq=dq, stg=stg: e.dma_start(
            out=stg.ap(),
            in_=d_wd[i * 128:(i + 1) * 128, dq * 512:(dq + 1) * 512]),
          [], [stn], tag=f"dld{dq}_{i}", fam=stn)
        E("act", lambda e, stg=stg, qd=qd: e.activation(
            qd.ap(), stg.ap(), AF.Sign, bias=s_bcols[:, 10:11]),
          [stn, "s_bcols"], [qdn], tag=f"dsg1{dq}_{i}")
        E("act", lambda e, stg=stg, q2=q2: e.activation(
            q2.ap(), stg.ap(), AF.Sign, bias=s_bcols[:, 5:6]),
          [stn, "s_bcols"], [q2n], tag=f"dsg2{dq}_{i}")
        E("dve", lambda e, qd=qd, q2=q2: e.tensor_tensor(
            qd.ap(), qd.ap(), q2.ap(), ALU.add),
          [qdn, q2n], [qdn], tag=f"dsga{dq}_{i}")

    def p4b_mm(dq, i):
        i3 = i % 3
        qd = s_qd[i3]
        qdn = f"s_qd{i3}"
        for t in range(8):
            E("pe", lambda e, i=i, t=t, qd=qd: e.matmul(
                pb[t][0:128, 0:512],
                s_hid[:, i * NT + t * 128: i * NT + t * 128 + 128],
                qd.ap(), start=(i == 0), stop=(i == 63)),
              [qdn, f"s_hid#{i}"], [f"pb{t}"], tag=f"dmm{dq}_{i}_{t}")

    for dq in range(4):
        for i in range(66):
            if i < 64:
                p4b_quant(dq, i)
            if i >= 2:
                p4b_mm(dq, i - 2)
        for t in range(8):
            i2 = t % 2
            dev = s_dev[i2]
            E("dve", lambda e, t=t, dev=dev: e.tensor_scalar(
                dev.ap(), pb[t][0:128, 0:512], s_bcols[:, 7:8], None, ALU.mult),
              [f"pb{t}", "s_bcols"], [f"s_dev{i2}"], tag=f"dev{dq}_{t}")
            D("sp", lambda e, t=t, dq=dq, dev=dev: e.dma_start(
                out=d_da[t * 128:(t + 1) * 128, dq * 512:(dq + 1) * 512],
                in_=dev.ap()),
              [f"s_dev{i2}"], [f"dda{t % 2}"], tag=f"dst{dq}_{t}",
              fam=f"dda{t % 2}")
            if dq == 0:
                E("dve", lambda e, t=t, dev=dev: e.tensor_reduce(
                    s_l3s[:, t:t + 1], dev.ap(), AX.X, ALU.add),
                  [f"s_dev{i2}"], [f"s_l3s#{t}"], tag=f"l3s{dq}_{t}")
                E("act", lambda e, t=t, dev=dev, i2=i2: e.activation(
                    s_dsq[i2].ap(), dev.ap(), AF.Square,
                    accum_out=s_l3q[:, t:t + 1]),
                  [f"s_dev{i2}"], [f"s_dsq{i2}", f"s_l3q#{t}"], tag=f"l3q{dq}_{t}")
            else:
                E("dve", lambda e, t=t, dev=dev: e.tensor_reduce(
                    s_l3s[:, 8 + t:9 + t], dev.ap(), AX.X, ALU.add),
                  [f"s_dev{i2}"], ["l3tmp"], tag=f"l3s{dq}_{t}")
                E("dve", lambda e, t=t: e.tensor_tensor(
                    s_l3s[:, t:t + 1], s_l3s[:, t:t + 1], s_l3s[:, 8 + t:9 + t],
                    ALU.add),
                  ["l3tmp", f"s_l3s#{t}"], [f"s_l3s#{t}"], tag=f"l3sa{dq}_{t}")
                E("act", lambda e, t=t, dev=dev, i2=i2: e.activation(
                    s_dsq[i2].ap(), dev.ap(), AF.Square,
                    accum_out=s_l3q[:, 8 + t:9 + t]),
                  [f"s_dev{i2}"], [f"s_dsq{i2}", "l3qtmp"], tag=f"l3q{dq}_{t}")
                E("dve", lambda e, t=t: e.tensor_tensor(
                    s_l3q[:, t:t + 1], s_l3q[:, t:t + 1], s_l3q[:, 8 + t:9 + t],
                    ALU.add),
                  ["l3qtmp", f"s_l3q#{t}"], [f"s_l3q#{t}"], tag=f"l3qa{dq}_{t}")

    tap("dacc", lambda: d_da.ap(), ["dda0", "dda1"])
    tap("l3s", lambda: s_l3s.ap(), [f"s_l3s#{t}" for t in range(8)])
    tap("l3q", lambda: s_l3q.ap(), [f"s_l3q#{t}" for t in range(8)])
    out_evs = []
    for t in range(8):
        i2 = t % 2
        dar, x1r, fin = s_dar[i2], s_x1r[i2], s_fin[i2]
        darn, x1n, finn = f"s_dar{i2}", f"s_x1r{i2}", f"s_fin{i2}"
        D("sp", lambda e, t=t, dar=dar: e.dma_start(
            out=dar.ap(), in_=d_da[t * 128:(t + 1) * 128, :]),
          ["dda0", "dda1"], [darn], tag=f"dald{t}", fam=darn)
        D("sp", lambda e, t=t, x1r=x1r: e.dma_start(
            out=x1r.ap(), in_=d_x1[t * 128:(t + 1) * 128, :]),
          ["dx1_0", "dx1_1"], [x1n], tag=f"x1ld{t}", fam=x1n)
        cb = i2 * 8
        E("dve", lambda e, t=t, cb=cb: e.tensor_scalar(
            s_col[:, cb + 2:cb + 3], s_l3s[:, t:t + 1], 1.0 / DM, None, ALU.mult),
          [f"s_l3s#{t}"], [f"s_col#{cb + 2}"], tag=f"l3m_{t}")
        E("dve", lambda e, cb=cb: e.tensor_tensor(
            s_col[:, cb + 4:cb + 5], s_col[:, cb + 2:cb + 3],
            s_col[:, cb + 2:cb + 3], ALU.mult),
          [f"s_col#{cb + 2}"], [f"s_col#{cb + 4}"], tag=f"l3m2_{t}")
        E("dve", lambda e, t=t, cb=cb: e.tensor_scalar(
            s_col[:, cb + 5:cb + 6], s_l3q[:, t:t + 1], 1.0 / DM, None, ALU.mult),
          [f"s_l3q#{t}"], [f"s_col#{cb + 5}"], tag=f"l3v1_{t}")
        E("dve", lambda e, cb=cb: e.tensor_tensor(
            s_col[:, cb + 5:cb + 6], s_col[:, cb + 5:cb + 6],
            s_col[:, cb + 4:cb + 5], ALU.subtract),
          [f"s_col#{cb + 4}", f"s_col#{cb + 5}"], [f"s_col#{cb + 5}"],
          tag=f"l3v2_{t}")
        E("act", lambda e, cb=cb: e.activation(
            s_col[:, cb + 6:cb + 7], s_col[:, cb + 5:cb + 6], AF.Sqrt,
            bias=s_kc[:, 0:1]),
          [f"s_col#{cb + 5}", "s_kc"], [f"s_col#{cb + 6}"], tag=f"l3sd_{t}")
        E("dve", lambda e, cb=cb: e.reciprocal(
            s_col[:, cb + 7:cb + 8], s_col[:, cb + 6:cb + 7]),
          [f"s_col#{cb + 6}"], [f"s_col#{cb + 7}"], tag=f"l3r_{t}")
        E("dve", lambda e, dar=dar, fin=fin, cb=cb: e.tensor_scalar(
            fin.ap(), dar.ap(), s_col[:, cb + 2:cb + 3], s_col[:, cb + 7:cb + 8],
            ALU.subtract, ALU.mult),
          [darn, f"s_col#{cb + 2}", f"s_col#{cb + 7}"], [finn], tag=f"ln3a_{t}")
        E("dve", lambda e, fin=fin, x1r=x1r: e.tensor_tensor(
            fin.ap(), fin.ap(), x1r.ap(), ALU.add),
          [finn, x1n], [finn], tag=f"res2_{t}")
        ev = D("sp", lambda e, t=t, fin=fin: e.dma_start(
            out=d_out[t * 128:(t + 1) * 128, :], in_=fin.ap()),
          [finn], [f"dout{t}"], tag=f"outst{t}", fam=f"dout{t % 2}")
        out_evs.append(ev)
    bld.op("sp", lambda e: e.nop(), [], [], set(out_evs), tag="final_wait")

    return _finish(nc, bld, tap_t, ar, arena_addr, B)


def _finish(nc, bld, tap_t, ar, arena_addr, B):
    import contextlib
    items = ar.commit(arena_addr)
    by_name = {}
    for (off, size, ph0, ph1, name) in items:
        by_name[name] = (off, size, ph0, ph1)
    owners = {}
    for bn, buf in B.items():
        owner = bn.split("#")[0]
        if owner in by_name:
            owners.setdefault(owner, []).append(buf)
    names = list(by_name)
    for i in range(len(names)):
        for j in range(len(names)):
            if i == j:
                continue
            a, b = names[i], names[j]
            (ao, asz, ap0, ap1) = by_name[a]
            (bo, bsz, bp0, bp1) = by_name[b]
            if ao + asz <= bo or bo + bsz <= ao:
                continue
            if not (ap1 < bp0):
                continue
            last_evs = set()
            for buf in owners.get(a, ()):
                if buf.last_w is not None:
                    last_evs.add(buf.last_w)
                last_evs.update(buf.readers)
            for buf in owners.get(b, ()):
                if buf.first_w_op is not None:
                    eng, idx = buf.first_w_op
                    bld.ops[eng][idx].deps.update(last_evs)

    ctx = contextlib.ExitStack()
    sems = {e: ctx.enter_context(nc.semaphore(f"sem_{e}")) for e in ENGINES}
    dma_sems = {f: ctx.enter_context(nc.semaphore(f"dsem_{f}"))
                for f in bld.nfam}
    cc_sem = ctx.enter_context(nc.semaphore("ccsem"))
    with ctx, nc.Block() as block:
        bld.finalize(block, sems, dma_sems, cc_sem)
    return nc, tap_t


def prep_inputs(inputs):
    _ = inputs.get("L0_attn_mask")
    x = np.asarray(inputs["L0_attn_input_states"], dtype=np.float32)
    wip_t = np.ascontiguousarray(np.asarray(inputs["L0_attn_in_proj_w"], np.float32).T)
    wop_t = np.ascontiguousarray(np.asarray(inputs["L0_attn_out_proj_w"], np.float32).T)
    wu_t = np.ascontiguousarray(np.asarray(inputs["L0_ffn_W_up"], np.float32).T)
    wg_t = np.ascontiguousarray(np.asarray(inputs["L0_ffn_W_gate"], np.float32).T)
    wd_t = np.ascontiguousarray(np.asarray(inputs["L0_ffn_W_down"], np.float32).T)
    cw = np.asarray(inputs["L0_attn_conv_w"], np.float32).reshape(4352, 4)
    cw_pp = np.zeros((128, 136), np.float32)
    for j in range(34):
        cw_pp[:, j * 4:(j + 1) * 4] = cw[j * 128:(j + 1) * 128, :]
    cb = np.asarray(inputs["L0_attn_conv_b"], np.float32)
    cb_pp = np.ascontiguousarray(cb.reshape(34, 128).T)
    alog = np.asarray(inputs["L0_attn_A_log"], np.float32).reshape(64, 1)
    dtb = np.asarray(inputs["L0_attn_dt_bias"], np.float32).reshape(64, 1)
    drow = np.asarray(inputs["L0_attn_D"], np.float32).reshape(1, 64)
    nw = np.ascontiguousarray(
        np.asarray(inputs["L0_attn_norm_w"], np.float32).reshape(32, 128).T)

    in_maps = []
    for r in range(8):
        seq, half = r // 2, r % 2
        xs = np.zeros((NROWS, DM), np.float32)
        tok0 = half * NT
        xs[128:, :] = x[seq, tok0:tok0 + NT, :]
        if half == 1:
            xs[125:128, :] = x[seq, NT - 3:NT, :]
        e0m = np.zeros((1, 8), np.float32)
        if half == 1:
            e0m[0, r - 1] = 1.0
        in_maps.append({
            "x": np.ascontiguousarray(xs),
            "wip_t": wip_t, "wop_t": wop_t,
            "wu_t": wu_t, "wg_t": wg_t, "wd_t": wd_t,
            "wu_s": np.ascontiguousarray(wu_t[r * 256:(r + 1) * 256, :]),
            "wg_s": np.ascontiguousarray(wg_t[r * 256:(r + 1) * 256, :]),
            "wd_s": np.ascontiguousarray(wd_t[r * 1024:(r + 1) * 1024, :]),
            "conv_w": cw_pp, "conv_b": cb_pp,
            "alog": alog, "dtb": dtb, "d_row": drow, "normw": nw,
            "e0m": e0m,
        })
    return in_maps


def kernel(**inputs):
    in_maps = prep_inputs(inputs)
    nc, _ = build_program()
    res = run_bass_kernel_spmd(nc, in_maps, core_ids=list(range(8)))
    out = np.zeros((4, 2048, 2048), np.float32)
    for r in range(8):
        seq, half = r // 2, r % 2
        out[seq, half * NT:(half + 1) * NT, :] = res.results[r]["out"]
    return out


# version 18
# speedup vs baseline: 1.0515x; 1.0515x over previous
import numpy as np

import concourse.bass as bass
import concourse.mybir as mybir
from concourse.bass_utils import run_bass_kernel_spmd

F32 = mybir.dt.float32
BF16 = mybir.dt.bfloat16
AF = mybir.ActivationFunctionType
ALU = mybir.AluOpType
AX = mybir.AxisListType

NT = 1024
NROWS = 1152
DM = 2048
PROJ = 8512
INTER = 4096
H = 64
CH = 128
NCH = 8
DE = 8192
KD = 16
EPS_LN = 1e-5
EPS_RMS = 1e-5
EPS_Q = 1e-8

(PH_P0, PH_P1, PH_P1B, PH_P2A, PH_CC, PH_PG, PH_E0, PH_P2B, PH_P3M, PH_P3T,
 PH_P4A, PH_P4B, PH_END) = range(13)

SBUF_BUDGET = 212600
ENGINES = ("pe", "act", "dve", "pool", "sp")


def _bf():
    import ml_dtypes
    return ml_dtypes.bfloat16


class Buf:
    __slots__ = ("name", "last_w", "readers", "first_w_op")

    def __init__(self, name):
        self.name = name
        self.last_w = None
        self.readers = []
        self.first_w_op = None


class Op:
    __slots__ = ("eng", "fn", "deps", "inc", "inc_idx", "kind", "tag", "fam")

    def __init__(self, eng, fn, deps, kind, tag):
        self.eng = eng
        self.fn = fn
        self.deps = deps
        self.inc = False
        self.inc_idx = 0
        self.kind = kind
        self.tag = tag


class Builder:
    CHAINED = {"ldc0", "ldc1", "spall", "ccin", "tap"}

    def __init__(self):
        self.ops = {e: [] for e in ENGINES}
        self.nfam = {}
        self.fam_last = {}
        self.wait_all_fams = set()
        self.ncc = 0

    def _deps(self, reads, writes, extra):
        deps = set(extra)
        for b in reads:
            if b.last_w is not None:
                deps.add(b.last_w)
        for b in writes:
            if b.last_w is not None:
                deps.add(b.last_w)
            deps.update(b.readers)
        return deps

    def _rec(self, eng, fn, reads, writes, extra, kind, tag, fam=None):
        deps = self._deps(reads, writes, extra)
        idx = len(self.ops[eng])
        o = Op(eng, fn, deps, kind, tag)
        if kind == "dma":
            o.inc_idx = self.nfam.setdefault(fam, 0)
            self.nfam[fam] += 1
            o.fam = fam
            ev = ("d", fam, o.inc_idx)
        elif kind == "cc":
            o.inc_idx = self.ncc
            self.ncc += 1
            ev = ("c", 0, o.inc_idx)
        else:
            ev = ("e", eng, idx)
        self.ops[eng].append(o)
        for b in reads:
            b.readers.append(ev)
        for b in writes:
            if b.first_w_op is None:
                b.first_w_op = (eng, idx)
            b.last_w = ev
            b.readers = []
        return ev

    def op(self, eng, fn, reads=(), writes=(), extra=(), tag=""):
        return self._rec(eng, fn, reads, writes, extra, "op", tag)

    def dma(self, q, fn, reads=(), writes=(), extra=(), tag="", fam=None):
        extra = set(extra)
        if fam in self.CHAINED and fam in self.fam_last:
            extra.add(self.fam_last[fam])
        ev = self._rec(q, fn, reads, writes, extra, "dma", tag, fam=fam)
        self.fam_last[fam] = ev
        return ev

    def cc(self, fn, reads=(), writes=(), extra=(), tag="cc"):
        return self._rec("pool", fn, reads, writes, extra, "cc", tag)

    def finalize(self, block, sems, dma_sems, cc_sem):
        for eng in ENGINES:
            for o in self.ops[eng]:
                for (k, src, idx) in o.deps:
                    if k == "e":
                        if src == "pe" and eng == "pe":
                            continue
                        self.ops[src][idx].inc = True
        for eng in ENGINES:
            n = 0
            for o in self.ops[eng]:
                if o.kind == "op" and o.inc:
                    n += 1
                    o.inc_idx = n

        ops = self.ops

        nfam = self.nfam
        wa = self.wait_all_fams

        def wait_value(ev):
            k, src, idx = ev
            if k == "e":
                o = ops[src][idx]
                assert o.inc, f"dep on non-inc op {src}:{idx}:{o.tag}"
                return ("e", src), o.inc_idx
            if k == "d":
                n = nfam[src] if src in wa else (idx + 1)
                return ("d", src), 16 * n
            return ("c", 0), idx + 1

        def make_body(eng):
            def body(engobj):
                hw = {}
                for o in ops[eng]:
                    for ev in sorted(o.deps, key=lambda t: (t[0], str(t[1]), t[2])):
                        if ev[0] == "e" and ev[1] == eng == "pe":
                            continue
                        if (o.kind == "dma" and ev[0] == "d"
                                and ev[1] == o.fam
                                and o.fam not in self.CHAINED):
                            continue
                        key, val = wait_value(ev)
                        if hw.get(key, 0) >= val:
                            continue
                        if key[0] == "e":
                            engobj.wait_ge(sems[key[1]], val)
                        elif key[0] == "d":
                            engobj.wait_ge(dma_sems[key[1]], val)
                        else:
                            engobj.wait_ge(cc_sem, val)
                        hw[key] = val
                    inst = o.fn(engobj)
                    if o.kind == "dma":
                        inst.then_inc(dma_sems[o.fam], 16)
                    elif o.kind == "cc":
                        inst.then_inc(cc_sem)
                    elif o.inc:
                        inst.then_inc(sems[eng], 1)
            return body

        block.tensor(make_body("pe"))
        block.scalar(make_body("act"))
        block.vector(make_body("dve"))
        block.gpsimd(make_body("pool"))
        block.sync(make_body("sp"))


class Arena:

    def __init__(self, budget):
        self.budget = budget
        self.reqs = []

    def place(self, size, ph0, ph1, name, mloc):
        size = (size + 31) // 32 * 32
        self.reqs.append((size, ph0, ph1, name, mloc))

    def commit(self, base_addr):
        items = []
        for (size, ph0, ph1, name, mloc) in sorted(
                self.reqs, key=lambda r: (-r[0], r[3])):
            taken = sorted((off, off + sz) for (off, sz, a, b, _n) in items
                           if not (b < ph0 or a > ph1))
            off = 0
            for s, e in taken:
                if off + size <= s:
                    break
                off = max(off, e)
            assert off + size <= self.budget, \
                f"SBUF overflow: {name} ({size}B, ph {ph0}-{ph1}) at {off}"
            items.append((off, size, ph0, ph1, name))
            mloc.addr = base_addr + off
        return items


def build_program(taps=(), upto=PH_END):
    bf = _bf()
    nc = bass.Bass("TRN2", target_bir_lowering=False)
    bld = Builder()
    bld.wait_all_fams = set()

    d_x = nc.dram_tensor("x", [NROWS, DM], F32, kind="ExternalInput")
    d_wip = nc.dram_tensor("wip_t", [DM, PROJ], F32, kind="ExternalInput")
    d_wop = nc.dram_tensor("wop_t", [INTER, DM], F32, kind="ExternalInput")
    d_wu = nc.dram_tensor("wu_t", [DM, DE], F32, kind="ExternalInput")
    d_wg = nc.dram_tensor("wg_t", [DM, DE], F32, kind="ExternalInput")
    d_wd = nc.dram_tensor("wd_t", [DE, DM], F32, kind="ExternalInput")
    d_wus = nc.dram_tensor("wu_s", [256, DE], F32, kind="ExternalInput")
    d_wgs = nc.dram_tensor("wg_s", [256, DE], F32, kind="ExternalInput")
    d_wds = nc.dram_tensor("wd_s", [1024, DM], F32, kind="ExternalInput")
    d_cw = nc.dram_tensor("conv_w", [128, 136], F32, kind="ExternalInput")
    d_cb = nc.dram_tensor("conv_b", [128, 34], F32, kind="ExternalInput")
    d_alog = nc.dram_tensor("alog", [64, 1], F32, kind="ExternalInput")
    d_dtb = nc.dram_tensor("dtb", [64, 1], F32, kind="ExternalInput")
    d_drow = nc.dram_tensor("d_row", [1, 64], F32, kind="ExternalInput")
    d_nw = nc.dram_tensor("normw", [128, 32], F32, kind="ExternalInput")
    d_e0m = nc.dram_tensor("e0m", [1, 8], F32, kind="ExternalInput")
    d_out = nc.dram_tensor("out", [NT, DM], F32, kind="ExternalOutput")

    d_sg = nc.dram_tensor("sg_scr", [INTER, NT], BF16)
    d_x1 = nc.dram_tensor("x1_scr", [NT, DM], F32)
    d_da = nc.dram_tensor("dacc_scr", [NT, DM], F32)
    d_ccin = nc.dram_tensor("cc_in", [128, 4100], F32)
    d_ccout = nc.dram_tensor("cc_out", [1024, 4100], F32, addr_space="Shared")

    c_idf = nc.inline_tensor(np.eye(128, dtype=np.float32), "identf")
    c_idb = nc.inline_tensor(np.eye(128, dtype=bf), "identb")
    c_tri = nc.inline_tensor(np.triu(np.ones((128, 128), dtype=bf)), "tri")
    sel_np = np.zeros((64, 64 * 128), dtype=bf)
    for h in range(64):
        sel_np[h, h * 128:(h + 1) * 128] = 1
    c_sel = nc.inline_tensor(sel_np, "selc")
    c_ocb = nc.inline_tensor(np.ones((128, 1), dtype=bf), "onescb")
    c_orf = nc.inline_tensor(np.ones((1, 128), dtype=np.float32), "onesrf")
    c_ocf = nc.inline_tensor(np.ones((128, 1), dtype=np.float32), "onescf")
    c_kc = nc.inline_tensor(np.tile(np.array([[EPS_LN, 1.0]], np.float32), (128, 1)), "kconst")

    ar = Arena(SBUF_BUDGET)
    arena = nc.alloc_sbuf_tensor("arena", [128, SBUF_BUDGET], mybir.dt.uint8)
    arena_addr = nc.lookup_mloc(arena).addr

    S = {}

    def place(name, shape, dtype, ph0, ph1):
        per_part = int(np.prod(shape[1:])) * mybir.dt.size(dtype)
        h = nc.virtual_sbuf_tensor(name, list(shape), dtype)
        nc._tensor(h.name, list(h.shape), h.dtype, type="SB")
        mloc = nc.lookup_mloc(h)
        mloc.allocated = True
        ar.place(per_part, ph0, ph1, name, mloc)
        S[name] = h
        return h

    s_idf = place("s_idf", [128, 128], F32, PH_P0, PH_END)
    s_idb = place("s_idb", [128, 128], BF16, PH_P0, PH_END)
    s_tri = place("s_tri", [128, 128], BF16, PH_P0, PH_P2B)
    s_ocb = place("s_ocb", [128, 1], BF16, PH_P0, PH_END)
    s_orf = place("s_orf", [1, 128], F32, PH_P0, PH_END)
    s_ocf = place("s_ocf", [128, 1], F32, PH_P0, PH_END)
    s_cw = place("s_cw", [128, 136], F32, PH_P0, PH_P1B)
    s_cb = place("s_cb", [128, 34], F32, PH_P0, PH_P1B)
    s_alog = place("s_alog", [64, 1], F32, PH_P0, PH_P1)
    s_dtb = place("s_dtb", [64, 1], F32, PH_P0, PH_P1)
    s_drow = place("s_drow", [1, 64], F32, PH_P0, PH_P2B)
    s_dbc = place("s_dbc", [128, 64], F32, PH_P0, PH_P2B)
    s_nw = place("s_nw", [128, 32], F32, PH_P0, PH_P3M)
    s_e0m = place("s_e0m", [1, 8], F32, PH_P0, PH_E0)
    s_mcols = place("s_mcols", [128, 8], F32, PH_E0, PH_E0)
    s_col = place("s_col", [128, 16], F32, PH_P0, PH_END)
    s_kc = place("s_kc", [128, 2], F32, PH_P0, PH_END)

    s_xlnT = place("s_xlnT", [128, KD * NROWS], BF16, PH_P0, PH_PG)
    s_xs = place("s_xs", [128, NCH * INTER], BF16, PH_P1B, PH_P2B)
    s_yg = place("s_yg", [128, 32 * NT], BF16, PH_P2B, PH_P3M)
    s_E = place("s_E", [128, H * 64], F32, PH_P2A, PH_P2B)
    s_sel = place("s_sel", [64, 64 * 128], BF16, PH_P2B, PH_P2B)

    s_xtm = [place(f"s_xtm{i}", [128, DM], F32, PH_P0, PH_P0) for i in range(2)]
    s_xln = [place(f"s_xln{i}", [128, DM], BF16, PH_P0, PH_P0) for i in range(2)]
    s_sq0 = [place(f"s_sq0{i}", [128, DM], BF16, PH_P0, PH_P0) for i in range(2)]
    s_wbf = [place(f"s_wbf{i}", [128, 2 * DM], BF16, PH_P1, PH_PG) for i in range(2)]
    s_raw = [place(f"s_raw{i}", [128, NROWS], BF16, PH_P1, PH_P1B) for i in range(3)]
    s_cacc = [place(f"s_cacc{i}", [128, NT], F32, PH_P1, PH_P1B) for i in range(2)]
    s_ca4 = [place(f"s_ca4{i}", [128, NT], F32, PH_P1, PH_P1B) for i in range(2)]
    s_hso = [place(f"s_hso{i}", [128, NT], BF16, PH_P1B, PH_P1B) for i in range(3)]
    s_Bfm = place("s_Bfm", [128, NT], BF16, PH_P1, PH_P2B)
    s_Cfm = place("s_Cfm", [128, NT], BF16, PH_P1, PH_P2B)
    s_sgb = [place(f"s_sgb{i}", [128, NT], BF16, PH_PG, PH_P2B) for i in range(2)]

    s_dtfm = place("s_dtfm", [64, NT], F32, PH_P1, PH_P1)
    s_sp1 = place("s_sp1", [64, 512], F32, PH_P1, PH_P1)
    s_sp2 = place("s_sp2", [64, 512], F32, PH_P1, PH_P1)
    s_sp3 = place("s_sp3", [64, 512], F32, PH_P1, PH_P1)
    s_negA = place("s_negA", [64, 1], F32, PH_P1, PH_P1)
    s_adt = place("s_adt", [64, NT], F32, PH_P1, PH_P1)
    s_acum = place("s_acum", [64, NT], F32, PH_P1, PH_P1)
    s_z64 = place("s_z64", [64, 128], F32, PH_P0, PH_P1)
    s_dsf = place("s_dsf", [64, 128], F32, PH_P1, PH_P1)
    s_dsf2 = place("s_dsf2", [64, 128], F32, PH_P1, PH_P1)
    s_lrow = place("s_lrow", [1, 64], F32, PH_P1, PH_P1)
    s_actm = place("s_actm", [128, NCH * 64], F32, PH_P1, PH_P2B)
    s_dttm = place("s_dttm", [128, NCH * 64], F32, PH_P1, PH_P2B)
    s_dsdt = place("s_dsdt", [128, NCH * 64], F32, PH_P1, PH_P2B)
    s_lam = place("s_lam", [128, NCH * 64], F32, PH_P1, PH_P2B)
    s_btm = place("s_btm", [128, NCH * 128], BF16, PH_P1, PH_P2B)
    s_gmt = place("s_gmt", [128, NCH * 128], BF16, PH_P1, PH_P2B)
    s_hi = place("s_hi", [64, NCH * 128], BF16, PH_P1, PH_P2B)
    s_lo = place("s_lo", [64, NCH * 128], BF16, PH_P1, PH_P2B)

    s_bd = [place(f"s_bd{i}", [128, 128], BF16, PH_P2A, PH_P2B) for i in range(4)]
    s_xc = [place(f"s_xc{i}", [128, 64], BF16, PH_P2A, PH_P2B) for i in range(4)]
    s_scr = [place(f"s_scr{i}", [128, 128], F32, PH_P2B, PH_P2B) for i in range(4)]
    s_et = [place(f"s_et{i}", [128, 128], BF16, PH_P2B, PH_P2B) for i in range(4)]
    s_sdt = [place(f"s_sdt{i}", [128, 128], BF16, PH_P2B, PH_P2B) for i in range(4)]
    s_mt = [place(f"s_mt{i}", [128, 128], BF16, PH_P2B, PH_P2B) for i in range(4)]
    s_cd = [place(f"s_cd{i}", [128, 128], BF16, PH_P2B, PH_P2B) for i in range(4)]
    s_ebf = [place(f"s_ebf{i}", [128, 64], BF16, PH_P2B, PH_P2B) for i in range(4)]
    s_di = [place(f"s_di{i}", [128, 128], BF16, PH_P2B, PH_P2B) for i in range(4)]
    s_slot = [place(f"s_slot{i}", [128, 4096], F32, PH_E0, PH_E0) for i in range(2)]

    s_spc = [place(f"s_spc{i}", [128, 2048], F32, PH_P1, PH_P2A) for i in range(2)]
    s_part = place("s_part", [128, 4], F32, PH_P0, PH_CC)
    s_spall = place("s_spall", [128, 24], F32, PH_E0, PH_E0)
    s_sp3r = place("s_sp3r", [128, 3], F32, PH_E0, PH_E0)
    s_srow = place("s_srow", [1, 8], F32, PH_E0, PH_P4B)
    s_bcols = place("s_bcols", [128, 12], F32, PH_E0, PH_P4B)

    s_wopg = [place(f"s_wopg{i}", [128, 4 * DM], BF16, PH_P3M, PH_P3M)
              for i in range(2)]
    s_oacc = place("s_oacc", [128, 8 * DM], F32, PH_P3M, PH_P3T)
    s_rb = place("s_rb", [128, NT], F32, PH_P3M, PH_P3M)
    s_r1 = place("s_r1", [1, 2 * NT], F32, PH_P3M, PH_P3M)
    s_rsum = place("s_rsum", [1, NT], F32, PH_P2B, PH_P3M)
    s_ys = [place(f"s_ys{i}", [128, NT], BF16, PH_P2B, PH_P3M) for i in range(2)]
    s_wops = [place(f"s_wops{i}", [128, DM], F32, PH_P3M, PH_P3M) for i in range(2)]
    s_xrd = [place(f"s_xrd{i}", [128, DM], F32, PH_P3T, PH_P3T) for i in range(2)]
    s_t1 = [place(f"s_t1{i}", [128, DM], F32, PH_P3T, PH_P3T) for i in range(2)]
    s_x1b = [place(f"s_x1b{i}", [128, DM], F32, PH_P3T, PH_P3T) for i in range(2)]
    s_xl2 = [place(f"s_xl2{i}", [128, DM], BF16, PH_P3T, PH_P3T) for i in range(2)]
    s_sqb = [place(f"s_sqb{i}", [128, DM], BF16, PH_P3T, PH_P4B) for i in range(2)]

    s_xl2T = place("s_xl2T", [128, KD * NT], BF16, PH_P3T, PH_P4A)
    s_hid = place("s_hid", [128, 64 * NT], BF16, PH_P4A, PH_P4B)
    s_ust = [place(f"s_ust{i}", [128, 1024], F32, PH_P4A, PH_P4A) for i in range(2)]
    s_gst = [place(f"s_gst{i}", [128, 1024], F32, PH_P4A, PH_P4A) for i in range(2)]
    s_q2 = place("s_q2", [128, 1024], BF16, PH_P4A, PH_P4A)
    s_qu = [place(f"s_qu{i}", [128, 2048], BF16, PH_P4A, PH_P4A) for i in range(2)]
    s_qg = [place(f"s_qg{i}", [128, 2048], BF16, PH_P4A, PH_P4A) for i in range(2)]
    s_hs2 = [place(f"s_hs2{i}", [128, 512], BF16, PH_P4A, PH_P4A) for i in range(2)]
    s_dst = [place(f"s_dst{i}", [128, 512], F32, PH_P4B, PH_P4B) for i in range(3)]
    s_q2d = [place(f"s_q2d{i}", [128, 512], BF16, PH_P4B, PH_P4B) for i in range(2)]
    s_qd = [place(f"s_qd{i}", [128, 512], BF16, PH_P4B, PH_P4B) for i in range(3)]
    s_dev = [place(f"s_dev{i}", [128, 512], F32, PH_P4B, PH_P4B) for i in range(2)]
    s_l3s = place("s_l3s", [128, 16], F32, PH_P4B, PH_P4B)
    s_l3q = place("s_l3q", [128, 16], F32, PH_P4B, PH_P4B)
    s_dsq = [place(f"s_dsq{i}", [128, 512], BF16, PH_P4B, PH_P4B) for i in range(2)]
    s_dar = [place(f"s_dar{i}", [128, DM], F32, PH_P4B, PH_P4B) for i in range(2)]
    s_x1r = [place(f"s_x1r{i}", [128, DM], F32, PH_P4B, PH_P4B) for i in range(2)]
    s_fin = [place(f"s_fin{i}", [128, DM], F32, PH_P4B, PH_P4B) for i in range(2)]

    pb = [nc.alloc_psum_tensor(f"pb{i}", [128, 512], F32) for i in range(8)]
    pbb = [t.ap().bitcast(BF16) for t in pb]

    B = {}

    def g(name):
        if name not in B:
            B[name] = Buf(name)
        return B[name]

    def E(eng, fn, reads=(), writes=(), extra=(), tag=""):
        return bld.op(eng, fn, [g(r) for r in reads], [g(w) for w in writes], extra, tag)

    def D(q, fn, reads=(), writes=(), extra=(), tag="", fam=None):
        return bld.dma(q, fn, [g(r) for r in reads], [g(w) for w in writes],
                       extra, tag, fam=fam or tag)

    tap_t = {}

    def add_tap(name, shape, dtype):
        tap_t[name] = nc.dram_tensor("tap_" + name, list(shape), dtype,
                                     kind="ExternalOutput")

    def tap(name, ap_fn, reads):
        if name in taps:
            D("sp", lambda e, t=tap_t[name], f=ap_fn: e.dma_start(out=t.ap(), in_=f()),
              reads=reads, tag=f"tap_{name}", fam="tap")

    if taps:
        add_tap("dacc", [NT, DM], F32)
        add_tap("l3s", [128, 16], F32)
        add_tap("l3q", [128, 16], F32)
        add_tap("xlnT", [128, KD * NROWS], BF16)
        add_tap("dtfm", [64, NT], F32)
        add_tap("acum", [64, NT], F32)
        add_tap("Bfm", [128, NT], BF16)
        add_tap("Cfm", [128, NT], BF16)
        add_tap("xs", [128, NCH * INTER], BF16)
        add_tap("E", [128, H * 64], F32)
        add_tap("yg", [128, 32 * NT], BF16)
        add_tap("oacc", [128, 8 * DM], F32)
        add_tap("hid", [128, 64 * NT], BF16)
        add_tap("srow", [1, 8], F32)
        add_tap("gmt", [128, NCH * 128], BF16)
        add_tap("hilo", [64, NCH * 128], BF16)
        add_tap("actm", [128, NCH * 64], F32)
        add_tap("dsdt", [128, NCH * 64], F32)
        add_tap("lam", [128, NCH * 64], F32)

    for nm, dd in [("s_idf", c_idf), ("s_idb", c_idb), ("s_tri", c_tri),
                   ("s_ocb", c_ocb), ("s_orf", c_orf), ("s_ocf", c_ocf),
                   ("s_cw", d_cw), ("s_cb", d_cb), ("s_alog", d_alog),
                   ("s_dtb", d_dtb), ("s_drow", d_drow), ("s_nw", d_nw),
                   ("s_e0m", d_e0m), ("s_kc", c_kc)]:
        D("sp", lambda e, s=S[nm], dd=dd: e.dma_start(out=s.ap(), in_=dd.ap()),
          [], [nm], tag="ldc", fam="ldc" + str(len(B) % 2))
    E("dve", lambda e: e.memset(s_z64.ap(), 0.0), [], ["s_z64"], tag="z64")
    E("dve", lambda e: e.memset(s_part[:, 0:3], 0.0),
      [], ["s_part#0", "s_part#1", "s_part#2"], tag="partz")

    E("pe", lambda e: e.matmul(pb[7][:, 0:64], s_orf.ap(), s_drow.ap(),
                               start=True, stop=True),
      ["s_orf", "s_drow"], ["pb7"], tag="dbc_mm")
    E("dve", lambda e: e.tensor_copy(s_dbc.ap(), pb[7][:, 0:64]),
      ["pb7"], ["s_dbc"], tag="dbc_ev")

    def ln_stats(src_ap, src_bufs, sqbuf, sqname, tagp, cb=0):
        Fdim = src_ap.shape[-1]

        def cc_(i):
            return s_col[:, cb + i:cb + i + 1]

        def cn_(i):
            return f"s_col#{cb + i}"

        E("dve", lambda e: e.tensor_reduce(cc_(0), src_ap, AX.X, ALU.add),
          src_bufs, [cn_(0)], tag=tagp + "_s")
        E("act", lambda e: e.activation(sqbuf, src_ap, AF.Square,
                                        accum_out=cc_(1)),
          src_bufs, [sqname, cn_(1)], tag=tagp + "_q")
        E("dve", lambda e: e.tensor_scalar(cc_(2), cc_(0),
                                           1.0 / Fdim, None, ALU.mult),
          [cn_(0)], [cn_(2)], tag=tagp + "_m")
        E("dve", lambda e: e.tensor_tensor(cc_(4), cc_(2), cc_(2), ALU.mult),
          [cn_(2)], [cn_(4)], tag=tagp + "_m2")
        E("dve", lambda e: e.tensor_scalar(cc_(5), cc_(1),
                                           1.0 / Fdim, None, ALU.mult),
          [cn_(1)], [cn_(5)], tag=tagp + "_v1")
        E("dve", lambda e: e.tensor_tensor(cc_(5), cc_(5), cc_(4), ALU.subtract),
          [cn_(4), cn_(5)], [cn_(5)], tag=tagp + "_v2")
        E("act", lambda e: e.activation(cc_(6), cc_(5), AF.Sqrt,
                                        bias=s_kc[:, 0:1]),
          [cn_(5), "s_kc"], [cn_(6)], tag=tagp + "_sd")
        E("dve", lambda e: e.reciprocal(cc_(7), cc_(6)), [cn_(6)], [cn_(7)],
          tag=tagp + "_r")

    for ti in range(9):
        i2 = ti % 2
        xb, xl, sq = s_xtm[i2], s_xln[i2], s_sq0[i2]
        xbn, xln_n, sqn = f"s_xtm{i2}", f"s_xln{i2}", f"s_sq0{i2}"
        D("sp", lambda e, ti=ti, xb=xb: e.dma_start(
            out=xb.ap(), in_=d_x[ti * 128:(ti + 1) * 128, :]), [], [xbn],
          tag=f"ldx{ti}", fam=xbn)
        ln_stats(xb.ap(), [xbn], sq.ap(), sqn, f"ln1_{ti}")
        E("dve", lambda e, xb=xb, xl=xl: e.tensor_scalar(
            xl.ap(), xb.ap(), s_col[:, 2:3], s_col[:, 7:8], ALU.subtract, ALU.mult),
          [xbn, "s_col#2", "s_col#7"], [xln_n], tag=f"ln1a_{ti}")
        for kc in range(KD):
            bk = 6 + (kc % 2)
            E("pe", lambda e, xl=xl, kc=kc, bk=bk: e.transpose(
                pbb[bk][:, 0:128], xl[:, kc * 128:(kc + 1) * 128], s_idb.ap()),
              [xln_n, "s_idb"], [f"pb{bk}"], tag="xt")
            E("dve", lambda e, kc=kc, ti=ti, bk=bk: e.tensor_copy(
                s_xlnT[:, kc * NROWS + ti * 128: kc * NROWS + (ti + 1) * 128],
                pbb[bk][:, 0:128]),
              [f"pb{bk}"], [f"s_xlnT#{kc}"], tag="xte")
    tap("xlnT", lambda: s_xlnT.ap(), [f"s_xlnT#{k}" for k in range(KD)])

    for wi, (dsrc, nrow) in enumerate([(d_wus, 256), (d_wgs, 256), (d_wds, 1024)]):
        ncol = DE if wi < 2 else DM
        rb_, cb_ = nrow // 128, ncol // 2048
        for pi in range(8):
            a, b2 = pi // cb_, pi % cb_
            buf = s_spc[pi % 2]
            bn = f"s_spc{pi % 2}"
            D("act", lambda e, dsrc=dsrc, a=a, b2=b2, buf=buf: e.dma_start(
                out=buf.ap(),
                in_=dsrc[a * 128:(a + 1) * 128, b2 * 2048:(b2 + 1) * 2048]),
              [], [bn], tag=f"scld{wi}_{pi}", fam=bn)
            E("dve", lambda e, buf=buf: e.tensor_reduce(
                s_part[:, 3:4], buf.ap(), AX.X, ALU.add, apply_absolute_value=True),
              [bn], ["s_part#3"], tag=f"scr{wi}_{pi}")
            E("dve", lambda e, wi=wi: e.tensor_tensor(
                s_part[:, wi:wi + 1], s_part[:, wi:wi + 1], s_part[:, 3:4], ALU.add),
              ["s_part#3", f"s_part#{wi}"], [f"s_part#{wi}"], tag=f"sca{wi}_{pi}")

    if upto < PH_P1:
        return _finish(nc, bld, tap_t, ar, arena_addr, B)

    lpair = {}

    def inproj_load(fidxs, pi):
        fws = [64 if f == 66 else 128 for f in fidxs]
        fw = sum(fws)
        f0 = fidxs[0] * 128
        wbf = s_wbf[pi]
        wbfn = f"s_wbf{pi}"
        src = d_wip.ap()[:, f0:f0 + fw].rearrange("(kc kp) f -> kp kc f", kp=128)
        D("pool", lambda e, src=src, wbf=wbf, fw=fw: e.dma_start(
            out=wbf[:, 0:KD * fw].rearrange("p (kc f) -> p kc f", f=fw), in_=src),
          [], [wbfn], tag=f"ipld{fidxs[0]}", fam=wbfn)
        for k, f in enumerate(fidxs):
            lpair[f] = (pi, k * fws[0], fw)

    def emit_inproj(fidx, kind, slab_i=0, tp_thunks=None):
        fw = 64 if fidx == 66 else 128
        pi, sub, pfw = lpair[fidx]
        wbf = s_wbf[pi]
        wbfn = f"s_wbf{pi}"
        halo = kind in ("B", "C", "hs")
        bs = ((fidx % 2) * 2) if kind == "gate" else (0 if fidx % 2 == 0 else 3)
        for kc in range(KD):
            if tp_thunks and kc % 2 == 1:
                tp_thunks.pop(0)()
            st, sp_ = (kc == 0), (kc == KD - 1)
            lsl = slice(kc * pfw + sub, kc * pfw + sub + fw)
            E("pe", lambda e, kc=kc, fw=fw, wbf=wbf, bs=bs, st=st, sp_=sp_,
              lsl=lsl: e.matmul(
                pb[bs][0:fw, 0:512], wbf[:, lsl],
                s_xlnT[:, kc * NROWS + 128: kc * NROWS + 640], start=st, stop=sp_),
              [wbfn, f"s_xlnT#{kc}"], [f"pb{bs}"], tag=f"ipA{fidx}")
            E("pe", lambda e, kc=kc, fw=fw, wbf=wbf, bs=bs, st=st, sp_=sp_,
              lsl=lsl: e.matmul(
                pb[bs + 1][0:fw, 0:512], wbf[:, lsl],
                s_xlnT[:, kc * NROWS + 640: kc * NROWS + 1152], start=st, stop=sp_),
              [wbfn, f"s_xlnT#{kc}"], [f"pb{bs + 1}"], tag=f"ipB{fidx}")
            if halo:
                E("pe", lambda e, kc=kc, fw=fw, wbf=wbf, bs=bs, st=st, sp_=sp_,
                  lsl=lsl: e.matmul(
                    pb[bs + 2][0:fw, 0:128], wbf[:, lsl],
                    s_xlnT[:, kc * NROWS: kc * NROWS + 128], start=st, stop=sp_),
                  [wbfn, f"s_xlnT#{kc}"], [f"pb{bs + 2}"], tag=f"ipH{fidx}")

        if kind == "dt":
            for tt in range(2):
                bk = bs + tt
                pp = pb[bk]
                E("act", lambda e, pp=pp: e.activation(
                    s_sp1.ap(), pp[0:64, 0:512], AF.Abs, bias=s_dtb.ap()),
                  [f"pb{bk}", "s_dtb"], ["s_sp1"], tag="dt_abs")
                E("act", lambda e: e.activation(s_sp2.ap(), s_sp1.ap(), AF.Exp,
                                                scale=-1.0),
                  ["s_sp1"], ["s_sp2"], tag="dt_exp")
                E("act", lambda e: e.activation(s_sp1.ap(), s_sp2.ap(), AF.Ln,
                                                bias=s_kc[0:64, 1:2]),
                  ["s_sp2", "s_kc"], ["s_sp1"], tag="dt_ln")
                E("act", lambda e, pp=pp: e.activation(
                    s_sp3.ap(), pp[0:64, 0:512], AF.Relu, bias=s_dtb.ap()),
                  [f"pb{bk}", "s_dtb"], ["s_sp3"], tag="dt_rel")
                E("dve", lambda e, tt=tt: e.tensor_tensor(
                    s_dtfm[:, tt * 512:(tt + 1) * 512], s_sp1.ap(), s_sp3.ap(),
                    ALU.add),
                  ["s_sp1", "s_sp3"], ["s_dtfm"], tag="dt_add")
            return

        if kind == "gate":
            j = fidx
            sgb = s_sgb[j % 2]
            sgn = f"s_sgb{j % 2}"
            E("act", lambda e, sgb=sgb, bs=bs: e.activation(
                sgb[:, 0:512], pb[bs][0:128, 0:512], AF.Silu),
              [f"pb{bs}"], [sgn], tag=f"g_s0_{j}")
            E("act", lambda e, sgb=sgb, bs=bs: e.activation(
                sgb[:, 512:1024], pb[bs + 1][0:128, 0:512], AF.Silu),
              [f"pb{bs + 1}"], [sgn], tag=f"g_s1_{j}")
            D("sp", lambda e, sgb=sgb, j=j: e.dma_start(
                out=d_sg[j * 128:(j + 1) * 128, :], in_=sgb.ap()),
              [sgn], [f"dsg{j % 2}"], tag=f"g_st_{j}", fam=f"dsg{j % 2}")
            return

        slab = s_raw[slab_i % 3]
        slabn = f"s_raw{slab_i % 3}"
        E("dve", lambda e, slab=slab, bs=bs: e.tensor_copy(
            slab[:, 0:128], pb[bs + 2][0:128, 0:128]),
          [f"pb{bs + 2}"], [slabn], tag="cv_e0")
        E("dve", lambda e, slab=slab, bs=bs: e.tensor_copy(
            slab[:, 128:640], pb[bs][0:128, 0:512]),
          [f"pb{bs}"], [slabn], tag="cv_e1")
        E("dve", lambda e, slab=slab, bs=bs: e.tensor_copy(
            slab[:, 640:1152], pb[bs + 1][0:128, 0:512]),
          [f"pb{bs + 1}"], [slabn], tag="cv_e2")
        j34 = fidx - 32
        ca, cb2 = s_cacc[0], s_cacc[1]
        a0, a1 = s_ca4[0], s_ca4[1]
        E("act", lambda e, slab=slab, j=j34, a0=a0: e.activation(
            a0.ap(), slab[:, 125:1149], AF.Copy, scale=s_cw[:, j * 4: j * 4 + 1]),
          [slabn, "s_cw"], ["s_ca40"], tag="cv_t0")
        E("act", lambda e, slab=slab, j=j34, a1=a1: e.activation(
            a1.ap(), slab[:, 126:1150], AF.Copy,
            scale=s_cw[:, j * 4 + 1: j * 4 + 2]),
          [slabn, "s_cw"], ["s_ca41"], tag="cv_t1")
        E("pool", lambda e, a0=a0, a1=a1, ca=ca: e.tensor_tensor(
            ca.ap(), a0.ap(), a1.ap(), ALU.add),
          ["s_ca40", "s_ca41"], ["s_cacc0"], tag="cv_p01")
        E("dve", lambda e, slab=slab, j=j34, ca=ca, cb2=cb2: e.scalar_tensor_tensor(
            cb2.ap(), slab[:, 127:1151], s_cw[:, j * 4 + 2: j * 4 + 3], ca.ap(),
            ALU.mult, ALU.add),
          [slabn, "s_cacc0", "s_cw"], ["s_cacc1"], tag="cv_t2")
        E("dve", lambda e, slab=slab, j=j34, ca=ca, cb2=cb2: e.scalar_tensor_tensor(
            ca.ap(), slab[:, 128:1152], s_cw[:, j * 4 + 3: j * 4 + 4], cb2.ap(),
            ALU.mult, ALU.add),
          [slabn, "s_cacc1", "s_cw"], ["s_cacc0"], tag="cv_t3")
        if kind == "B":
            dst, dstn = s_Bfm, "s_Bfm"
        elif kind == "C":
            dst, dstn = s_Cfm, "s_Cfm"
        else:
            dst, dstn = s_hso[slab_i % 3], f"s_hso{slab_i % 3}"
        E("act", lambda e, dst=dst, j=j34, ca=ca: e.activation(
            dst.ap(), ca.ap(), AF.Silu, bias=s_cb[:, j:j + 1]),
          ["s_cacc0", "s_cb"], [dstn], tag=f"cv_si{fidx}")
        if kind == "hs":
            return (dst, dstn, fidx - 32)

    inproj_load([66], 0)
    emit_inproj(66, "dt")
    inproj_load([64, 65], 1)
    emit_inproj(64, "B", 0)
    emit_inproj(65, "C", 1)
    tap("dtfm", lambda: s_dtfm.ap(), ["s_dtfm"])
    tap("Bfm", lambda: s_Bfm.ap(), ["s_Bfm"])
    tap("Cfm", lambda: s_Cfm.ap(), ["s_Cfm"])

    E("act", lambda e: e.activation(s_sp1[:, 0:1], s_alog.ap(), AF.Exp),
      ["s_alog"], ["s_sp1"], tag="negA_e")
    E("dve", lambda e: e.tensor_scalar(s_negA.ap(), s_sp1[:, 0:1], -1.0, None,
                                       ALU.mult),
      ["s_sp1"], ["s_negA"], tag="negA_m")
    E("dve", lambda e: e.tensor_scalar(s_adt.ap(), s_dtfm.ap(), s_negA.ap(), None,
                                       ALU.mult),
      ["s_dtfm", "s_negA"], ["s_adt"], tag="adt")

    inproj_load([32, 33], 0)
    for c in range(NCH):
        cs = slice(c * 128, (c + 1) * 128)
        hc = slice(c * 64, (c + 1) * 64)
        E("dve", lambda e, cs=cs: e.tensor_tensor_scan(
            s_acum[:, cs], s_adt[:, cs], s_z64.ap(), 0.0, ALU.add, ALU.add),
          ["s_adt", "s_z64"], ["s_acum"], tag=f"scan{c}")
        E("dve", lambda e, c=c, cs=cs: e.tensor_scalar(
            s_dsf.ap(), s_acum[:, cs], s_acum[:, c * 128 + 127: c * 128 + 128],
            None, ALU.subtract),
          ["s_acum"], ["s_dsf"], tag=f"ds1_{c}")
        E("act", lambda e: e.activation(s_dsf2.ap(), s_dsf.ap(), AF.Exp, scale=-1.0),
          ["s_dsf"], ["s_dsf2"], tag=f"ds2_{c}")
        E("pe", lambda e, c=c: e.transpose(
            pb[5][0:1, 0:64], s_acum[:, c * 128 + 127: c * 128 + 128],
            s_idf[0:64, 0:64]),
          ["s_acum", "s_idf"], ["pb5"], tag=f"lamt{c}")
        E("dve", lambda e: e.tensor_copy(s_lrow.ap(), pb[5][0:1, 0:64]),
          ["pb5"], ["s_lrow"], tag=f"lame{c}")
        E("pe", lambda e: e.matmul(pb[5][:, 64:128], s_orf.ap(), s_lrow.ap(),
                                   start=True, stop=True),
          ["s_lrow", "s_orf"], ["pb5"], tag=f"lamb{c}")
        E("act", lambda e, hc=hc: e.activation(
            s_lam[:, hc], pb[5][:, 64:128], AF.Exp),
          ["pb5"], ["s_lam"], tag=f"lamx{c}")
        E("pe", lambda e, cs=cs: e.transpose(pb[4][:, 0:64], s_acum[:, cs],
                                             s_idf[0:64, 0:64]),
          ["s_acum", "s_idf"], ["pb4"], tag=f"at{c}")
        E("dve", lambda e, hc=hc: e.tensor_copy(s_actm[:, hc], pb[4][:, 0:64]),
          ["pb4"], ["s_actm"], tag=f"ate{c}")
        E("pe", lambda e, cs=cs: e.transpose(pb[4][:, 64:128], s_dtfm[:, cs],
                                             s_idf[0:64, 0:64]),
          ["s_dtfm", "s_idf"], ["pb4"], tag=f"dtt{c}")
        E("dve", lambda e, hc=hc: e.tensor_copy(s_dttm[:, hc], pb[4][:, 64:128]),
          ["pb4"], ["s_dttm"], tag=f"dtte{c}")
        E("pe", lambda e: e.transpose(pb[4][:, 128:192], s_dsf2.ap(),
                                      s_idf[0:64, 0:64]),
          ["s_dsf2", "s_idf"], ["pb4"], tag=f"dst{c}")
        E("dve", lambda e, hc=hc: e.tensor_tensor(
            s_dsdt[:, hc], pb[4][:, 128:192], s_dttm[:, hc], ALU.mult),
          ["pb4", "s_dttm"], ["s_dsdt"], tag=f"dsdt{c}")
        E("dve", lambda e, cs=cs: e.tensor_copy(s_hi[:, cs], s_acum[:, cs]),
          ["s_acum"], ["s_hi"], tag=f"hi{c}")
        E("dve", lambda e, cs=cs: e.tensor_tensor(
            s_lo[:, cs], s_acum[:, cs], s_hi[:, cs], ALU.subtract),
          ["s_acum", "s_hi"], ["s_lo"], tag=f"lo{c}")
        E("pe", lambda e, cs=cs: e.matmul(pb[6][0:128, 0:128], s_Bfm[:, cs],
                                          s_Cfm[:, cs], start=True, stop=True),
          ["s_Bfm", "s_Cfm"], ["pb6"], tag=f"gm{c}")
        E("dve", lambda e, cs=cs: e.tensor_tensor(
            s_gmt[:, cs], pb[6][0:128, 0:128], s_tri.ap(), ALU.mult),
          ["pb6", "s_tri"], ["s_gmt"], tag=f"gme{c}")
        E("pe", lambda e, cs=cs: e.transpose(pbb[7][:, 0:128], s_Bfm[:, cs],
                                             s_idb.ap()),
          ["s_Bfm", "s_idb"], ["pb7"], tag=f"bt{c}")
        E("dve", lambda e, cs=cs: e.tensor_copy(s_btm[:, cs], pbb[7][:, 0:128]),
          ["pb7"], ["s_btm"], tag=f"bte{c}")

    tap("acum", lambda: s_acum.ap(), ["s_acum"])
    tap("gmt", lambda: s_gmt.ap(), ["s_gmt"])
    tap("hilo", lambda: s_hi.ap(), ["s_hi"])
    tap("actm", lambda: s_actm.ap(), ["s_actm"])
    tap("dsdt", lambda: s_dsdt.ap(), ["s_dsdt"])
    tap("lam", lambda: s_lam.ap(), ["s_lam"])

    if upto < PH_P1B:
        return _finish(nc, bld, tap_t, ar, arena_addr, B)

    def hs_transpose_thunks(dst, dstn, jj):
        thunks = []
        for c in range(NCH):
            bk = 6 + (c % 2)

            def thunk(c=c, bk=bk, dst=dst, dstn=dstn, jj=jj):
                E("pe", lambda e, dst=dst, c=c, bk=bk: e.transpose(
                    pbb[bk][:, 0:128], dst[:, c * 128:(c + 1) * 128], s_idb.ap()),
                  [dstn, "s_idb"], [f"pb{bk}"], tag="hs_t")
                E("dve", lambda e, c=c, jj=jj, bk=bk: e.tensor_copy(
                    s_xs[:, c * INTER + jj * 128: c * INTER + jj * 128 + 128],
                    pbb[bk][:, 0:128]),
                  [f"pb{bk}"], [f"s_xs#{c}"], tag="hs_te")
            thunks.append(thunk)
        return thunks

    hs_pend = []
    for j, fidx in enumerate(range(32, 64)):
        if fidx % 2 == 0 and fidx + 2 < 64:
            inproj_load([fidx + 2, fidx + 3], (fidx // 2 + 1) % 2)
        tps_ = hs_pend.pop(0) if len(hs_pend) > 2 else None
        hs_pend.append(hs_transpose_thunks(*emit_inproj(fidx, "hs", slab_i=j,
                                                        tp_thunks=tps_)))
    for item in hs_pend:
        for t_ in item:
            t_()
    tap("xs", lambda: s_xs.ap(), [f"s_xs#{c}" for c in range(NCH)])

    if upto < PH_P2A:
        return _finish(nc, bld, tap_t, ar, arena_addr, B)

    E("dve", lambda e: e.memset(s_E.ap(), 0.0), [],
      [f"s_E#{h}" for h in range(H)], tag="Ez")

    def p2a_s0(it):
        c, h = divmod(it, H)
        i2 = it % 4
        hcol = slice(c * 64 + h, c * 64 + h + 1)
        E("dve", lambda e, i2=i2, c=c, h=h, hcol=hcol: e.tensor_scalar(
            s_xc[i2].ap(), s_xs[:, c * INTER + h * 64: c * INTER + h * 64 + 64],
            s_dsdt[:, hcol], None, ALU.mult),
          [f"s_xs#{c}", "s_dsdt"], [f"s_xc{i2}"], tag=f"Axc{it}")

    def p2a_s1(it):
        c, h = divmod(it, H)
        sb_ = 4 + (it % 2)
        cs = slice(c * 128, (c + 1) * 128)
        E("pe", lambda e, sb_=sb_, cs=cs, i2=it % 4: e.matmul(
            pb[sb_][0:128, 0:64], s_btm[:, cs], s_xc[i2].ap(),
            start=True, stop=True),
          ["s_btm", f"s_xc{it % 4}"], [f"pb{sb_}"], tag=f"Amm{it}")

    def p2a_s2(it):
        c, h = divmod(it, H)
        sb_ = 4 + (it % 2)
        hcol = slice(c * 64 + h, c * 64 + h + 1)
        E("dve", lambda e, h=h, hcol=hcol, sb_=sb_: e.scalar_tensor_tensor(
            s_E[:, h * 64:(h + 1) * 64], s_E[:, h * 64:(h + 1) * 64],
            s_lam[:, hcol], pb[sb_][0:128, 0:64], ALU.mult, ALU.add),
          [f"s_E#{h}", "s_lam", f"pb{sb_}"], [f"s_E#{h}"], tag=f"Aeu{it}")

    NITA = NCH * H
    for k in range(NITA + 2):
        if k < NITA:
            p2a_s0(k)
        if 1 <= k <= NITA:
            p2a_s1(k - 1)
        if 2 <= k <= NITA + 1:
            p2a_s2(k - 2)
        if k % 40 == 0 and k // 40 < 8:
            g2 = (k // 40) * 2
            inproj_load([g2, g2 + 1], (k // 40) % 2)
        if k % 40 == 12 and k // 40 < 8:
            emit_inproj((k // 40) * 2, "gate")
        if k % 40 == 28 and k // 40 < 8:
            emit_inproj((k // 40) * 2 + 1, "gate")

    if upto < PH_CC:
        return _finish(nc, bld, tap_t, ar, arena_addr, B)

    D("act", lambda e: e.dma_start(out=d_ccin.ap()[:, 0:4096], in_=s_E.ap()),
      [f"s_E#{h}" for h in range(H)], ["dccin"], tag="ccs_e", fam="ccin")
    D("act", lambda e: e.dma_start(out=d_ccin.ap()[:, 4096:4100],
                                   in_=s_part[:, 0:4]),
      ["s_part#0", "s_part#1", "s_part#2", "s_part#3"], ["dccin"],
      tag="ccs_p", fam="ccin")
    bld.cc(lambda e: e.collective_compute(
        "AllGather", ALU.bypass, replica_groups=[list(range(8))],
        ins=[d_ccin.ap().opt()], outs=[d_ccout.ap().opt()]),
        reads=[g("dccin")], writes=[g("dccout")], tag="allgather")
    for g2 in range(8, 16):
        inproj_load([g2 * 2, g2 * 2 + 1], g2 % 2)
        emit_inproj(g2 * 2, "gate")
        emit_inproj(g2 * 2 + 1, "gate")

    if upto < PH_E0:
        return _finish(nc, bld, tap_t, ar, arena_addr, B)

    E("dve", lambda e: e.memset(s_E.ap(), 0.0), [],
      [f"s_E#{h}" for h in range(H)], tag="E0z")
    E("pe", lambda e: e.matmul(pb[5][:, 0:8], s_orf.ap(), s_e0m.ap(),
                               start=True, stop=True),
      ["s_orf", "s_e0m"], ["pb5"], tag="e0m_mm")
    E("dve", lambda e: e.tensor_copy(s_mcols.ap(), pb[5][:, 0:8]),
      ["pb5"], ["s_mcols"], tag="e0m_ev")
    allE = [f"s_E#{h}" for h in range(H)]
    for q in range(8):
        i2 = q % 2
        sl = s_slot[i2]
        sln = f"s_slot{i2}"
        D("act", lambda e, q=q, sl=sl: e.dma_start(
            out=sl.ap(), in_=d_ccout[q * 128:(q + 1) * 128, 0:4096]),
          ["dccout"], [sln], tag=f"e0ld{q}", fam=sln)
        E("dve", lambda e, q=q, sl=sl: e.scalar_tensor_tensor(
            s_E.ap(), sl.ap(), s_mcols[:, q:q + 1], s_E.ap(),
            ALU.mult, ALU.add),
          [sln, "s_mcols"] + allE, allE, tag=f"e0a{q}")
    tap("E", lambda: s_E.ap(), [f"s_E#{h}" for h in range(H)])

    for q in range(8):
        D("pool", lambda e, q=q: e.dma_start(
            out=s_spall[:, q * 3:(q + 1) * 3],
            in_=d_ccout[q * 128:(q + 1) * 128, 4096:4099]),
          ["dccout"], ["s_spall"], tag=f"scg{q}", fam="spall")
    E("dve", lambda e: e.tensor_copy(s_sp3r.ap(), s_spall[:, 0:3]),
      ["s_spall"], ["s_sp3r"], tag="scc0")
    for q in range(1, 8):
        E("dve", lambda e, q=q: e.tensor_tensor(
            s_sp3r.ap(), s_sp3r.ap(), s_spall[:, q * 3:(q + 1) * 3], ALU.add),
          ["s_spall", "s_sp3r"], ["s_sp3r"], tag=f"scc{q}")
    E("pe", lambda e: e.matmul(pb[5][0:1, 8:11], s_ocf.ap(), s_sp3r.ap(),
                               start=True, stop=True),
      ["s_ocf", "s_sp3r"], ["pb5"], tag="sc_mm")
    E("dve", lambda e: e.tensor_copy(s_srow[0:1, 0:3], pb[5][0:1, 8:11]),
      ["pb5"], ["s_srow"], tag="sc_ev")
    E("dve", lambda e: e.tensor_scalar(
        s_srow[0:1, 0:3], s_srow[0:1, 0:3], 1.0 / (DE * DM), EPS_Q,
        ALU.mult, ALU.add),
      ["s_srow"], ["s_srow"], tag="sc_sc")
    E("dve", lambda e: e.tensor_scalar(
        s_srow[0:1, 3:6], s_srow[0:1, 0:3], 0.5, None, ALU.mult),
      ["s_srow"], ["s_srow"], tag="sc_thr")
    E("dve", lambda e: e.tensor_tensor(
        s_srow[0:1, 6:7], s_srow[0:1, 4:5], s_srow[0:1, 5:6], ALU.mult),
      ["s_srow"], ["s_srow"], tag="sc_fg")
    E("dve", lambda e: e.tensor_tensor(
        s_srow[0:1, 7:8], s_srow[0:1, 1:2], s_srow[0:1, 5:6], ALU.mult),
      ["s_srow"], ["s_srow"], tag="sc_fg2")
    tap("srow", lambda: s_srow.ap(), ["s_srow"])
    E("pe", lambda e: e.matmul(pb[5][:, 16:24], s_orf.ap(), s_srow.ap(),
                               start=True, stop=True),
      ["s_orf", "s_srow"], ["pb5"], tag="sc_bc")
    E("dve", lambda e: e.tensor_copy(s_bcols[:, 0:8], pb[5][:, 16:24]),
      ["pb5"], ["s_bcols"], tag="sc_bce")
    E("dve", lambda e: e.tensor_scalar(
        s_bcols[:, 8:11], s_bcols[:, 3:6], -1.0, None, ALU.mult),
      ["s_bcols"], ["s_bcols"], tag="sc_neg")

    if upto < PH_P2B:
        return _finish(nc, bld, tap_t, ar, arena_addr, B)

    D("sp", lambda e: e.dma_start(out=s_sel.ap(), in_=c_sel.ap()),
      [], ["s_sel"], tag="selld", fam="sel")

    def it2idx(it):
        hp, r = divmod(it, 16)
        c, hh = divmod(r, 2)
        return hp, c, hh

    def p2b_s0(it):
        hp, c, hh = it2idx(it)
        h = 2 * hp + hh
        i4 = it % 4
        cs = slice(c * 128, (c + 1) * 128)
        hcol = slice(c * 64 + h, c * 64 + h + 1)
        bcb = [2, 3, 7][it % 3]
        if c == 0 and hh == 0:
            sgb = s_sgb[hp % 2]
            D("sp", lambda e, hp=hp, sgb=sgb: e.dma_start(
                out=sgb.ap(), in_=d_sg[hp * 128:(hp + 1) * 128, :]),
              ["dsg0", "dsg1"], [f"s_sgb{hp % 2}"], tag=f"sgld{hp}",
              fam=f"s_sgb{hp % 2}")
        if c == 0:
            dii = (hp % 2) * 2 + hh
            E("dve", lambda e, dii=dii, h=h: e.tensor_scalar(
                s_di[dii].ap(), s_idb.ap(), s_dbc[:, h:h + 1], None, ALU.mult),
              ["s_idb", "s_dbc"], [f"s_di{dii}"], tag=f"dih{h}")
        E("pe", lambda e, h=h, cs=cs, bcb=bcb: e.matmul(
            pb[bcb][0:128, 0:128], s_sel[:, h * 128:(h + 1) * 128],
            s_hi[:, cs], start=True, stop=False),
          ["s_sel", "s_hi"], [f"pb{bcb}"], tag=f"bc1_{it}")
        E("pe", lambda e, h=h, cs=cs, bcb=bcb: e.matmul(
            pb[bcb][0:128, 0:128], s_sel[:, h * 128:(h + 1) * 128],
            s_lo[:, cs], start=False, stop=True),
          ["s_sel", "s_lo"], [f"pb{bcb}"], tag=f"bc2_{it}")
        E("dve", lambda e, i4=i4, c=c, h=h, hcol=hcol: e.tensor_scalar(
            s_xc[i4].ap(), s_xs[:, c * INTER + h * 64: c * INTER + h * 64 + 64],
            s_dttm[:, hcol], None, ALU.mult),
          [f"s_xs#{c}", "s_dttm"], [f"s_xc{i4}"], tag=f"xc_{it}")
        E("dve", lambda e, i4=i4, c=c, h=h, hcol=hcol: e.tensor_scalar(
            s_bd[i4][:, 0:64], s_xs[:, c * INTER + h * 64: c * INTER + h * 64 + 64],
            s_dsdt[:, hcol], None, ALU.mult),
          [f"s_xs#{c}", "s_dsdt"], [f"s_bd{i4}"], tag=f"xc2_{it}")

    def p2b_s1(it):
        hp, c, hh = it2idx(it)
        h = 2 * hp + hh
        i4 = it % 4
        cs = slice(c * 128, (c + 1) * 128)
        hcol = slice(c * 64 + h, c * 64 + h + 1)
        bcb = [2, 3, 7][it % 3]
        sb_ = it % 2
        E("act", lambda e, i4=i4, hcol=hcol, bcb=bcb: e.activation(
            s_scr[i4].ap(), pb[bcb][0:128, 0:128], AF.Relu, scale=-1.0,
            bias=s_actm[:, hcol]),
          [f"pb{bcb}", "s_actm"], [f"s_scr{i4}"], tag=f"rel_{it}")
        E("act", lambda e, i4=i4, bcb=bcb: e.activation(
            s_sdt[i4].ap(), pb[bcb][0:128, 0:128], AF.Exp),
          [f"pb{bcb}"], [f"s_sdt{i4}"], tag=f"sd_{it}")
        E("act", lambda e, i4=i4: e.activation(
            s_et[i4].ap(), s_scr[i4].ap(), AF.Exp, scale=-1.0),
          [f"s_scr{i4}"], [f"s_et{i4}"], tag=f"exp_{it}")
        E("pool", lambda e, i4=i4, cs=cs: e.tensor_tensor(
            s_mt[i4].ap(), s_et[i4].ap(), s_gmt[:, cs], ALU.mult),
          [f"s_et{i4}", "s_gmt"], [f"s_mt{i4}"], tag=f"mt_{it}")
        E("pool", lambda e, i4=i4, cs=cs: e.tensor_tensor(
            s_cd[i4].ap(), s_Cfm[:, cs], s_sdt[i4].ap(), ALU.mult),
          ["s_Cfm", f"s_sdt{i4}"], [f"s_cd{i4}"], tag=f"cd_{it}")
        E("pe", lambda e, i4=i4, sb_=sb_, cs=cs: e.matmul(
            pb[sb_][0:128, 0:64], s_btm[:, cs], s_bd[i4][:, 0:64],
            start=True, stop=True),
          ["s_btm", f"s_bd{i4}"], [f"pb{sb_}"], tag=f"smm_{it}")
        E("dve", lambda e, i4=i4, h=h: e.tensor_copy(
            s_ebf[i4].ap(), s_E[:, h * 64:(h + 1) * 64]),
          [f"s_E#{h}"], [f"s_ebf{i4}"], tag=f"ebf_{it}")
        E("dve", lambda e, h=h, hcol=hcol, sb_=sb_, i4=i4: e.scalar_tensor_tensor(
            s_E[:, h * 64:(h + 1) * 64], s_E[:, h * 64:(h + 1) * 64],
            s_lam[:, hcol], pb[sb_][0:128, 0:64], ALU.mult, ALU.add),
          [f"s_E#{h}", "s_lam", f"pb{sb_}", f"s_ebf{i4}"], [f"s_E#{h}"],
          tag=f"eu_{it}")

    def p2b_s2(it):
        hp, c, hh = it2idx(it)
        h = 2 * hp + hh
        i4 = it % 4
        dii = (hp % 2) * 2 + hh
        yb = 4 + ((hp * NCH + c) % 3)
        tp = None if hh == 0 else (0, 64)
        lo, hi2 = hh * 64, hh * 64 + 64
        E("pe", lambda e, yb=yb, lo=lo, hi2=hi2, tp=tp, i4=i4: e.matmul(
            pb[yb][lo:hi2, 0:128], s_xc[i4].ap(),
            s_mt[i4].ap(), start=True, stop=False, tile_position=tp),
          [f"s_xc{i4}", f"s_mt{i4}"], [f"pb{yb}"], tag=f"yd_{it}")
        E("pe", lambda e, h=h, c=c, yb=yb, lo=lo, hi2=hi2, tp=tp, dii=dii:
          e.matmul(
            pb[yb][lo:hi2, 0:128],
            s_xs[:, c * INTER + h * 64: c * INTER + h * 64 + 64],
            s_di[dii].ap(), start=False, stop=False, tile_position=tp),
          [f"s_xs#{c}", f"s_di{dii}"], [f"pb{yb}"], tag=f"yr_{it}")
        E("pe", lambda e, yb=yb, lo=lo, hi2=hi2, tp=tp, i4=i4: e.matmul(
            pb[yb][lo:hi2, 0:128], s_ebf[i4].ap(), s_cd[i4].ap(),
            start=False, stop=True, tile_position=tp),
          [f"s_ebf{i4}", f"s_cd{i4}"], [f"pb{yb}"], tag=f"yo_{it}")
        if hh == 1:
            sgb = s_sgb[hp % 2]
            E("dve", lambda e, hp=hp, c=c, yb=yb, sgb=sgb: e.tensor_tensor(
                s_yg[:, hp * NT + c * 128: hp * NT + (c + 1) * 128],
                pb[yb][0:128, 0:128], sgb[:, c * 128:(c + 1) * 128], ALU.mult),
              [f"pb{yb}", f"s_sgb{hp % 2}"], [f"s_yg#{hp}"], tag=f"yg_{it}")
            if c == NCH - 1:
                ys = s_ys[hp % 2]
                ysn = f"s_ys{hp % 2}"
                E("act", lambda e, hp=hp, ys=ys: e.activation(
                    ys.ap(), s_yg[:, hp * NT:(hp + 1) * NT], AF.Square),
                  [f"s_yg#{hp}"], [ysn], tag=f"sq_{hp}")
                sbk = 4 + (hp % 3)
                for half in range(2):
                    E("pe", lambda e, ys=ys, half=half, sbk=sbk: e.matmul(
                        pb[sbk][0:1, 0:512], s_ocb.ap(),
                        ys[:, half * 512:(half + 1) * 512],
                        start=True, stop=True),
                      [ysn, "s_ocb"], [f"pb{sbk}"], tag=f"st{half}_{hp}")
                    if hp == 0:
                        E("dve", lambda e, half=half, sbk=sbk: e.tensor_copy(
                            s_rsum[0:1, half * 512:(half + 1) * 512],
                            pb[sbk][0:1, 0:512]),
                          [f"pb{sbk}"], [f"s_rsum#{half}"], tag=f"sta{half}_{hp}")
                    else:
                        E("dve", lambda e, half=half, sbk=sbk: e.tensor_tensor(
                            s_rsum[0:1, half * 512:(half + 1) * 512],
                            s_rsum[0:1, half * 512:(half + 1) * 512],
                            pb[sbk][0:1, 0:512], ALU.add),
                          [f"pb{sbk}", f"s_rsum#{half}"], [f"s_rsum#{half}"],
                          tag=f"sta{half}_{hp}")

    NITB = 32 * 16
    for k in range(NITB + 2):
        if k < NITB:
            p2b_s0(k)
        if 1 <= k <= NITB:
            p2b_s1(k - 1)
        if 2 <= k <= NITB + 1:
            p2b_s2(k - 2)
    tap("yg", lambda: s_yg.ap(), [f"s_yg#{hp}" for hp in range(32)])

    if upto < PH_P3M:
        return _finish(nc, bld, tap_t, ar, arena_addr, B)

    for half in range(2):
        E("act", lambda e, half=half: e.activation(
            s_r1[0:1, half * 512:(half + 1) * 512],
            s_rsum[0:1, half * 512:(half + 1) * 512], AF.Sqrt,
            scale=1.0 / INTER, bias=s_kc[0:1, 0:1]),
          [f"s_rsum#{half}", "s_kc"], ["s_r1"], tag=f"rms_s{half}")
    E("dve", lambda e: e.reciprocal(s_r1[0:1, NT:2 * NT], s_r1[0:1, 0:NT]),
      ["s_r1"], ["s_r1"], tag="rms_r")
    for half, bk in ((0, 4), (1, 5)):
        E("pe", lambda e, half=half, bk=bk: e.matmul(
            pb[bk][:, 0:512], s_orf.ap(),
            s_r1[0:1, NT + half * 512: NT + (half + 1) * 512],
            start=True, stop=True),
          ["s_r1", "s_orf"], [f"pb{bk}"], tag=f"rb_{half}")
        E("dve", lambda e, half=half, bk=bk: e.tensor_copy(
            s_rb[:, half * 512:(half + 1) * 512], pb[bk][:, 0:512]),
          [f"pb{bk}"], ["s_rb"], tag=f"rbe_{half}")
    for j in range(32):
        E("dve", lambda e, j=j: e.tensor_tensor(
            s_yg[:, j * NT:(j + 1) * NT], s_yg[:, j * NT:(j + 1) * NT],
            s_rb.ap(), ALU.mult),
          [f"s_yg#{j}", "s_rb"], [f"s_yg#{j}"], tag=f"yn_{j}")

    def p3_load(gi):
        wg_ = s_wopg[gi % 2]
        for icg in range(4):
            ic = gi * 4 + icg
            ws = s_wops[ic % 2]
            wsn = f"s_wops{ic % 2}"
            D("sp", lambda e, ic=ic, ws=ws: e.dma_start(
                out=ws.ap(), in_=d_wop[ic * 128:(ic + 1) * 128, :]),
              [], [wsn], tag=f"wold{ic}", fam=wsn)
            E("dve", lambda e, ic=ic, icg=icg, ws=ws, wg_=wg_: e.tensor_scalar(
                wg_[:, icg * DM:(icg + 1) * DM], ws.ap(), s_nw[:, ic:ic + 1],
                None, ALU.mult),
              [wsn, "s_nw"], [f"s_wopg{gi % 2}"], tag=f"woc{ic}")

    def p3_mms(gi):
        wg_ = s_wopg[gi % 2]
        first, last = True, True
        for t in range(8):
            for eq in range(4):
                bk = (t * 4 + eq) % 4
                for icg in range(4):
                    ic = gi * 4 + icg
                    E("pe", lambda e, t=t, eq=eq, icg=icg, ic=ic, bk=bk, wg_=wg_,
                      first=first, last=last: e.matmul(
                        pb[bk][0:128, 0:512],
                        s_yg[:, ic * NT + t * 128: ic * NT + t * 128 + 128],
                        wg_[:, icg * DM + eq * 512: icg * DM + eq * 512 + 512],
                        start=(first and icg == 0), stop=(last and icg == 3)),
                      [f"s_yg#{ic}", f"s_wopg{gi % 2}"], [f"pb{bk}"],
                      tag=f"op{gi}_{t}_{eq}")
                if not last:
                    continue
                osl = slice(t * DM + eq * 512, t * DM + eq * 512 + 512)
                if gi == 0:
                    E("dve", lambda e, osl=osl, bk=bk: e.tensor_copy(
                        s_oacc[:, osl], pb[bk][0:128, 0:512]),
                      [f"pb{bk}"], [f"s_oacc#{osl.start // DM}"],
                      tag=f"oe{gi}_{t}_{eq}")
                else:
                    E("dve", lambda e, osl=osl, bk=bk: e.tensor_tensor(
                        s_oacc[:, osl], s_oacc[:, osl], pb[bk][0:128, 0:512],
                        ALU.add),
                      [f"pb{bk}", f"s_oacc#{osl.start // DM}"],
                      [f"s_oacc#{osl.start // DM}"], tag=f"oe{gi}_{t}_{eq}")

    for gi in range(9):
        if gi < 8:
            p3_load(gi)
        if gi >= 1:
            p3_mms(gi - 1)
    tap("oacc", lambda: s_oacc.ap(), [f"s_oacc#{t}" for t in range(8)])

    if upto < PH_P3T:
        return _finish(nc, bld, tap_t, ar, arena_addr, B)

    for t in range(8):
        i2 = t % 2
        oa = s_oacc[:, t * DM:(t + 1) * DM]
        oan = f"s_oacc#{t}"
        xr, t1, x1, xl2 = s_xrd[i2], s_t1[i2], s_x1b[i2], s_xl2[i2]
        xrn, x1n, xl2n = f"s_xrd{i2}", f"s_x1b{i2}", f"s_xl2{i2}"
        D("sp", lambda e, t=t, xr=xr: e.dma_start(
            out=xr.ap(), in_=d_x[128 + t * 128: 256 + t * 128, :]),
          [], [xrn], tag=f"xrd{t}", fam=xrn)
        cb = i2 * 8
        ln_stats(oa, [oan], s_sqb[i2].ap(), f"s_sqb{i2}", f"ln2_{t}", cb=cb)
        E("dve", lambda e, oa=oa, t1=t1, cb=cb: e.tensor_scalar(
            t1.ap(), oa, s_col[:, cb + 2:cb + 3], s_col[:, cb + 7:cb + 8],
            ALU.subtract, ALU.mult),
          [oan, f"s_col#{cb + 2}", f"s_col#{cb + 7}"], [f"s_t1{i2}"],
          tag=f"ln2a_{t}")
        E("dve", lambda e, t1=t1, xr=xr, x1=x1: e.tensor_tensor(
            x1.ap(), t1.ap(), xr.ap(), ALU.add),
          [f"s_t1{i2}", xrn], [x1n], tag=f"res1_{t}")
        D("sp", lambda e, t=t, x1=x1: e.dma_start(
            out=d_x1[t * 128:(t + 1) * 128, :], in_=x1.ap()),
          [x1n], [f"dx1_{t % 2}"], tag=f"x1st{t}", fam=f"dx1{t % 2}")
        ln_stats(x1.ap(), [x1n], s_sqb[i2].ap(), f"s_sqb{i2}", f"lnf_{t}", cb=cb)
        E("dve", lambda e, x1=x1, xl2=xl2, cb=cb: e.tensor_scalar(
            xl2.ap(), x1.ap(), s_col[:, cb + 2:cb + 3], s_col[:, cb + 7:cb + 8],
            ALU.subtract, ALU.mult),
          [x1n, f"s_col#{cb + 2}", f"s_col#{cb + 7}"], [xl2n], tag=f"lnfa_{t}")
        for kc in range(KD):
            bk = 6 + (kc % 2)
            E("pe", lambda e, xl2=xl2, kc=kc, bk=bk: e.transpose(
                pbb[bk][:, 0:128], xl2[:, kc * 128:(kc + 1) * 128], s_idb.ap()),
              [xl2n, "s_idb"], [f"pb{bk}"], tag="x2t")
            E("dve", lambda e, kc=kc, t=t, bk=bk: e.tensor_copy(
                s_xl2T[:, kc * NT + t * 128: kc * NT + (t + 1) * 128],
                pbb[bk][:, 0:128]),
              [f"pb{bk}"], [f"s_xl2T#{kc}"], tag="x2te")

    if upto < PH_P4A:
        return _finish(nc, bld, tap_t, ar, arena_addr, B)

    def p4a_quant(ei):
        i2 = ei % 2
        qu, qg = s_qu[i2], s_qg[i2]
        qun, qgn = f"s_qu{i2}", f"s_qg{i2}"
        for (dsrc, stg, stgn, qq, qqn, thr_c, nthr_c, on_act) in (
                (d_wu, s_ust, "s_ust", qu, qun, 3, 8, True),
                (d_wg, s_gst, "s_gst", qg, qgn, 4, 9, False)):
            src = dsrc.ap()[:, ei * 128:(ei + 1) * 128].rearrange(
                "(kc kp) f -> kp kc f", kp=128)
            for halfk in range(2):
                st = stg[halfk]
                stn = f"{stgn}{halfk}"
                hs_ = slice(halfk * 1024, (halfk + 1) * 1024)
                D("sp", lambda e, src=src, st=st, halfk=halfk: e.dma_start(
                    out=st.ap().rearrange("p (kc f) -> p kc f", f=128),
                    in_=src[:, halfk * 8:(halfk + 1) * 8, :]),
                  [], [stn], tag=f"uld{ei}", fam=stn)
                if on_act:
                    E("act", lambda e, st=st, qq=qq, hs_=hs_, nthr_c=nthr_c:
                      e.activation(
                        qq[:, hs_], st.ap(), AF.Sign,
                        bias=s_bcols[:, nthr_c:nthr_c + 1]),
                      [stn, "s_bcols"], [qqn], tag=f"sg1{ei}")
                    E("act", lambda e, st=st, thr_c=thr_c: e.activation(
                        s_q2[:, 0:1024], st.ap(), AF.Sign,
                        bias=s_bcols[:, thr_c:thr_c + 1]),
                      [stn, "s_bcols"], ["s_q2"], tag=f"sg2{ei}")
                    E("dve", lambda e, qq=qq, hs_=hs_: e.tensor_tensor(
                        qq[:, hs_], qq[:, hs_], s_q2[:, 0:1024], ALU.add),
                      [qqn, "s_q2"], [qqn], tag=f"sga{ei}")
                else:
                    E("dve", lambda e, st=st, qq=qq, hs_=hs_, thr_c=thr_c:
                      e.tensor_scalar(
                        qq[:, hs_], st.ap(), s_bcols[:, thr_c:thr_c + 1], None,
                        ALU.is_gt),
                      [stn, "s_bcols"], [qqn], tag=f"gg1{ei}")
                    E("dve", lambda e, st=st, nthr_c=nthr_c: e.tensor_scalar(
                        s_q2[:, 0:1024], st.ap(), s_bcols[:, nthr_c:nthr_c + 1],
                        None, ALU.is_lt),
                      [stn, "s_bcols"], ["s_q2"], tag=f"gg2{ei}")
                    E("dve", lambda e, qq=qq, hs_=hs_: e.tensor_tensor(
                        qq[:, hs_], qq[:, hs_], s_q2[:, 0:1024], ALU.subtract),
                      [qqn, "s_q2"], [qqn], tag=f"gga{ei}")

    def p4a_mm(ei):
        i2 = ei % 2
        qu, qg = s_qu[i2], s_qg[i2]
        qun, qgn = f"s_qu{i2}", f"s_qg{i2}"
        b0 = i2 * 4
        for kc in range(KD):
            st_, sp_ = (kc == 0), (kc == KD - 1)
            for tt in range(2):
                tsl = slice(kc * NT + tt * 512, kc * NT + tt * 512 + 512)
                E("pe", lambda e, qu=qu, kc=kc, tsl=tsl, b=b0 + tt, st_=st_,
                  sp_=sp_: e.matmul(
                    pb[b][0:128, 0:512], qu[:, kc * 128:(kc + 1) * 128],
                    s_xl2T[:, tsl], start=st_, stop=sp_),
                  [qun, f"s_xl2T#{kc}"], [f"pb{b0 + tt}"], tag=f"ump{ei}")
                E("pe", lambda e, qg=qg, kc=kc, tsl=tsl, b=b0 + 2 + tt, st_=st_,
                  sp_=sp_: e.matmul(
                    pb[b][0:128, 0:512], qg[:, kc * 128:(kc + 1) * 128],
                    s_xl2T[:, tsl], start=st_, stop=sp_),
                  [qgn, f"s_xl2T#{kc}"], [f"pb{b0 + 2 + tt}"], tag=f"gmp{ei}")
        for tt in range(2):
            hsb = s_hs2[tt]
            E("act", lambda e, hsb=hsb, b=b0 + tt: e.activation(
                hsb.ap(), pb[b][0:128, 0:512], AF.Silu, scale=s_bcols[:, 3:4]),
              [f"pb{b0 + tt}", "s_bcols"], [f"s_hs2{tt}"], tag=f"usi{ei}")
            E("dve", lambda e, hsb=hsb, ei=ei, tt=tt, b=b0 + 2 + tt:
              e.tensor_tensor(
                s_hid[:, ei * NT + tt * 512: ei * NT + tt * 512 + 512],
                hsb.ap(), pb[b][0:128, 0:512], ALU.mult),
              [f"s_hs2{tt}", f"pb{b0 + 2 + tt}"], [f"s_hid#{ei}"], tag=f"uhm{ei}")

    for ei in range(65):
        if ei < 64:
            p4a_quant(ei)
        if ei >= 1:
            p4a_mm(ei - 1)
    tap("hid", lambda: s_hid.ap(), [f"s_hid#{ei}" for ei in range(64)])

    if upto < PH_P4B:
        return _finish(nc, bld, tap_t, ar, arena_addr, B)

    def p4b_quant(dq, i):
        i3 = i % 3
        i2 = i % 2
        stg, qd, q2 = s_dst[i3], s_qd[i3], s_q2d[i2]
        stn, qdn, q2n = f"s_dst{i3}", f"s_qd{i3}", f"s_q2d{i2}"
        D("sp", lambda e, i=i, dq=dq, stg=stg: e.dma_start(
            out=stg.ap(),
            in_=d_wd[i * 128:(i + 1) * 128, dq * 512:(dq + 1) * 512]),
          [], [stn], tag=f"dld{dq}_{i}", fam=stn)
        E("act", lambda e, stg=stg, qd=qd: e.activation(
            qd.ap(), stg.ap(), AF.Sign, bias=s_bcols[:, 10:11]),
          [stn, "s_bcols"], [qdn], tag=f"dsg1{dq}_{i}")
        E("act", lambda e, stg=stg, q2=q2: e.activation(
            q2.ap(), stg.ap(), AF.Sign, bias=s_bcols[:, 5:6]),
          [stn, "s_bcols"], [q2n], tag=f"dsg2{dq}_{i}")
        E("dve", lambda e, qd=qd, q2=q2: e.tensor_tensor(
            qd.ap(), qd.ap(), q2.ap(), ALU.add),
          [qdn, q2n], [qdn], tag=f"dsga{dq}_{i}")

    def p4b_mm(dq, i):
        i3 = i % 3
        qd = s_qd[i3]
        qdn = f"s_qd{i3}"
        for t in range(8):
            E("pe", lambda e, i=i, t=t, qd=qd: e.matmul(
                pb[t][0:128, 0:512],
                s_hid[:, i * NT + t * 128: i * NT + t * 128 + 128],
                qd.ap(), start=(i == 0), stop=(i == 63)),
              [qdn, f"s_hid#{i}"], [f"pb{t}"], tag=f"dmm{dq}_{i}_{t}")

    for dq in range(4):
        for i in range(66):
            if i < 64:
                p4b_quant(dq, i)
            if i >= 2:
                p4b_mm(dq, i - 2)
        for t in range(8):
            i2 = t % 2
            dev = s_dev[i2]
            E("dve", lambda e, t=t, dev=dev: e.tensor_scalar(
                dev.ap(), pb[t][0:128, 0:512], s_bcols[:, 7:8], None, ALU.mult),
              [f"pb{t}", "s_bcols"], [f"s_dev{i2}"], tag=f"dev{dq}_{t}")
            D("sp", lambda e, t=t, dq=dq, dev=dev: e.dma_start(
                out=d_da[t * 128:(t + 1) * 128, dq * 512:(dq + 1) * 512],
                in_=dev.ap()),
              [f"s_dev{i2}"], [f"dda{t % 2}"], tag=f"dst{dq}_{t}",
              fam=f"dda{t % 2}")
            if dq == 0:
                E("dve", lambda e, t=t, dev=dev: e.tensor_reduce(
                    s_l3s[:, t:t + 1], dev.ap(), AX.X, ALU.add),
                  [f"s_dev{i2}"], [f"s_l3s#{t}"], tag=f"l3s{dq}_{t}")
                E("act", lambda e, t=t, dev=dev, i2=i2: e.activation(
                    s_dsq[i2].ap(), dev.ap(), AF.Square,
                    accum_out=s_l3q[:, t:t + 1]),
                  [f"s_dev{i2}"], [f"s_dsq{i2}", f"s_l3q#{t}"], tag=f"l3q{dq}_{t}")
            else:
                E("dve", lambda e, t=t, dev=dev: e.tensor_reduce(
                    s_l3s[:, 8 + t:9 + t], dev.ap(), AX.X, ALU.add),
                  [f"s_dev{i2}"], ["l3tmp"], tag=f"l3s{dq}_{t}")
                E("dve", lambda e, t=t: e.tensor_tensor(
                    s_l3s[:, t:t + 1], s_l3s[:, t:t + 1], s_l3s[:, 8 + t:9 + t],
                    ALU.add),
                  ["l3tmp", f"s_l3s#{t}"], [f"s_l3s#{t}"], tag=f"l3sa{dq}_{t}")
                E("act", lambda e, t=t, dev=dev, i2=i2: e.activation(
                    s_dsq[i2].ap(), dev.ap(), AF.Square,
                    accum_out=s_l3q[:, 8 + t:9 + t]),
                  [f"s_dev{i2}"], [f"s_dsq{i2}", "l3qtmp"], tag=f"l3q{dq}_{t}")
                E("dve", lambda e, t=t: e.tensor_tensor(
                    s_l3q[:, t:t + 1], s_l3q[:, t:t + 1], s_l3q[:, 8 + t:9 + t],
                    ALU.add),
                  ["l3qtmp", f"s_l3q#{t}"], [f"s_l3q#{t}"], tag=f"l3qa{dq}_{t}")

    tap("dacc", lambda: d_da.ap(), ["dda0", "dda1"])
    tap("l3s", lambda: s_l3s.ap(), [f"s_l3s#{t}" for t in range(8)])
    tap("l3q", lambda: s_l3q.ap(), [f"s_l3q#{t}" for t in range(8)])
    out_evs = []
    for t in range(8):
        i2 = t % 2
        dar, x1r, fin = s_dar[i2], s_x1r[i2], s_fin[i2]
        darn, x1n, finn = f"s_dar{i2}", f"s_x1r{i2}", f"s_fin{i2}"
        D("sp", lambda e, t=t, dar=dar: e.dma_start(
            out=dar.ap(), in_=d_da[t * 128:(t + 1) * 128, :]),
          ["dda0", "dda1"], [darn], tag=f"dald{t}", fam=darn)
        D("sp", lambda e, t=t, x1r=x1r: e.dma_start(
            out=x1r.ap(), in_=d_x1[t * 128:(t + 1) * 128, :]),
          ["dx1_0", "dx1_1"], [x1n], tag=f"x1ld{t}", fam=x1n)
        cb = i2 * 8
        E("dve", lambda e, t=t, cb=cb: e.tensor_scalar(
            s_col[:, cb + 2:cb + 3], s_l3s[:, t:t + 1], 1.0 / DM, None, ALU.mult),
          [f"s_l3s#{t}"], [f"s_col#{cb + 2}"], tag=f"l3m_{t}")
        E("dve", lambda e, cb=cb: e.tensor_tensor(
            s_col[:, cb + 4:cb + 5], s_col[:, cb + 2:cb + 3],
            s_col[:, cb + 2:cb + 3], ALU.mult),
          [f"s_col#{cb + 2}"], [f"s_col#{cb + 4}"], tag=f"l3m2_{t}")
        E("dve", lambda e, t=t, cb=cb: e.tensor_scalar(
            s_col[:, cb + 5:cb + 6], s_l3q[:, t:t + 1], 1.0 / DM, None, ALU.mult),
          [f"s_l3q#{t}"], [f"s_col#{cb + 5}"], tag=f"l3v1_{t}")
        E("dve", lambda e, cb=cb: e.tensor_tensor(
            s_col[:, cb + 5:cb + 6], s_col[:, cb + 5:cb + 6],
            s_col[:, cb + 4:cb + 5], ALU.subtract),
          [f"s_col#{cb + 4}", f"s_col#{cb + 5}"], [f"s_col#{cb + 5}"],
          tag=f"l3v2_{t}")
        E("act", lambda e, cb=cb: e.activation(
            s_col[:, cb + 6:cb + 7], s_col[:, cb + 5:cb + 6], AF.Sqrt,
            bias=s_kc[:, 0:1]),
          [f"s_col#{cb + 5}", "s_kc"], [f"s_col#{cb + 6}"], tag=f"l3sd_{t}")
        E("dve", lambda e, cb=cb: e.reciprocal(
            s_col[:, cb + 7:cb + 8], s_col[:, cb + 6:cb + 7]),
          [f"s_col#{cb + 6}"], [f"s_col#{cb + 7}"], tag=f"l3r_{t}")
        E("dve", lambda e, dar=dar, fin=fin, cb=cb: e.tensor_scalar(
            fin.ap(), dar.ap(), s_col[:, cb + 2:cb + 3], s_col[:, cb + 7:cb + 8],
            ALU.subtract, ALU.mult),
          [darn, f"s_col#{cb + 2}", f"s_col#{cb + 7}"], [finn], tag=f"ln3a_{t}")
        E("dve", lambda e, fin=fin, x1r=x1r: e.tensor_tensor(
            fin.ap(), fin.ap(), x1r.ap(), ALU.add),
          [finn, x1n], [finn], tag=f"res2_{t}")
        ev = D("sp", lambda e, t=t, fin=fin: e.dma_start(
            out=d_out[t * 128:(t + 1) * 128, :], in_=fin.ap()),
          [finn], [f"dout{t}"], tag=f"outst{t}", fam=f"dout{t % 2}")
        out_evs.append(ev)
    bld.op("sp", lambda e: e.nop(), [], [], set(out_evs), tag="final_wait")

    return _finish(nc, bld, tap_t, ar, arena_addr, B)


def _finish(nc, bld, tap_t, ar, arena_addr, B):
    import contextlib
    items = ar.commit(arena_addr)
    by_name = {}
    for (off, size, ph0, ph1, name) in items:
        by_name[name] = (off, size, ph0, ph1)
    owners = {}
    for bn, buf in B.items():
        owner = bn.split("#")[0]
        if owner in by_name:
            owners.setdefault(owner, []).append(buf)
    names = list(by_name)
    for i in range(len(names)):
        for j in range(len(names)):
            if i == j:
                continue
            a, b = names[i], names[j]
            (ao, asz, ap0, ap1) = by_name[a]
            (bo, bsz, bp0, bp1) = by_name[b]
            if ao + asz <= bo or bo + bsz <= ao:
                continue
            if not (ap1 < bp0):
                continue
            last_evs = set()
            for buf in owners.get(a, ()):
                if buf.last_w is not None:
                    last_evs.add(buf.last_w)
                last_evs.update(buf.readers)
            for buf in owners.get(b, ()):
                if buf.first_w_op is not None:
                    eng, idx = buf.first_w_op
                    bld.ops[eng][idx].deps.update(last_evs)

    ctx = contextlib.ExitStack()
    sems = {e: ctx.enter_context(nc.semaphore(f"sem_{e}")) for e in ENGINES}
    dma_sems = {f: ctx.enter_context(nc.semaphore(f"dsem_{f}"))
                for f in bld.nfam}
    cc_sem = ctx.enter_context(nc.semaphore("ccsem"))
    with ctx, nc.Block() as block:
        bld.finalize(block, sems, dma_sems, cc_sem)
    return nc, tap_t


def prep_inputs(inputs):
    _ = inputs.get("L0_attn_mask")
    x = np.asarray(inputs["L0_attn_input_states"], dtype=np.float32)
    wip_t = np.ascontiguousarray(np.asarray(inputs["L0_attn_in_proj_w"], np.float32).T)
    wop_t = np.ascontiguousarray(np.asarray(inputs["L0_attn_out_proj_w"], np.float32).T)
    wu_t = np.ascontiguousarray(np.asarray(inputs["L0_ffn_W_up"], np.float32).T)
    wg_t = np.ascontiguousarray(np.asarray(inputs["L0_ffn_W_gate"], np.float32).T)
    wd_t = np.ascontiguousarray(np.asarray(inputs["L0_ffn_W_down"], np.float32).T)
    cw = np.asarray(inputs["L0_attn_conv_w"], np.float32).reshape(4352, 4)
    cw_pp = np.zeros((128, 136), np.float32)
    for j in range(34):
        cw_pp[:, j * 4:(j + 1) * 4] = cw[j * 128:(j + 1) * 128, :]
    cb = np.asarray(inputs["L0_attn_conv_b"], np.float32)
    cb_pp = np.ascontiguousarray(cb.reshape(34, 128).T)
    alog = np.asarray(inputs["L0_attn_A_log"], np.float32).reshape(64, 1)
    dtb = np.asarray(inputs["L0_attn_dt_bias"], np.float32).reshape(64, 1)
    drow = np.asarray(inputs["L0_attn_D"], np.float32).reshape(1, 64)
    nw = np.ascontiguousarray(
        np.asarray(inputs["L0_attn_norm_w"], np.float32).reshape(32, 128).T)

    in_maps = []
    for r in range(8):
        seq, half = r // 2, r % 2
        xs = np.zeros((NROWS, DM), np.float32)
        tok0 = half * NT
        xs[128:, :] = x[seq, tok0:tok0 + NT, :]
        if half == 1:
            xs[125:128, :] = x[seq, NT - 3:NT, :]
        e0m = np.zeros((1, 8), np.float32)
        if half == 1:
            e0m[0, r - 1] = 1.0
        in_maps.append({
            "x": np.ascontiguousarray(xs),
            "wip_t": wip_t, "wop_t": wop_t,
            "wu_t": wu_t, "wg_t": wg_t, "wd_t": wd_t,
            "wu_s": np.ascontiguousarray(wu_t[r * 256:(r + 1) * 256, :]),
            "wg_s": np.ascontiguousarray(wg_t[r * 256:(r + 1) * 256, :]),
            "wd_s": np.ascontiguousarray(wd_t[r * 1024:(r + 1) * 1024, :]),
            "conv_w": cw_pp, "conv_b": cb_pp,
            "alog": alog, "dtb": dtb, "d_row": drow, "normw": nw,
            "e0m": e0m,
        })
    return in_maps


def kernel(**inputs):
    in_maps = prep_inputs(inputs)
    nc, _ = build_program()
    res = run_bass_kernel_spmd(nc, in_maps, core_ids=list(range(8)))
    out = np.zeros((4, 2048, 2048), np.float32)
    for r in range(8):
        seq, half = r // 2, r % 2
        out[seq, half * NT:(half + 1) * NT, :] = res.results[r]["out"]
    return out


# version 19
# speedup vs baseline: 1.0762x; 1.0235x over previous
import numpy as np

import concourse.bass as bass
import concourse.mybir as mybir
from concourse.bass_utils import run_bass_kernel_spmd

F32 = mybir.dt.float32
BF16 = mybir.dt.bfloat16
AF = mybir.ActivationFunctionType
ALU = mybir.AluOpType
AX = mybir.AxisListType

NT = 1024
NROWS = 1152
DM = 2048
PROJ = 8512
INTER = 4096
H = 64
CH = 128
NCH = 8
DE = 8192
KD = 16
EPS_LN = 1e-5
EPS_RMS = 1e-5
EPS_Q = 1e-8

(PH_P0, PH_P1, PH_P1B, PH_P2A, PH_CC, PH_PG, PH_E0, PH_P2B, PH_P3M, PH_P3T,
 PH_P4A, PH_P4B, PH_END) = range(13)

SBUF_BUDGET = 212600
ENGINES = ("pe", "act", "dve", "pool", "sp")


def _bf():
    import ml_dtypes
    return ml_dtypes.bfloat16


class Buf:
    __slots__ = ("name", "last_w", "readers", "first_w_op")

    def __init__(self, name):
        self.name = name
        self.last_w = None
        self.readers = []
        self.first_w_op = None


class Op:
    __slots__ = ("eng", "fn", "deps", "inc", "inc_idx", "kind", "tag", "fam")

    def __init__(self, eng, fn, deps, kind, tag):
        self.eng = eng
        self.fn = fn
        self.deps = deps
        self.inc = False
        self.inc_idx = 0
        self.kind = kind
        self.tag = tag


class Builder:
    CHAINED = {"ldc0", "ldc1", "spall", "ccin", "tap"}

    def __init__(self):
        self.ops = {e: [] for e in ENGINES}
        self.nfam = {}
        self.fam_last = {}
        self.wait_all_fams = set()
        self.ncc = 0

    def _deps(self, reads, writes, extra):
        deps = set(extra)
        for b in reads:
            if b.last_w is not None:
                deps.add(b.last_w)
        for b in writes:
            if b.last_w is not None:
                deps.add(b.last_w)
            deps.update(b.readers)
        return deps

    def _rec(self, eng, fn, reads, writes, extra, kind, tag, fam=None):
        deps = self._deps(reads, writes, extra)
        idx = len(self.ops[eng])
        o = Op(eng, fn, deps, kind, tag)
        if kind == "dma":
            o.inc_idx = self.nfam.setdefault(fam, 0)
            self.nfam[fam] += 1
            o.fam = fam
            ev = ("d", fam, o.inc_idx)
        elif kind == "cc":
            o.inc_idx = self.ncc
            self.ncc += 1
            ev = ("c", 0, o.inc_idx)
        else:
            ev = ("e", eng, idx)
        self.ops[eng].append(o)
        for b in reads:
            b.readers.append(ev)
        for b in writes:
            if b.first_w_op is None:
                b.first_w_op = (eng, idx)
            b.last_w = ev
            b.readers = []
        return ev

    def op(self, eng, fn, reads=(), writes=(), extra=(), tag=""):
        return self._rec(eng, fn, reads, writes, extra, "op", tag)

    def dma(self, q, fn, reads=(), writes=(), extra=(), tag="", fam=None):
        extra = set(extra)
        if fam in self.CHAINED and fam in self.fam_last:
            extra.add(self.fam_last[fam])
        ev = self._rec(q, fn, reads, writes, extra, "dma", tag, fam=fam)
        self.fam_last[fam] = ev
        return ev

    def cc(self, fn, reads=(), writes=(), extra=(), tag="cc"):
        return self._rec("pool", fn, reads, writes, extra, "cc", tag)

    def finalize(self, block, sems, dma_sems, cc_sem):
        for eng in ENGINES:
            for o in self.ops[eng]:
                for (k, src, idx) in o.deps:
                    if k == "e":
                        if src == "pe" and eng == "pe":
                            continue
                        self.ops[src][idx].inc = True
        for eng in ENGINES:
            n = 0
            for o in self.ops[eng]:
                if o.kind == "op" and o.inc:
                    n += 1
                    o.inc_idx = n

        ops = self.ops

        nfam = self.nfam
        wa = self.wait_all_fams

        def wait_value(ev):
            k, src, idx = ev
            if k == "e":
                o = ops[src][idx]
                assert o.inc, f"dep on non-inc op {src}:{idx}:{o.tag}"
                return ("e", src), o.inc_idx
            if k == "d":
                n = nfam[src] if src in wa else (idx + 1)
                return ("d", src), 16 * n
            return ("c", 0), idx + 1

        def make_body(eng):
            def body(engobj):
                hw = {}
                for o in ops[eng]:
                    for ev in sorted(o.deps, key=lambda t: (t[0], str(t[1]), t[2])):
                        if ev[0] == "e" and ev[1] == eng == "pe":
                            continue
                        if (o.kind == "dma" and ev[0] == "d"
                                and ev[1] == o.fam
                                and o.fam not in self.CHAINED):
                            continue
                        key, val = wait_value(ev)
                        if hw.get(key, 0) >= val:
                            continue
                        if key[0] == "e":
                            engobj.wait_ge(sems[key[1]], val)
                        elif key[0] == "d":
                            engobj.wait_ge(dma_sems[key[1]], val)
                        else:
                            engobj.wait_ge(cc_sem, val)
                        hw[key] = val
                    inst = o.fn(engobj)
                    if o.kind == "dma":
                        inst.then_inc(dma_sems[o.fam], 16)
                    elif o.kind == "cc":
                        inst.then_inc(cc_sem)
                    elif o.inc:
                        inst.then_inc(sems[eng], 1)
            return body

        block.tensor(make_body("pe"))
        block.scalar(make_body("act"))
        block.vector(make_body("dve"))
        block.gpsimd(make_body("pool"))
        block.sync(make_body("sp"))


class Arena:

    def __init__(self, budget):
        self.budget = budget
        self.reqs = []

    def place(self, size, ph0, ph1, name, mloc):
        size = (size + 31) // 32 * 32
        self.reqs.append((size, ph0, ph1, name, mloc))

    def commit(self, base_addr):
        items = []
        for (size, ph0, ph1, name, mloc) in sorted(
                self.reqs, key=lambda r: (-r[0], r[3])):
            taken = sorted((off, off + sz) for (off, sz, a, b, _n) in items
                           if not (b < ph0 or a > ph1))
            off = 0
            for s, e in taken:
                if off + size <= s:
                    break
                off = max(off, e)
            assert off + size <= self.budget, \
                f"SBUF overflow: {name} ({size}B, ph {ph0}-{ph1}) at {off}"
            items.append((off, size, ph0, ph1, name))
            mloc.addr = base_addr + off
        return items


def build_program(taps=(), upto=PH_END):
    bf = _bf()
    nc = bass.Bass("TRN2", target_bir_lowering=False)
    bld = Builder()
    bld.wait_all_fams = set()

    d_x = nc.dram_tensor("x", [NROWS, DM], F32, kind="ExternalInput")
    d_wip = nc.dram_tensor("wip_t", [DM, PROJ], F32, kind="ExternalInput")
    d_wop = nc.dram_tensor("wop_t", [INTER, DM], F32, kind="ExternalInput")
    d_wu = nc.dram_tensor("wu_t", [DM, DE], F32, kind="ExternalInput")
    d_wg = nc.dram_tensor("wg_t", [DM, DE], F32, kind="ExternalInput")
    d_wd = nc.dram_tensor("wd_t", [DE, DM], F32, kind="ExternalInput")
    d_wus = nc.dram_tensor("wu_s", [256, DE], F32, kind="ExternalInput")
    d_wgs = nc.dram_tensor("wg_s", [256, DE], F32, kind="ExternalInput")
    d_wds = nc.dram_tensor("wd_s", [1024, DM], F32, kind="ExternalInput")
    d_cw = nc.dram_tensor("conv_w", [128, 136], F32, kind="ExternalInput")
    d_cb = nc.dram_tensor("conv_b", [128, 34], F32, kind="ExternalInput")
    d_alog = nc.dram_tensor("alog", [64, 1], F32, kind="ExternalInput")
    d_dtb = nc.dram_tensor("dtb", [64, 1], F32, kind="ExternalInput")
    d_drow = nc.dram_tensor("d_row", [1, 64], F32, kind="ExternalInput")
    d_nw = nc.dram_tensor("normw", [128, 32], F32, kind="ExternalInput")
    d_e0m = nc.dram_tensor("e0m", [1, 8], F32, kind="ExternalInput")
    d_out = nc.dram_tensor("out", [NT, DM], F32, kind="ExternalOutput")

    d_sg = nc.dram_tensor("sg_scr", [INTER, NT], BF16)
    d_x1 = nc.dram_tensor("x1_scr", [NT, DM], F32)
    d_da = nc.dram_tensor("dacc_scr", [NT, DM], F32)
    d_ccin = nc.dram_tensor("cc_in", [128, 4100], F32)
    d_ccout = nc.dram_tensor("cc_out", [1024, 4100], F32, addr_space="Shared")

    c_idf = nc.inline_tensor(np.eye(128, dtype=np.float32), "identf")
    c_idb = nc.inline_tensor(np.eye(128, dtype=bf), "identb")
    c_tri = nc.inline_tensor(np.triu(np.ones((128, 128), dtype=bf)), "tri")
    sel_np = np.zeros((64, 64 * 128), dtype=bf)
    for h in range(64):
        sel_np[h, h * 128:(h + 1) * 128] = 1
    c_sel = nc.inline_tensor(sel_np, "selc")
    c_ocb = nc.inline_tensor(np.ones((128, 1), dtype=bf), "onescb")
    c_orf = nc.inline_tensor(np.ones((1, 128), dtype=np.float32), "onesrf")
    c_ocf = nc.inline_tensor(np.ones((128, 1), dtype=np.float32), "onescf")
    c_kc = nc.inline_tensor(np.tile(np.array([[EPS_LN, 1.0]], np.float32), (128, 1)), "kconst")

    ar = Arena(SBUF_BUDGET)
    arena = nc.alloc_sbuf_tensor("arena", [128, SBUF_BUDGET], mybir.dt.uint8)
    arena_addr = nc.lookup_mloc(arena).addr

    S = {}

    def place(name, shape, dtype, ph0, ph1):
        per_part = int(np.prod(shape[1:])) * mybir.dt.size(dtype)
        h = nc.virtual_sbuf_tensor(name, list(shape), dtype)
        nc._tensor(h.name, list(h.shape), h.dtype, type="SB")
        mloc = nc.lookup_mloc(h)
        mloc.allocated = True
        ar.place(per_part, ph0, ph1, name, mloc)
        S[name] = h
        return h

    s_idf = place("s_idf", [128, 128], F32, PH_P0, PH_END)
    s_idb = place("s_idb", [128, 128], BF16, PH_P0, PH_END)
    s_tri = place("s_tri", [128, 128], BF16, PH_P0, PH_P2B)
    s_ocb = place("s_ocb", [128, 1], BF16, PH_P0, PH_END)
    s_orf = place("s_orf", [1, 128], F32, PH_P0, PH_END)
    s_ocf = place("s_ocf", [128, 1], F32, PH_P0, PH_END)
    s_cw = place("s_cw", [128, 136], F32, PH_P0, PH_P1B)
    s_cb = place("s_cb", [128, 34], F32, PH_P0, PH_P1B)
    s_alog = place("s_alog", [64, 1], F32, PH_P0, PH_P1)
    s_dtb = place("s_dtb", [64, 1], F32, PH_P0, PH_P1)
    s_drow = place("s_drow", [1, 64], F32, PH_P0, PH_P2B)
    s_dbc = place("s_dbc", [128, 64], F32, PH_P0, PH_P2B)
    s_nw = place("s_nw", [128, 32], F32, PH_P0, PH_P3M)
    s_e0m = place("s_e0m", [1, 8], F32, PH_P0, PH_E0)
    s_mcols = place("s_mcols", [128, 8], F32, PH_E0, PH_E0)
    s_col = place("s_col", [128, 16], F32, PH_P0, PH_END)
    s_kc = place("s_kc", [128, 2], F32, PH_P0, PH_END)

    s_xlnT = place("s_xlnT", [128, KD * NROWS], BF16, PH_P0, PH_PG)
    s_xs = place("s_xs", [128, NCH * INTER], BF16, PH_P1B, PH_P2B)
    s_yg = place("s_yg", [128, 32 * NT], BF16, PH_P2B, PH_P3M)
    s_E = place("s_E", [128, H * 64], F32, PH_P2A, PH_P2B)
    s_sel = place("s_sel", [64, 64 * 128], BF16, PH_P2B, PH_P2B)

    s_xtm = [place(f"s_xtm{i}", [128, DM], F32, PH_P0, PH_P0) for i in range(2)]
    s_xln = [place(f"s_xln{i}", [128, DM], BF16, PH_P0, PH_P0) for i in range(2)]
    s_sq0 = [place(f"s_sq0{i}", [128, DM], BF16, PH_P0, PH_P0) for i in range(2)]
    s_wbf = [place(f"s_wbf{i}", [128, 2 * DM], BF16, PH_P1, PH_PG) for i in range(2)]
    s_raw = [place(f"s_raw{i}", [128, NROWS], BF16, PH_P1, PH_P1B) for i in range(3)]
    s_cacc = [place(f"s_cacc{i}", [128, NT], F32, PH_P1, PH_P1B) for i in range(2)]
    s_ca4 = [place(f"s_ca4{i}", [128, NT], F32, PH_P1, PH_P1B) for i in range(2)]
    s_hso = [place(f"s_hso{i}", [128, NT], BF16, PH_P1B, PH_P1B) for i in range(3)]
    s_Bfm = place("s_Bfm", [128, NT], BF16, PH_P1, PH_P2B)
    s_Cfm = place("s_Cfm", [128, NT], BF16, PH_P1, PH_P2B)
    s_sgb = [place(f"s_sgb{i}", [128, NT], BF16, PH_PG, PH_P2B) for i in range(2)]

    s_dtfm = place("s_dtfm", [64, NT], F32, PH_P1, PH_P1)
    s_sp1 = place("s_sp1", [64, 512], F32, PH_P1, PH_P1)
    s_sp2 = place("s_sp2", [64, 512], F32, PH_P1, PH_P1)
    s_sp3 = place("s_sp3", [64, 512], F32, PH_P1, PH_P1)
    s_negA = place("s_negA", [64, 1], F32, PH_P1, PH_P1)
    s_adt = place("s_adt", [64, NT], F32, PH_P1, PH_P1)
    s_acum = place("s_acum", [64, NT], F32, PH_P1, PH_P1)
    s_z64 = place("s_z64", [64, 128], F32, PH_P0, PH_P1)
    s_dsf = place("s_dsf", [64, 128], F32, PH_P1, PH_P1)
    s_dsf2 = place("s_dsf2", [64, 128], F32, PH_P1, PH_P1)
    s_lrow = place("s_lrow", [1, 64], F32, PH_P1, PH_P1)
    s_actm = place("s_actm", [128, NCH * 64], F32, PH_P1, PH_P2B)
    s_dttm = place("s_dttm", [128, NCH * 64], F32, PH_P1, PH_P2B)
    s_dsdt = place("s_dsdt", [128, NCH * 64], F32, PH_P1, PH_P2B)
    s_lam = place("s_lam", [128, NCH * 64], F32, PH_P1, PH_P2B)
    s_btm = place("s_btm", [128, NCH * 128], BF16, PH_P1, PH_P2B)
    s_gmt = place("s_gmt", [128, NCH * 128], BF16, PH_P1, PH_P2B)
    s_hi = place("s_hi", [64, NCH * 128], BF16, PH_P1, PH_P2B)
    s_lo = place("s_lo", [64, NCH * 128], BF16, PH_P1, PH_P2B)

    s_bd = [place(f"s_bd{i}", [128, 128], BF16, PH_P2A, PH_P2B) for i in range(4)]
    s_xc = [place(f"s_xc{i}", [128, 64], BF16, PH_P2A, PH_P2B) for i in range(4)]
    s_scr = [place(f"s_scr{i}", [128, 128], F32, PH_P2B, PH_P2B) for i in range(4)]
    s_et = [place(f"s_et{i}", [128, 128], BF16, PH_P2B, PH_P2B) for i in range(4)]
    s_sdt = [place(f"s_sdt{i}", [128, 128], BF16, PH_P2B, PH_P2B) for i in range(4)]
    s_mt = [place(f"s_mt{i}", [128, 128], BF16, PH_P2B, PH_P2B) for i in range(4)]
    s_cd = [place(f"s_cd{i}", [128, 128], BF16, PH_P2B, PH_P2B) for i in range(4)]
    s_ebf = [place(f"s_ebf{i}", [128, 64], BF16, PH_P2B, PH_P2B) for i in range(4)]
    s_di = [place(f"s_di{i}", [128, 128], BF16, PH_P2B, PH_P2B) for i in range(4)]
    s_slot = [place(f"s_slot{i}", [128, 4096], F32, PH_E0, PH_E0) for i in range(2)]

    s_spc = [place(f"s_spc{i}", [128, 2048], F32, PH_P1, PH_P2A) for i in range(2)]
    s_part = place("s_part", [128, 4], F32, PH_P0, PH_CC)
    s_spall = place("s_spall", [128, 24], F32, PH_E0, PH_E0)
    s_sp3r = place("s_sp3r", [128, 3], F32, PH_E0, PH_E0)
    s_srow = place("s_srow", [1, 8], F32, PH_E0, PH_P4B)
    s_bcols = place("s_bcols", [128, 12], F32, PH_E0, PH_P4B)

    s_wopg = [place(f"s_wopg{i}", [128, 4 * DM], BF16, PH_P3M, PH_P3M)
              for i in range(2)]
    s_oacc = place("s_oacc", [128, 8 * DM], F32, PH_P3M, PH_P3T)
    s_rb = place("s_rb", [128, NT], F32, PH_P3M, PH_P3M)
    s_r1 = place("s_r1", [1, 2 * NT], F32, PH_P3M, PH_P3M)
    s_rsum = place("s_rsum", [1, NT], F32, PH_P2B, PH_P3M)
    s_ys = [place(f"s_ys{i}", [128, NT], BF16, PH_P2B, PH_P3M) for i in range(2)]
    s_wops = [place(f"s_wops{i}", [128, DM], F32, PH_P3M, PH_P3M) for i in range(2)]
    s_xrd = [place(f"s_xrd{i}", [128, DM], F32, PH_P3T, PH_P3T) for i in range(2)]
    s_t1 = [place(f"s_t1{i}", [128, DM], F32, PH_P3T, PH_P3T) for i in range(2)]
    s_x1b = [place(f"s_x1b{i}", [128, DM], F32, PH_P3T, PH_P3T) for i in range(2)]
    s_xl2 = [place(f"s_xl2{i}", [128, DM], BF16, PH_P3T, PH_P3T) for i in range(2)]
    s_sqb = [place(f"s_sqb{i}", [128, DM], BF16, PH_P3T, PH_P4B) for i in range(2)]

    s_xl2T = place("s_xl2T", [128, KD * NT], BF16, PH_P3T, PH_P4A)
    s_hid = place("s_hid", [128, 64 * NT], BF16, PH_P4A, PH_P4B)
    s_ust = [place(f"s_ust{i}", [128, 1024], F32, PH_P4A, PH_P4A) for i in range(2)]
    s_gst = [place(f"s_gst{i}", [128, 1024], F32, PH_P4A, PH_P4A) for i in range(2)]
    s_q2 = place("s_q2", [128, 1024], BF16, PH_P4A, PH_P4A)
    s_qu = [place(f"s_qu{i}", [128, 2048], BF16, PH_P4A, PH_P4A) for i in range(2)]
    s_qg = [place(f"s_qg{i}", [128, 2048], BF16, PH_P4A, PH_P4A) for i in range(2)]
    s_hs2 = [place(f"s_hs2{i}", [128, 512], BF16, PH_P4A, PH_P4A) for i in range(2)]
    s_dst = [place(f"s_dst{i}", [128, 512], F32, PH_P4B, PH_P4B) for i in range(3)]
    s_q2d = [place(f"s_q2d{i}", [128, 512], BF16, PH_P4B, PH_P4B) for i in range(2)]
    s_qd = [place(f"s_qd{i}", [128, 512], BF16, PH_P4B, PH_P4B) for i in range(3)]
    s_dev = [place(f"s_dev{i}", [128, 512], F32, PH_P4B, PH_P4B) for i in range(2)]
    s_l3s = place("s_l3s", [128, 16], F32, PH_P4B, PH_P4B)
    s_l3q = place("s_l3q", [128, 16], F32, PH_P4B, PH_P4B)
    s_dsq = [place(f"s_dsq{i}", [128, 512], BF16, PH_P4B, PH_P4B) for i in range(2)]
    s_dar = [place(f"s_dar{i}", [128, DM], F32, PH_P4B, PH_P4B) for i in range(2)]
    s_x1r = [place(f"s_x1r{i}", [128, DM], F32, PH_P4B, PH_P4B) for i in range(2)]
    s_fin = [place(f"s_fin{i}", [128, DM], F32, PH_P4B, PH_P4B) for i in range(2)]

    pb = [nc.alloc_psum_tensor(f"pb{i}", [128, 512], F32) for i in range(8)]
    pbb = [t.ap().bitcast(BF16) for t in pb]

    B = {}

    def g(name):
        if name not in B:
            B[name] = Buf(name)
        return B[name]

    def E(eng, fn, reads=(), writes=(), extra=(), tag=""):
        return bld.op(eng, fn, [g(r) for r in reads], [g(w) for w in writes], extra, tag)

    def D(q, fn, reads=(), writes=(), extra=(), tag="", fam=None):
        return bld.dma(q, fn, [g(r) for r in reads], [g(w) for w in writes],
                       extra, tag, fam=fam or tag)

    tap_t = {}

    def add_tap(name, shape, dtype):
        tap_t[name] = nc.dram_tensor("tap_" + name, list(shape), dtype,
                                     kind="ExternalOutput")

    def tap(name, ap_fn, reads):
        if name in taps:
            D("sp", lambda e, t=tap_t[name], f=ap_fn: e.dma_start(out=t.ap(), in_=f()),
              reads=reads, tag=f"tap_{name}", fam="tap")

    if taps:
        add_tap("dacc", [NT, DM], F32)
        add_tap("l3s", [128, 16], F32)
        add_tap("l3q", [128, 16], F32)
        add_tap("xlnT", [128, KD * NROWS], BF16)
        add_tap("dtfm", [64, NT], F32)
        add_tap("acum", [64, NT], F32)
        add_tap("Bfm", [128, NT], BF16)
        add_tap("Cfm", [128, NT], BF16)
        add_tap("xs", [128, NCH * INTER], BF16)
        add_tap("E", [128, H * 64], F32)
        add_tap("yg", [128, 32 * NT], BF16)
        add_tap("oacc", [128, 8 * DM], F32)
        add_tap("hid", [128, 64 * NT], BF16)
        add_tap("srow", [1, 8], F32)
        add_tap("gmt", [128, NCH * 128], BF16)
        add_tap("hilo", [64, NCH * 128], BF16)
        add_tap("actm", [128, NCH * 64], F32)
        add_tap("dsdt", [128, NCH * 64], F32)
        add_tap("lam", [128, NCH * 64], F32)

    for nm, dd in [("s_idf", c_idf), ("s_idb", c_idb), ("s_tri", c_tri),
                   ("s_ocb", c_ocb), ("s_orf", c_orf), ("s_ocf", c_ocf),
                   ("s_cw", d_cw), ("s_cb", d_cb), ("s_alog", d_alog),
                   ("s_dtb", d_dtb), ("s_drow", d_drow), ("s_nw", d_nw),
                   ("s_e0m", d_e0m), ("s_kc", c_kc)]:
        D("sp", lambda e, s=S[nm], dd=dd: e.dma_start(out=s.ap(), in_=dd.ap()),
          [], [nm], tag="ldc", fam="ldc" + str(len(B) % 2))
    E("dve", lambda e: e.memset(s_z64.ap(), 0.0), [], ["s_z64"], tag="z64")
    E("dve", lambda e: e.memset(s_part[:, 0:3], 0.0),
      [], ["s_part#0", "s_part#1", "s_part#2"], tag="partz")

    E("pe", lambda e: e.matmul(pb[7][:, 0:64], s_orf.ap(), s_drow.ap(),
                               start=True, stop=True),
      ["s_orf", "s_drow"], ["pb7"], tag="dbc_mm")
    E("dve", lambda e: e.tensor_copy(s_dbc.ap(), pb[7][:, 0:64]),
      ["pb7"], ["s_dbc"], tag="dbc_ev")

    def ln_stats(src_ap, src_bufs, sqbuf, sqname, tagp, cb=0):
        Fdim = src_ap.shape[-1]

        def cc_(i):
            return s_col[:, cb + i:cb + i + 1]

        def cn_(i):
            return f"s_col#{cb + i}"

        E("dve", lambda e: e.tensor_reduce(cc_(0), src_ap, AX.X, ALU.add),
          src_bufs, [cn_(0)], tag=tagp + "_s")
        E("act", lambda e: e.activation(sqbuf, src_ap, AF.Square,
                                        accum_out=cc_(1)),
          src_bufs, [sqname, cn_(1)], tag=tagp + "_q")
        E("dve", lambda e: e.tensor_scalar(cc_(2), cc_(0),
                                           1.0 / Fdim, None, ALU.mult),
          [cn_(0)], [cn_(2)], tag=tagp + "_m")
        E("dve", lambda e: e.tensor_tensor(cc_(4), cc_(2), cc_(2), ALU.mult),
          [cn_(2)], [cn_(4)], tag=tagp + "_m2")
        E("dve", lambda e: e.tensor_scalar(cc_(5), cc_(1),
                                           1.0 / Fdim, None, ALU.mult),
          [cn_(1)], [cn_(5)], tag=tagp + "_v1")
        E("dve", lambda e: e.tensor_tensor(cc_(5), cc_(5), cc_(4), ALU.subtract),
          [cn_(4), cn_(5)], [cn_(5)], tag=tagp + "_v2")
        E("act", lambda e: e.activation(cc_(6), cc_(5), AF.Sqrt,
                                        bias=s_kc[:, 0:1]),
          [cn_(5), "s_kc"], [cn_(6)], tag=tagp + "_sd")
        E("dve", lambda e: e.reciprocal(cc_(7), cc_(6)), [cn_(6)], [cn_(7)],
          tag=tagp + "_r")

    for ti in range(9):
        i2 = ti % 2
        xb, xl, sq = s_xtm[i2], s_xln[i2], s_sq0[i2]
        xbn, xln_n, sqn = f"s_xtm{i2}", f"s_xln{i2}", f"s_sq0{i2}"
        D("sp", lambda e, ti=ti, xb=xb: e.dma_start(
            out=xb.ap(), in_=d_x[ti * 128:(ti + 1) * 128, :]), [], [xbn],
          tag=f"ldx{ti}", fam=xbn)
        ln_stats(xb.ap(), [xbn], sq.ap(), sqn, f"ln1_{ti}")
        E("dve", lambda e, xb=xb, xl=xl: e.tensor_scalar(
            xl.ap(), xb.ap(), s_col[:, 2:3], s_col[:, 7:8], ALU.subtract, ALU.mult),
          [xbn, "s_col#2", "s_col#7"], [xln_n], tag=f"ln1a_{ti}")
        for kc in range(KD):
            bk = 6 + (kc % 2)
            E("pe", lambda e, xl=xl, kc=kc, bk=bk: e.transpose(
                pbb[bk][:, 0:128], xl[:, kc * 128:(kc + 1) * 128], s_idb.ap()),
              [xln_n, "s_idb"], [f"pb{bk}"], tag="xt")
            E("dve", lambda e, kc=kc, ti=ti, bk=bk: e.tensor_copy(
                s_xlnT[:, kc * NROWS + ti * 128: kc * NROWS + (ti + 1) * 128],
                pbb[bk][:, 0:128]),
              [f"pb{bk}"], [f"s_xlnT#{kc}"], tag="xte")
    tap("xlnT", lambda: s_xlnT.ap(), [f"s_xlnT#{k}" for k in range(KD)])

    for wi, (dsrc, nrow) in enumerate([(d_wus, 256), (d_wgs, 256), (d_wds, 1024)]):
        ncol = DE if wi < 2 else DM
        rb_, cb_ = nrow // 128, ncol // 2048
        for pi in range(8):
            a, b2 = pi // cb_, pi % cb_
            buf = s_spc[pi % 2]
            bn = f"s_spc{pi % 2}"
            D("act", lambda e, dsrc=dsrc, a=a, b2=b2, buf=buf: e.dma_start(
                out=buf.ap(),
                in_=dsrc[a * 128:(a + 1) * 128, b2 * 2048:(b2 + 1) * 2048]),
              [], [bn], tag=f"scld{wi}_{pi}", fam=bn)
            E("dve", lambda e, buf=buf: e.tensor_reduce(
                s_part[:, 3:4], buf.ap(), AX.X, ALU.add, apply_absolute_value=True),
              [bn], ["s_part#3"], tag=f"scr{wi}_{pi}")
            E("dve", lambda e, wi=wi: e.tensor_tensor(
                s_part[:, wi:wi + 1], s_part[:, wi:wi + 1], s_part[:, 3:4], ALU.add),
              ["s_part#3", f"s_part#{wi}"], [f"s_part#{wi}"], tag=f"sca{wi}_{pi}")

    if upto < PH_P1:
        return _finish(nc, bld, tap_t, ar, arena_addr, B)

    lpair = {}

    def inproj_load(fidxs, pi):
        fws = [64 if f == 66 else 128 for f in fidxs]
        fw = sum(fws)
        f0 = fidxs[0] * 128
        wbf = s_wbf[pi]
        wbfn = f"s_wbf{pi}"
        src = d_wip.ap()[:, f0:f0 + fw].rearrange("(kc kp) f -> kp kc f", kp=128)
        D("pool", lambda e, src=src, wbf=wbf, fw=fw: e.dma_start(
            out=wbf[:, 0:KD * fw].rearrange("p (kc f) -> p kc f", f=fw), in_=src),
          [], [wbfn], tag=f"ipld{fidxs[0]}", fam=wbfn)
        for k, f in enumerate(fidxs):
            lpair[f] = (pi, k * fws[0], fw)

    def emit_inproj(fidx, kind, slab_i=0, tp_thunks=None):
        fw = 64 if fidx == 66 else 128
        pi, sub, pfw = lpair[fidx]
        wbf = s_wbf[pi]
        wbfn = f"s_wbf{pi}"
        halo = kind in ("B", "C", "hs")
        bs = ((fidx % 2) * 2) if kind == "gate" else (0 if fidx % 2 == 0 else 3)
        for kc in range(KD):
            if tp_thunks and kc % 2 == 1:
                tp_thunks.pop(0)()
            st, sp_ = (kc == 0), (kc == KD - 1)
            lsl = slice(kc * pfw + sub, kc * pfw + sub + fw)
            E("pe", lambda e, kc=kc, fw=fw, wbf=wbf, bs=bs, st=st, sp_=sp_,
              lsl=lsl: e.matmul(
                pb[bs][0:fw, 0:512], wbf[:, lsl],
                s_xlnT[:, kc * NROWS + 128: kc * NROWS + 640], start=st, stop=sp_),
              [wbfn, f"s_xlnT#{kc}"], [f"pb{bs}"], tag=f"ipA{fidx}")
            E("pe", lambda e, kc=kc, fw=fw, wbf=wbf, bs=bs, st=st, sp_=sp_,
              lsl=lsl: e.matmul(
                pb[bs + 1][0:fw, 0:512], wbf[:, lsl],
                s_xlnT[:, kc * NROWS + 640: kc * NROWS + 1152], start=st, stop=sp_),
              [wbfn, f"s_xlnT#{kc}"], [f"pb{bs + 1}"], tag=f"ipB{fidx}")
            if halo:
                E("pe", lambda e, kc=kc, fw=fw, wbf=wbf, bs=bs, st=st, sp_=sp_,
                  lsl=lsl: e.matmul(
                    pb[bs + 2][0:fw, 0:128], wbf[:, lsl],
                    s_xlnT[:, kc * NROWS: kc * NROWS + 128], start=st, stop=sp_),
                  [wbfn, f"s_xlnT#{kc}"], [f"pb{bs + 2}"], tag=f"ipH{fidx}")

        if kind == "dt":
            for tt in range(2):
                bk = bs + tt
                pp = pb[bk]
                E("act", lambda e, pp=pp: e.activation(
                    s_sp1.ap(), pp[0:64, 0:512], AF.Abs, bias=s_dtb.ap()),
                  [f"pb{bk}", "s_dtb"], ["s_sp1"], tag="dt_abs")
                E("act", lambda e: e.activation(s_sp2.ap(), s_sp1.ap(), AF.Exp,
                                                scale=-1.0),
                  ["s_sp1"], ["s_sp2"], tag="dt_exp")
                E("act", lambda e: e.activation(s_sp1.ap(), s_sp2.ap(), AF.Ln,
                                                bias=s_kc[0:64, 1:2]),
                  ["s_sp2", "s_kc"], ["s_sp1"], tag="dt_ln")
                E("act", lambda e, pp=pp: e.activation(
                    s_sp3.ap(), pp[0:64, 0:512], AF.Relu, bias=s_dtb.ap()),
                  [f"pb{bk}", "s_dtb"], ["s_sp3"], tag="dt_rel")
                E("dve", lambda e, tt=tt: e.tensor_tensor(
                    s_dtfm[:, tt * 512:(tt + 1) * 512], s_sp1.ap(), s_sp3.ap(),
                    ALU.add),
                  ["s_sp1", "s_sp3"], ["s_dtfm"], tag="dt_add")
            return

        if kind == "gate":
            j = fidx
            sgb = s_sgb[j % 2]
            sgn = f"s_sgb{j % 2}"
            E("act", lambda e, sgb=sgb, bs=bs: e.activation(
                sgb[:, 0:512], pb[bs][0:128, 0:512], AF.Silu),
              [f"pb{bs}"], [sgn], tag=f"g_s0_{j}")
            E("act", lambda e, sgb=sgb, bs=bs: e.activation(
                sgb[:, 512:1024], pb[bs + 1][0:128, 0:512], AF.Silu),
              [f"pb{bs + 1}"], [sgn], tag=f"g_s1_{j}")
            D("sp", lambda e, sgb=sgb, j=j: e.dma_start(
                out=d_sg[j * 128:(j + 1) * 128, :], in_=sgb.ap()),
              [sgn], [f"dsg{j % 2}"], tag=f"g_st_{j}", fam=f"dsg{j % 2}")
            return

        slab = s_raw[slab_i % 3]
        slabn = f"s_raw{slab_i % 3}"
        E("dve", lambda e, slab=slab, bs=bs: e.tensor_copy(
            slab[:, 0:128], pb[bs + 2][0:128, 0:128]),
          [f"pb{bs + 2}"], [slabn], tag="cv_e0")
        E("dve", lambda e, slab=slab, bs=bs: e.tensor_copy(
            slab[:, 128:640], pb[bs][0:128, 0:512]),
          [f"pb{bs}"], [slabn], tag="cv_e1")
        E("dve", lambda e, slab=slab, bs=bs: e.tensor_copy(
            slab[:, 640:1152], pb[bs + 1][0:128, 0:512]),
          [f"pb{bs + 1}"], [slabn], tag="cv_e2")
        j34 = fidx - 32
        ca, cb2 = s_cacc[0], s_cacc[1]
        a0, a1 = s_ca4[0], s_ca4[1]
        E("act", lambda e, slab=slab, j=j34, a0=a0: e.activation(
            a0.ap(), slab[:, 125:1149], AF.Copy, scale=s_cw[:, j * 4: j * 4 + 1]),
          [slabn, "s_cw"], ["s_ca40"], tag="cv_t0")
        E("act", lambda e, slab=slab, j=j34, a1=a1: e.activation(
            a1.ap(), slab[:, 126:1150], AF.Copy,
            scale=s_cw[:, j * 4 + 1: j * 4 + 2]),
          [slabn, "s_cw"], ["s_ca41"], tag="cv_t1")
        E("pool", lambda e, a0=a0, a1=a1, ca=ca: e.tensor_tensor(
            ca.ap(), a0.ap(), a1.ap(), ALU.add),
          ["s_ca40", "s_ca41"], ["s_cacc0"], tag="cv_p01")
        E("dve", lambda e, slab=slab, j=j34, ca=ca, cb2=cb2: e.scalar_tensor_tensor(
            cb2.ap(), slab[:, 127:1151], s_cw[:, j * 4 + 2: j * 4 + 3], ca.ap(),
            ALU.mult, ALU.add),
          [slabn, "s_cacc0", "s_cw"], ["s_cacc1"], tag="cv_t2")
        E("dve", lambda e, slab=slab, j=j34, ca=ca, cb2=cb2: e.scalar_tensor_tensor(
            ca.ap(), slab[:, 128:1152], s_cw[:, j * 4 + 3: j * 4 + 4], cb2.ap(),
            ALU.mult, ALU.add),
          [slabn, "s_cacc1", "s_cw"], ["s_cacc0"], tag="cv_t3")
        if kind == "B":
            dst, dstn = s_Bfm, "s_Bfm"
        elif kind == "C":
            dst, dstn = s_Cfm, "s_Cfm"
        else:
            dst, dstn = s_hso[slab_i % 3], f"s_hso{slab_i % 3}"
        E("act", lambda e, dst=dst, j=j34, ca=ca: e.activation(
            dst.ap(), ca.ap(), AF.Silu, bias=s_cb[:, j:j + 1]),
          ["s_cacc0", "s_cb"], [dstn], tag=f"cv_si{fidx}")
        if kind == "hs":
            return (dst, dstn, fidx - 32)

    inproj_load([66], 0)
    emit_inproj(66, "dt")
    inproj_load([64, 65], 1)
    emit_inproj(64, "B", 0)
    emit_inproj(65, "C", 1)
    tap("dtfm", lambda: s_dtfm.ap(), ["s_dtfm"])
    tap("Bfm", lambda: s_Bfm.ap(), ["s_Bfm"])
    tap("Cfm", lambda: s_Cfm.ap(), ["s_Cfm"])

    E("act", lambda e: e.activation(s_sp1[:, 0:1], s_alog.ap(), AF.Exp),
      ["s_alog"], ["s_sp1"], tag="negA_e")
    E("dve", lambda e: e.tensor_scalar(s_negA.ap(), s_sp1[:, 0:1], -1.0, None,
                                       ALU.mult),
      ["s_sp1"], ["s_negA"], tag="negA_m")
    E("dve", lambda e: e.tensor_scalar(s_adt.ap(), s_dtfm.ap(), s_negA.ap(), None,
                                       ALU.mult),
      ["s_dtfm", "s_negA"], ["s_adt"], tag="adt")

    inproj_load([32, 33], 0)
    for c in range(NCH):
        cs = slice(c * 128, (c + 1) * 128)
        hc = slice(c * 64, (c + 1) * 64)
        E("dve", lambda e, cs=cs: e.tensor_tensor_scan(
            s_acum[:, cs], s_adt[:, cs], s_z64.ap(), 0.0, ALU.add, ALU.add),
          ["s_adt", "s_z64"], ["s_acum"], tag=f"scan{c}")
        E("dve", lambda e, c=c, cs=cs: e.tensor_scalar(
            s_dsf.ap(), s_acum[:, cs], s_acum[:, c * 128 + 127: c * 128 + 128],
            None, ALU.subtract),
          ["s_acum"], ["s_dsf"], tag=f"ds1_{c}")
        E("act", lambda e: e.activation(s_dsf2.ap(), s_dsf.ap(), AF.Exp, scale=-1.0),
          ["s_dsf"], ["s_dsf2"], tag=f"ds2_{c}")
        E("pe", lambda e, c=c: e.transpose(
            pb[5][0:1, 0:64], s_acum[:, c * 128 + 127: c * 128 + 128],
            s_idf[0:64, 0:64]),
          ["s_acum", "s_idf"], ["pb5"], tag=f"lamt{c}")
        E("dve", lambda e: e.tensor_copy(s_lrow.ap(), pb[5][0:1, 0:64]),
          ["pb5"], ["s_lrow"], tag=f"lame{c}")
        E("pe", lambda e: e.matmul(pb[5][:, 64:128], s_orf.ap(), s_lrow.ap(),
                                   start=True, stop=True),
          ["s_lrow", "s_orf"], ["pb5"], tag=f"lamb{c}")
        E("act", lambda e, hc=hc: e.activation(
            s_lam[:, hc], pb[5][:, 64:128], AF.Exp),
          ["pb5"], ["s_lam"], tag=f"lamx{c}")
        E("pe", lambda e, cs=cs: e.transpose(pb[4][:, 0:64], s_acum[:, cs],
                                             s_idf[0:64, 0:64]),
          ["s_acum", "s_idf"], ["pb4"], tag=f"at{c}")
        E("dve", lambda e, hc=hc: e.tensor_copy(s_actm[:, hc], pb[4][:, 0:64]),
          ["pb4"], ["s_actm"], tag=f"ate{c}")
        E("pe", lambda e, cs=cs: e.transpose(pb[4][:, 64:128], s_dtfm[:, cs],
                                             s_idf[0:64, 0:64]),
          ["s_dtfm", "s_idf"], ["pb4"], tag=f"dtt{c}")
        E("dve", lambda e, hc=hc: e.tensor_copy(s_dttm[:, hc], pb[4][:, 64:128]),
          ["pb4"], ["s_dttm"], tag=f"dtte{c}")
        E("pe", lambda e: e.transpose(pb[4][:, 128:192], s_dsf2.ap(),
                                      s_idf[0:64, 0:64]),
          ["s_dsf2", "s_idf"], ["pb4"], tag=f"dst{c}")
        E("dve", lambda e, hc=hc: e.tensor_tensor(
            s_dsdt[:, hc], pb[4][:, 128:192], s_dttm[:, hc], ALU.mult),
          ["pb4", "s_dttm"], ["s_dsdt"], tag=f"dsdt{c}")
        E("dve", lambda e, cs=cs: e.tensor_copy(s_hi[:, cs], s_acum[:, cs]),
          ["s_acum"], ["s_hi"], tag=f"hi{c}")
        E("dve", lambda e, cs=cs: e.tensor_tensor(
            s_lo[:, cs], s_acum[:, cs], s_hi[:, cs], ALU.subtract),
          ["s_acum", "s_hi"], ["s_lo"], tag=f"lo{c}")
        E("pe", lambda e, cs=cs: e.matmul(pb[6][0:128, 0:128], s_Bfm[:, cs],
                                          s_Cfm[:, cs], start=True, stop=True),
          ["s_Bfm", "s_Cfm"], ["pb6"], tag=f"gm{c}")
        E("dve", lambda e, cs=cs: e.tensor_tensor(
            s_gmt[:, cs], pb[6][0:128, 0:128], s_tri.ap(), ALU.mult),
          ["pb6", "s_tri"], ["s_gmt"], tag=f"gme{c}")
        E("pe", lambda e, cs=cs: e.transpose(pbb[7][:, 0:128], s_Bfm[:, cs],
                                             s_idb.ap()),
          ["s_Bfm", "s_idb"], ["pb7"], tag=f"bt{c}")
        E("dve", lambda e, cs=cs: e.tensor_copy(s_btm[:, cs], pbb[7][:, 0:128]),
          ["pb7"], ["s_btm"], tag=f"bte{c}")

    tap("acum", lambda: s_acum.ap(), ["s_acum"])
    tap("gmt", lambda: s_gmt.ap(), ["s_gmt"])
    tap("hilo", lambda: s_hi.ap(), ["s_hi"])
    tap("actm", lambda: s_actm.ap(), ["s_actm"])
    tap("dsdt", lambda: s_dsdt.ap(), ["s_dsdt"])
    tap("lam", lambda: s_lam.ap(), ["s_lam"])

    if upto < PH_P1B:
        return _finish(nc, bld, tap_t, ar, arena_addr, B)

    def hs_transpose_thunks(dst, dstn, jj):
        thunks = []
        for c in range(NCH):
            bk = 6 + (c % 2)

            def thunk(c=c, bk=bk, dst=dst, dstn=dstn, jj=jj):
                E("pe", lambda e, dst=dst, c=c, bk=bk: e.transpose(
                    pbb[bk][:, 0:128], dst[:, c * 128:(c + 1) * 128], s_idb.ap()),
                  [dstn, "s_idb"], [f"pb{bk}"], tag="hs_t")
                E("dve", lambda e, c=c, jj=jj, bk=bk: e.tensor_copy(
                    s_xs[:, c * INTER + jj * 128: c * INTER + jj * 128 + 128],
                    pbb[bk][:, 0:128]),
                  [f"pb{bk}"], [f"s_xs#{c}"], tag="hs_te")
            thunks.append(thunk)
        return thunks

    hs_pend = []
    for j, fidx in enumerate(range(32, 64)):
        if fidx % 2 == 0 and fidx + 2 < 64:
            inproj_load([fidx + 2, fidx + 3], (fidx // 2 + 1) % 2)
        tps_ = hs_pend.pop(0) if len(hs_pend) > 2 else None
        hs_pend.append(hs_transpose_thunks(*emit_inproj(fidx, "hs", slab_i=j,
                                                        tp_thunks=tps_)))
    for item in hs_pend:
        for t_ in item:
            t_()
    tap("xs", lambda: s_xs.ap(), [f"s_xs#{c}" for c in range(NCH)])

    if upto < PH_P2A:
        return _finish(nc, bld, tap_t, ar, arena_addr, B)

    E("dve", lambda e: e.memset(s_E.ap(), 0.0), [],
      [f"s_E#{h}" for h in range(H)], tag="Ez")

    def p2a_s0(it):
        c, h = divmod(it, H)
        i2 = it % 4
        hcol = slice(c * 64 + h, c * 64 + h + 1)
        E("dve", lambda e, i2=i2, c=c, h=h, hcol=hcol: e.tensor_scalar(
            s_xc[i2].ap(), s_xs[:, c * INTER + h * 64: c * INTER + h * 64 + 64],
            s_dsdt[:, hcol], None, ALU.mult),
          [f"s_xs#{c}", "s_dsdt"], [f"s_xc{i2}"], tag=f"Axc{it}")

    def p2a_s1(it):
        c, h = divmod(it, H)
        sb_ = 4 + (it % 2)
        cs = slice(c * 128, (c + 1) * 128)
        E("pe", lambda e, sb_=sb_, cs=cs, i2=it % 4: e.matmul(
            pb[sb_][0:128, 0:64], s_btm[:, cs], s_xc[i2].ap(),
            start=True, stop=True),
          ["s_btm", f"s_xc{it % 4}"], [f"pb{sb_}"], tag=f"Amm{it}")

    def p2a_s2(it):
        c, h = divmod(it, H)
        sb_ = 4 + (it % 2)
        hcol = slice(c * 64 + h, c * 64 + h + 1)
        E("dve", lambda e, h=h, hcol=hcol, sb_=sb_: e.scalar_tensor_tensor(
            s_E[:, h * 64:(h + 1) * 64], s_E[:, h * 64:(h + 1) * 64],
            s_lam[:, hcol], pb[sb_][0:128, 0:64], ALU.mult, ALU.add),
          [f"s_E#{h}", "s_lam", f"pb{sb_}"], [f"s_E#{h}"], tag=f"Aeu{it}")

    NITA = NCH * H
    for k in range(NITA + 2):
        if k < NITA:
            p2a_s0(k)
        if 1 <= k <= NITA:
            p2a_s1(k - 1)
        if 2 <= k <= NITA + 1:
            p2a_s2(k - 2)
        if k % 40 == 0 and k // 40 < 8:
            g2 = (k // 40) * 2
            inproj_load([g2, g2 + 1], (k // 40) % 2)
        if k % 40 == 12 and k // 40 < 8:
            emit_inproj((k // 40) * 2, "gate")
        if k % 40 == 28 and k // 40 < 8:
            emit_inproj((k // 40) * 2 + 1, "gate")

    if upto < PH_CC:
        return _finish(nc, bld, tap_t, ar, arena_addr, B)

    D("act", lambda e: e.dma_start(out=d_ccin.ap()[:, 0:4096], in_=s_E.ap()),
      [f"s_E#{h}" for h in range(H)], ["dccin"], tag="ccs_e", fam="ccin")
    D("act", lambda e: e.dma_start(out=d_ccin.ap()[:, 4096:4100],
                                   in_=s_part[:, 0:4]),
      ["s_part#0", "s_part#1", "s_part#2", "s_part#3"], ["dccin"],
      tag="ccs_p", fam="ccin")
    bld.cc(lambda e: e.collective_compute(
        "AllGather", ALU.bypass, replica_groups=[list(range(8))],
        ins=[d_ccin.ap().opt()], outs=[d_ccout.ap().opt()]),
        reads=[g("dccin")], writes=[g("dccout")], tag="allgather")
    for g2 in range(8, 16):
        inproj_load([g2 * 2, g2 * 2 + 1], g2 % 2)
        emit_inproj(g2 * 2, "gate")
        emit_inproj(g2 * 2 + 1, "gate")

    if upto < PH_E0:
        return _finish(nc, bld, tap_t, ar, arena_addr, B)

    E("dve", lambda e: e.memset(s_E.ap(), 0.0), [],
      [f"s_E#{h}" for h in range(H)], tag="E0z")
    E("pe", lambda e: e.matmul(pb[5][:, 0:8], s_orf.ap(), s_e0m.ap(),
                               start=True, stop=True),
      ["s_orf", "s_e0m"], ["pb5"], tag="e0m_mm")
    E("dve", lambda e: e.tensor_copy(s_mcols.ap(), pb[5][:, 0:8]),
      ["pb5"], ["s_mcols"], tag="e0m_ev")
    allE = [f"s_E#{h}" for h in range(H)]
    for q in range(8):
        i2 = q % 2
        sl = s_slot[i2]
        sln = f"s_slot{i2}"
        D("act", lambda e, q=q, sl=sl: e.dma_start(
            out=sl.ap(), in_=d_ccout[q * 128:(q + 1) * 128, 0:4096]),
          ["dccout"], [sln], tag=f"e0ld{q}", fam=sln)
        E("dve", lambda e, q=q, sl=sl: e.scalar_tensor_tensor(
            s_E.ap(), sl.ap(), s_mcols[:, q:q + 1], s_E.ap(),
            ALU.mult, ALU.add),
          [sln, "s_mcols"] + allE, allE, tag=f"e0a{q}")
    tap("E", lambda: s_E.ap(), [f"s_E#{h}" for h in range(H)])

    for q in range(8):
        D("pool", lambda e, q=q: e.dma_start(
            out=s_spall[:, q * 3:(q + 1) * 3],
            in_=d_ccout[q * 128:(q + 1) * 128, 4096:4099]),
          ["dccout"], ["s_spall"], tag=f"scg{q}", fam="spall")
    E("dve", lambda e: e.tensor_copy(s_sp3r.ap(), s_spall[:, 0:3]),
      ["s_spall"], ["s_sp3r"], tag="scc0")
    for q in range(1, 8):
        E("dve", lambda e, q=q: e.tensor_tensor(
            s_sp3r.ap(), s_sp3r.ap(), s_spall[:, q * 3:(q + 1) * 3], ALU.add),
          ["s_spall", "s_sp3r"], ["s_sp3r"], tag=f"scc{q}")
    E("pe", lambda e: e.matmul(pb[5][0:1, 8:11], s_ocf.ap(), s_sp3r.ap(),
                               start=True, stop=True),
      ["s_ocf", "s_sp3r"], ["pb5"], tag="sc_mm")
    E("dve", lambda e: e.tensor_copy(s_srow[0:1, 0:3], pb[5][0:1, 8:11]),
      ["pb5"], ["s_srow"], tag="sc_ev")
    E("dve", lambda e: e.tensor_scalar(
        s_srow[0:1, 0:3], s_srow[0:1, 0:3], 1.0 / (DE * DM), EPS_Q,
        ALU.mult, ALU.add),
      ["s_srow"], ["s_srow"], tag="sc_sc")
    E("dve", lambda e: e.tensor_scalar(
        s_srow[0:1, 3:6], s_srow[0:1, 0:3], 0.5, None, ALU.mult),
      ["s_srow"], ["s_srow"], tag="sc_thr")
    E("dve", lambda e: e.tensor_tensor(
        s_srow[0:1, 6:7], s_srow[0:1, 4:5], s_srow[0:1, 5:6], ALU.mult),
      ["s_srow"], ["s_srow"], tag="sc_fg")
    E("dve", lambda e: e.tensor_tensor(
        s_srow[0:1, 7:8], s_srow[0:1, 1:2], s_srow[0:1, 5:6], ALU.mult),
      ["s_srow"], ["s_srow"], tag="sc_fg2")
    tap("srow", lambda: s_srow.ap(), ["s_srow"])
    E("pe", lambda e: e.matmul(pb[5][:, 16:24], s_orf.ap(), s_srow.ap(),
                               start=True, stop=True),
      ["s_orf", "s_srow"], ["pb5"], tag="sc_bc")
    E("dve", lambda e: e.tensor_copy(s_bcols[:, 0:8], pb[5][:, 16:24]),
      ["pb5"], ["s_bcols"], tag="sc_bce")
    E("dve", lambda e: e.tensor_scalar(
        s_bcols[:, 8:11], s_bcols[:, 3:6], -1.0, None, ALU.mult),
      ["s_bcols"], ["s_bcols"], tag="sc_neg")

    if upto < PH_P2B:
        return _finish(nc, bld, tap_t, ar, arena_addr, B)

    D("sp", lambda e: e.dma_start(out=s_sel.ap(), in_=c_sel.ap()),
      [], ["s_sel"], tag="selld", fam="sel")

    def it2idx(it):
        hp, r = divmod(it, 16)
        c, hh = divmod(r, 2)
        return hp, c, hh

    def p2b_s0(it):
        hp, c, hh = it2idx(it)
        h = 2 * hp + hh
        i4 = it % 4
        cs = slice(c * 128, (c + 1) * 128)
        hcol = slice(c * 64 + h, c * 64 + h + 1)
        bcb = [2, 3, 7][it % 3]
        if c == 0 and hh == 0:
            sgb = s_sgb[hp % 2]
            D("sp", lambda e, hp=hp, sgb=sgb: e.dma_start(
                out=sgb.ap(), in_=d_sg[hp * 128:(hp + 1) * 128, :]),
              ["dsg0", "dsg1"], [f"s_sgb{hp % 2}"], tag=f"sgld{hp}",
              fam=f"s_sgb{hp % 2}")
        if c == 0:
            dii = (hp % 2) * 2 + hh
            E("dve", lambda e, dii=dii, h=h: e.tensor_scalar(
                s_di[dii].ap(), s_idb.ap(), s_dbc[:, h:h + 1], None, ALU.mult),
              ["s_idb", "s_dbc"], [f"s_di{dii}"], tag=f"dih{h}")
        E("pe", lambda e, h=h, cs=cs, bcb=bcb: e.matmul(
            pb[bcb][0:128, 0:128], s_sel[:, h * 128:(h + 1) * 128],
            s_hi[:, cs], start=True, stop=False),
          ["s_sel", "s_hi"], [f"pb{bcb}"], tag=f"bc1_{it}")
        E("pe", lambda e, h=h, cs=cs, bcb=bcb: e.matmul(
            pb[bcb][0:128, 0:128], s_sel[:, h * 128:(h + 1) * 128],
            s_lo[:, cs], start=False, stop=True),
          ["s_sel", "s_lo"], [f"pb{bcb}"], tag=f"bc2_{it}")
        E("dve", lambda e, i4=i4, c=c, h=h, hcol=hcol: e.tensor_scalar(
            s_xc[i4].ap(), s_xs[:, c * INTER + h * 64: c * INTER + h * 64 + 64],
            s_dttm[:, hcol], None, ALU.mult),
          [f"s_xs#{c}", "s_dttm"], [f"s_xc{i4}"], tag=f"xc_{it}")
        E("dve", lambda e, i4=i4, c=c, h=h, hcol=hcol: e.tensor_scalar(
            s_bd[i4][:, 0:64], s_xs[:, c * INTER + h * 64: c * INTER + h * 64 + 64],
            s_dsdt[:, hcol], None, ALU.mult),
          [f"s_xs#{c}", "s_dsdt"], [f"s_bd{i4}"], tag=f"xc2_{it}")

    def p2b_s1(it):
        hp, c, hh = it2idx(it)
        h = 2 * hp + hh
        i4 = it % 4
        cs = slice(c * 128, (c + 1) * 128)
        hcol = slice(c * 64 + h, c * 64 + h + 1)
        bcb = [2, 3, 7][it % 3]
        sb_ = it % 2
        E("act", lambda e, i4=i4, hcol=hcol, bcb=bcb: e.activation(
            s_scr[i4].ap(), pb[bcb][0:128, 0:128], AF.Relu, scale=-1.0,
            bias=s_actm[:, hcol]),
          [f"pb{bcb}", "s_actm"], [f"s_scr{i4}"], tag=f"rel_{it}")
        E("act", lambda e, i4=i4, bcb=bcb: e.activation(
            s_sdt[i4].ap(), pb[bcb][0:128, 0:128], AF.Exp),
          [f"pb{bcb}"], [f"s_sdt{i4}"], tag=f"sd_{it}")
        E("act", lambda e, i4=i4: e.activation(
            s_et[i4].ap(), s_scr[i4].ap(), AF.Exp, scale=-1.0),
          [f"s_scr{i4}"], [f"s_et{i4}"], tag=f"exp_{it}")
        E("pool", lambda e, i4=i4, cs=cs: e.tensor_tensor(
            s_mt[i4].ap(), s_et[i4].ap(), s_gmt[:, cs], ALU.mult),
          [f"s_et{i4}", "s_gmt"], [f"s_mt{i4}"], tag=f"mt_{it}")
        E("pool", lambda e, i4=i4, cs=cs: e.tensor_tensor(
            s_cd[i4].ap(), s_Cfm[:, cs], s_sdt[i4].ap(), ALU.mult),
          ["s_Cfm", f"s_sdt{i4}"], [f"s_cd{i4}"], tag=f"cd_{it}")
        E("pe", lambda e, i4=i4, sb_=sb_, cs=cs: e.matmul(
            pb[sb_][0:128, 0:64], s_btm[:, cs], s_bd[i4][:, 0:64],
            start=True, stop=True),
          ["s_btm", f"s_bd{i4}"], [f"pb{sb_}"], tag=f"smm_{it}")
        E("dve", lambda e, i4=i4, h=h: e.tensor_copy(
            s_ebf[i4].ap(), s_E[:, h * 64:(h + 1) * 64]),
          [f"s_E#{h}"], [f"s_ebf{i4}"], tag=f"ebf_{it}")
        E("dve", lambda e, h=h, hcol=hcol, sb_=sb_, i4=i4: e.scalar_tensor_tensor(
            s_E[:, h * 64:(h + 1) * 64], s_E[:, h * 64:(h + 1) * 64],
            s_lam[:, hcol], pb[sb_][0:128, 0:64], ALU.mult, ALU.add),
          [f"s_E#{h}", "s_lam", f"pb{sb_}", f"s_ebf{i4}"], [f"s_E#{h}"],
          tag=f"eu_{it}")

    def p2b_s2(it):
        hp, c, hh = it2idx(it)
        h = 2 * hp + hh
        i4 = it % 4
        dii = (hp % 2) * 2 + hh
        yb = 4 + ((hp * NCH + c) % 3)
        tp = None if hh == 0 else (0, 64)
        lo, hi2 = hh * 64, hh * 64 + 64
        E("pe", lambda e, yb=yb, lo=lo, hi2=hi2, tp=tp, i4=i4: e.matmul(
            pb[yb][lo:hi2, 0:128], s_xc[i4].ap(),
            s_mt[i4].ap(), start=True, stop=False, tile_position=tp),
          [f"s_xc{i4}", f"s_mt{i4}"], [f"pb{yb}"], tag=f"yd_{it}")
        E("pe", lambda e, h=h, c=c, yb=yb, lo=lo, hi2=hi2, tp=tp, dii=dii:
          e.matmul(
            pb[yb][lo:hi2, 0:128],
            s_xs[:, c * INTER + h * 64: c * INTER + h * 64 + 64],
            s_di[dii].ap(), start=False, stop=False, tile_position=tp),
          [f"s_xs#{c}", f"s_di{dii}"], [f"pb{yb}"], tag=f"yr_{it}")
        E("pe", lambda e, yb=yb, lo=lo, hi2=hi2, tp=tp, i4=i4: e.matmul(
            pb[yb][lo:hi2, 0:128], s_ebf[i4].ap(), s_cd[i4].ap(),
            start=False, stop=True, tile_position=tp),
          [f"s_ebf{i4}", f"s_cd{i4}"], [f"pb{yb}"], tag=f"yo_{it}")
        if hh == 1:
            sgb = s_sgb[hp % 2]
            E("dve", lambda e, hp=hp, c=c, yb=yb, sgb=sgb: e.tensor_tensor(
                s_yg[:, hp * NT + c * 128: hp * NT + (c + 1) * 128],
                pb[yb][0:128, 0:128], sgb[:, c * 128:(c + 1) * 128], ALU.mult),
              [f"pb{yb}", f"s_sgb{hp % 2}"], [f"s_yg#{hp}"], tag=f"yg_{it}")
            if c == NCH - 1:
                ys = s_ys[hp % 2]
                ysn = f"s_ys{hp % 2}"
                E("act", lambda e, hp=hp, ys=ys: e.activation(
                    ys.ap(), s_yg[:, hp * NT:(hp + 1) * NT], AF.Square),
                  [f"s_yg#{hp}"], [ysn], tag=f"sq_{hp}")
                sbk = 4 + (hp % 3)
                for half in range(2):
                    E("pe", lambda e, ys=ys, half=half, sbk=sbk: e.matmul(
                        pb[sbk][0:1, 0:512], s_ocb.ap(),
                        ys[:, half * 512:(half + 1) * 512],
                        start=True, stop=True),
                      [ysn, "s_ocb"], [f"pb{sbk}"], tag=f"st{half}_{hp}")
                    if hp == 0:
                        E("dve", lambda e, half=half, sbk=sbk: e.tensor_copy(
                            s_rsum[0:1, half * 512:(half + 1) * 512],
                            pb[sbk][0:1, 0:512]),
                          [f"pb{sbk}"], [f"s_rsum#{half}"], tag=f"sta{half}_{hp}")
                    else:
                        E("dve", lambda e, half=half, sbk=sbk: e.tensor_tensor(
                            s_rsum[0:1, half * 512:(half + 1) * 512],
                            s_rsum[0:1, half * 512:(half + 1) * 512],
                            pb[sbk][0:1, 0:512], ALU.add),
                          [f"pb{sbk}", f"s_rsum#{half}"], [f"s_rsum#{half}"],
                          tag=f"sta{half}_{hp}")

    NITB = 32 * 16
    for k in range(NITB + 2):
        if k < NITB:
            p2b_s0(k)
        if 1 <= k <= NITB:
            p2b_s1(k - 1)
        if 2 <= k <= NITB + 1:
            p2b_s2(k - 2)
    tap("yg", lambda: s_yg.ap(), [f"s_yg#{hp}" for hp in range(32)])

    if upto < PH_P3M:
        return _finish(nc, bld, tap_t, ar, arena_addr, B)

    for half in range(2):
        E("act", lambda e, half=half: e.activation(
            s_r1[0:1, half * 512:(half + 1) * 512],
            s_rsum[0:1, half * 512:(half + 1) * 512], AF.Sqrt,
            scale=1.0 / INTER, bias=s_kc[0:1, 0:1]),
          [f"s_rsum#{half}", "s_kc"], ["s_r1"], tag=f"rms_s{half}")
    E("dve", lambda e: e.reciprocal(s_r1[0:1, NT:2 * NT], s_r1[0:1, 0:NT]),
      ["s_r1"], ["s_r1"], tag="rms_r")
    for half, bk in ((0, 4), (1, 5)):
        E("pe", lambda e, half=half, bk=bk: e.matmul(
            pb[bk][:, 0:512], s_orf.ap(),
            s_r1[0:1, NT + half * 512: NT + (half + 1) * 512],
            start=True, stop=True),
          ["s_r1", "s_orf"], [f"pb{bk}"], tag=f"rb_{half}")
        E("dve", lambda e, half=half, bk=bk: e.tensor_copy(
            s_rb[:, half * 512:(half + 1) * 512], pb[bk][:, 0:512]),
          [f"pb{bk}"], ["s_rb"], tag=f"rbe_{half}")
    for j in range(32):
        E("dve", lambda e, j=j: e.tensor_tensor(
            s_yg[:, j * NT:(j + 1) * NT], s_yg[:, j * NT:(j + 1) * NT],
            s_rb.ap(), ALU.mult),
          [f"s_yg#{j}", "s_rb"], [f"s_yg#{j}"], tag=f"yn_{j}")

    def p3_load(gi):
        wg_ = s_wopg[gi % 2]
        for icg in range(4):
            ic = gi * 4 + icg
            ws = s_wops[ic % 2]
            wsn = f"s_wops{ic % 2}"
            D("sp", lambda e, ic=ic, ws=ws: e.dma_start(
                out=ws.ap(), in_=d_wop[ic * 128:(ic + 1) * 128, :]),
              [], [wsn], tag=f"wold{ic}", fam=wsn)
            E("dve", lambda e, ic=ic, icg=icg, ws=ws, wg_=wg_: e.tensor_scalar(
                wg_[:, icg * DM:(icg + 1) * DM], ws.ap(), s_nw[:, ic:ic + 1],
                None, ALU.mult),
              [wsn, "s_nw"], [f"s_wopg{gi % 2}"], tag=f"woc{ic}")

    def p3_mms(gi):
        wg_ = s_wopg[gi % 2]
        first, last = True, True
        for t in range(8):
            for eq in range(4):
                bk = (t * 4 + eq) % 4
                for icg in range(4):
                    ic = gi * 4 + icg
                    E("pe", lambda e, t=t, eq=eq, icg=icg, ic=ic, bk=bk, wg_=wg_,
                      first=first, last=last: e.matmul(
                        pb[bk][0:128, 0:512],
                        s_yg[:, ic * NT + t * 128: ic * NT + t * 128 + 128],
                        wg_[:, icg * DM + eq * 512: icg * DM + eq * 512 + 512],
                        start=(first and icg == 0), stop=(last and icg == 3)),
                      [f"s_yg#{ic}", f"s_wopg{gi % 2}"], [f"pb{bk}"],
                      tag=f"op{gi}_{t}_{eq}")
                if not last:
                    continue
                osl = slice(t * DM + eq * 512, t * DM + eq * 512 + 512)
                if gi == 0:
                    E("dve", lambda e, osl=osl, bk=bk: e.tensor_copy(
                        s_oacc[:, osl], pb[bk][0:128, 0:512]),
                      [f"pb{bk}"], [f"s_oacc#{osl.start // DM}"],
                      tag=f"oe{gi}_{t}_{eq}")
                else:
                    E("dve", lambda e, osl=osl, bk=bk: e.tensor_tensor(
                        s_oacc[:, osl], s_oacc[:, osl], pb[bk][0:128, 0:512],
                        ALU.add),
                      [f"pb{bk}", f"s_oacc#{osl.start // DM}"],
                      [f"s_oacc#{osl.start // DM}"], tag=f"oe{gi}_{t}_{eq}")

    for gi in range(9):
        if gi < 8:
            p3_load(gi)
        if gi >= 1:
            p3_mms(gi - 1)
    tap("oacc", lambda: s_oacc.ap(), [f"s_oacc#{t}" for t in range(8)])

    if upto < PH_P3T:
        return _finish(nc, bld, tap_t, ar, arena_addr, B)

    for t in range(8):
        i2 = t % 2
        oa = s_oacc[:, t * DM:(t + 1) * DM]
        oan = f"s_oacc#{t}"
        xr, t1, x1, xl2 = s_xrd[i2], s_t1[i2], s_x1b[i2], s_xl2[i2]
        xrn, x1n, xl2n = f"s_xrd{i2}", f"s_x1b{i2}", f"s_xl2{i2}"
        D("sp", lambda e, t=t, xr=xr: e.dma_start(
            out=xr.ap(), in_=d_x[128 + t * 128: 256 + t * 128, :]),
          [], [xrn], tag=f"xrd{t}", fam=xrn)
        cb = i2 * 8
        ln_stats(oa, [oan], s_sqb[i2].ap(), f"s_sqb{i2}", f"ln2_{t}", cb=cb)
        E("dve", lambda e, oa=oa, t1=t1, cb=cb: e.tensor_scalar(
            t1.ap(), oa, s_col[:, cb + 2:cb + 3], s_col[:, cb + 7:cb + 8],
            ALU.subtract, ALU.mult),
          [oan, f"s_col#{cb + 2}", f"s_col#{cb + 7}"], [f"s_t1{i2}"],
          tag=f"ln2a_{t}")
        E("dve", lambda e, t1=t1, xr=xr, x1=x1: e.tensor_tensor(
            x1.ap(), t1.ap(), xr.ap(), ALU.add),
          [f"s_t1{i2}", xrn], [x1n], tag=f"res1_{t}")
        D("sp", lambda e, t=t, x1=x1: e.dma_start(
            out=d_x1[t * 128:(t + 1) * 128, :], in_=x1.ap()),
          [x1n], [f"dx1_{t % 2}"], tag=f"x1st{t}", fam=f"dx1{t % 2}")
        ln_stats(x1.ap(), [x1n], s_sqb[i2].ap(), f"s_sqb{i2}", f"lnf_{t}", cb=cb)
        E("dve", lambda e, x1=x1, xl2=xl2, cb=cb: e.tensor_scalar(
            xl2.ap(), x1.ap(), s_col[:, cb + 2:cb + 3], s_col[:, cb + 7:cb + 8],
            ALU.subtract, ALU.mult),
          [x1n, f"s_col#{cb + 2}", f"s_col#{cb + 7}"], [xl2n], tag=f"lnfa_{t}")
        for kc in range(KD):
            bk = 6 + (kc % 2)
            E("pe", lambda e, xl2=xl2, kc=kc, bk=bk: e.transpose(
                pbb[bk][:, 0:128], xl2[:, kc * 128:(kc + 1) * 128], s_idb.ap()),
              [xl2n, "s_idb"], [f"pb{bk}"], tag="x2t")
            E("dve", lambda e, kc=kc, t=t, bk=bk: e.tensor_copy(
                s_xl2T[:, kc * NT + t * 128: kc * NT + (t + 1) * 128],
                pbb[bk][:, 0:128]),
              [f"pb{bk}"], [f"s_xl2T#{kc}"], tag="x2te")

    if upto < PH_P4A:
        return _finish(nc, bld, tap_t, ar, arena_addr, B)

    def p4a_quant(ei):
        i2 = ei % 2
        qu, qg = s_qu[i2], s_qg[i2]
        qun, qgn = f"s_qu{i2}", f"s_qg{i2}"
        for (dsrc, stg, stgn, qq, qqn, thr_c, nthr_c, on_act) in (
                (d_wu, s_ust, "s_ust", qu, qun, 3, 8, True),
                (d_wg, s_gst, "s_gst", qg, qgn, 4, 9, False)):
            src = dsrc.ap()[:, ei * 128:(ei + 1) * 128].rearrange(
                "(kc kp) f -> kp kc f", kp=128)
            for halfk in range(2):
                st = stg[halfk]
                stn = f"{stgn}{halfk}"
                hs_ = slice(halfk * 1024, (halfk + 1) * 1024)
                D("sp", lambda e, src=src, st=st, halfk=halfk: e.dma_start(
                    out=st.ap().rearrange("p (kc f) -> p kc f", f=128),
                    in_=src[:, halfk * 8:(halfk + 1) * 8, :]),
                  [], [stn], tag=f"uld{ei}", fam=stn)
                if on_act:
                    E("act", lambda e, st=st, qq=qq, hs_=hs_, nthr_c=nthr_c:
                      e.activation(
                        qq[:, hs_], st.ap(), AF.Sign,
                        bias=s_bcols[:, nthr_c:nthr_c + 1]),
                      [stn, "s_bcols"], [qqn], tag=f"sg1{ei}")
                    E("act", lambda e, st=st, thr_c=thr_c: e.activation(
                        s_q2[:, 0:1024], st.ap(), AF.Sign,
                        bias=s_bcols[:, thr_c:thr_c + 1]),
                      [stn, "s_bcols"], ["s_q2"], tag=f"sg2{ei}")
                    E("dve", lambda e, qq=qq, hs_=hs_: e.tensor_tensor(
                        qq[:, hs_], qq[:, hs_], s_q2[:, 0:1024], ALU.add),
                      [qqn, "s_q2"], [qqn], tag=f"sga{ei}")
                else:
                    E("dve", lambda e, st=st, qq=qq, hs_=hs_, thr_c=thr_c:
                      e.tensor_scalar(
                        qq[:, hs_], st.ap(), s_bcols[:, thr_c:thr_c + 1], None,
                        ALU.is_gt),
                      [stn, "s_bcols"], [qqn], tag=f"gg1{ei}")
                    E("dve", lambda e, st=st, nthr_c=nthr_c: e.tensor_scalar(
                        s_q2[:, 0:1024], st.ap(), s_bcols[:, nthr_c:nthr_c + 1],
                        None, ALU.is_lt),
                      [stn, "s_bcols"], ["s_q2"], tag=f"gg2{ei}")
                    E("dve", lambda e, qq=qq, hs_=hs_: e.tensor_tensor(
                        qq[:, hs_], qq[:, hs_], s_q2[:, 0:1024], ALU.subtract),
                      [qqn, "s_q2"], [qqn], tag=f"gga{ei}")

    def p4a_mm(ei):
        i2 = ei % 2
        qu, qg = s_qu[i2], s_qg[i2]
        qun, qgn = f"s_qu{i2}", f"s_qg{i2}"
        b0 = i2 * 4
        for kc in range(KD):
            st_, sp_ = (kc == 0), (kc == KD - 1)
            for tt in range(2):
                tsl = slice(kc * NT + tt * 512, kc * NT + tt * 512 + 512)
                E("pe", lambda e, qu=qu, kc=kc, tsl=tsl, b=b0 + tt, st_=st_,
                  sp_=sp_: e.matmul(
                    pb[b][0:128, 0:512], qu[:, kc * 128:(kc + 1) * 128],
                    s_xl2T[:, tsl], start=st_, stop=sp_),
                  [qun, f"s_xl2T#{kc}"], [f"pb{b0 + tt}"], tag=f"ump{ei}")
                E("pe", lambda e, qg=qg, kc=kc, tsl=tsl, b=b0 + 2 + tt, st_=st_,
                  sp_=sp_: e.matmul(
                    pb[b][0:128, 0:512], qg[:, kc * 128:(kc + 1) * 128],
                    s_xl2T[:, tsl], start=st_, stop=sp_),
                  [qgn, f"s_xl2T#{kc}"], [f"pb{b0 + 2 + tt}"], tag=f"gmp{ei}")
        for tt in range(2):
            hsb = s_hs2[tt]
            E("act", lambda e, hsb=hsb, b=b0 + tt: e.activation(
                hsb.ap(), pb[b][0:128, 0:512], AF.Silu, scale=s_bcols[:, 3:4]),
              [f"pb{b0 + tt}", "s_bcols"], [f"s_hs2{tt}"], tag=f"usi{ei}")
            E("dve", lambda e, hsb=hsb, ei=ei, tt=tt, b=b0 + 2 + tt:
              e.tensor_tensor(
                s_hid[:, ei * NT + tt * 512: ei * NT + tt * 512 + 512],
                hsb.ap(), pb[b][0:128, 0:512], ALU.mult),
              [f"s_hs2{tt}", f"pb{b0 + 2 + tt}"], [f"s_hid#{ei}"], tag=f"uhm{ei}")

    for ei in range(65):
        if ei < 64:
            p4a_quant(ei)
        if ei >= 1:
            p4a_mm(ei - 1)
    tap("hid", lambda: s_hid.ap(), [f"s_hid#{ei}" for ei in range(64)])

    if upto < PH_P4B:
        return _finish(nc, bld, tap_t, ar, arena_addr, B)

    def p4b_quant(dq, i):
        i3 = i % 3
        i2 = i % 2
        stg, qd, q2 = s_dst[i3], s_qd[i3], s_q2d[i2]
        stn, qdn, q2n = f"s_dst{i3}", f"s_qd{i3}", f"s_q2d{i2}"
        D("sp", lambda e, i=i, dq=dq, stg=stg: e.dma_start(
            out=stg.ap(),
            in_=d_wd[i * 128:(i + 1) * 128, dq * 512:(dq + 1) * 512]),
          [], [stn], tag=f"dld{dq}_{i}", fam=stn)
        E("act", lambda e, stg=stg, qd=qd: e.activation(
            qd.ap(), stg.ap(), AF.Sign, bias=s_bcols[:, 10:11]),
          [stn, "s_bcols"], [qdn], tag=f"dsg1{dq}_{i}")
        E("act", lambda e, stg=stg, q2=q2: e.activation(
            q2.ap(), stg.ap(), AF.Sign, bias=s_bcols[:, 5:6]),
          [stn, "s_bcols"], [q2n], tag=f"dsg2{dq}_{i}")
        E("dve", lambda e, qd=qd, q2=q2: e.tensor_tensor(
            qd.ap(), qd.ap(), q2.ap(), ALU.add),
          [qdn, q2n], [qdn], tag=f"dsga{dq}_{i}")

    def p4b_mm(dq, i):
        i3 = i % 3
        qd = s_qd[i3]
        qdn = f"s_qd{i3}"
        for t in range(8):
            E("pe", lambda e, i=i, t=t, qd=qd: e.matmul(
                pb[t][0:128, 0:512],
                s_hid[:, i * NT + t * 128: i * NT + t * 128 + 128],
                qd.ap(), start=(i == 0), stop=(i == 63)),
              [qdn, f"s_hid#{i}"], [f"pb{t}"], tag=f"dmm{dq}_{i}_{t}")

    for dq in range(4):
        for i in range(66):
            if i < 64:
                p4b_quant(dq, i)
            if i >= 2:
                p4b_mm(dq, i - 2)
        for t in range(8):
            i2 = t % 2
            dev = s_dev[i2]
            E("dve", lambda e, t=t, dev=dev: e.tensor_scalar(
                dev.ap(), pb[t][0:128, 0:512], s_bcols[:, 7:8], None, ALU.mult),
              [f"pb{t}", "s_bcols"], [f"s_dev{i2}"], tag=f"dev{dq}_{t}")
            D("sp", lambda e, t=t, dq=dq, dev=dev: e.dma_start(
                out=d_da[t * 128:(t + 1) * 128, dq * 512:(dq + 1) * 512],
                in_=dev.ap()),
              [f"s_dev{i2}"], [f"dda{t % 2}"], tag=f"dst{dq}_{t}",
              fam=f"dda{t % 2}")
            if dq == 0:
                E("dve", lambda e, t=t, dev=dev: e.tensor_reduce(
                    s_l3s[:, t:t + 1], dev.ap(), AX.X, ALU.add),
                  [f"s_dev{i2}"], [f"s_l3s#{t}"], tag=f"l3s{dq}_{t}")
                E("act", lambda e, t=t, dev=dev, i2=i2: e.activation(
                    s_dsq[i2].ap(), dev.ap(), AF.Square,
                    accum_out=s_l3q[:, t:t + 1]),
                  [f"s_dev{i2}"], [f"s_dsq{i2}", f"s_l3q#{t}"], tag=f"l3q{dq}_{t}")
            else:
                E("dve", lambda e, t=t, dev=dev: e.tensor_reduce(
                    s_l3s[:, 8 + t:9 + t], dev.ap(), AX.X, ALU.add),
                  [f"s_dev{i2}"], ["l3tmp"], tag=f"l3s{dq}_{t}")
                E("dve", lambda e, t=t: e.tensor_tensor(
                    s_l3s[:, t:t + 1], s_l3s[:, t:t + 1], s_l3s[:, 8 + t:9 + t],
                    ALU.add),
                  ["l3tmp", f"s_l3s#{t}"], [f"s_l3s#{t}"], tag=f"l3sa{dq}_{t}")
                E("act", lambda e, t=t, dev=dev, i2=i2: e.activation(
                    s_dsq[i2].ap(), dev.ap(), AF.Square,
                    accum_out=s_l3q[:, 8 + t:9 + t]),
                  [f"s_dev{i2}"], [f"s_dsq{i2}", "l3qtmp"], tag=f"l3q{dq}_{t}")
                E("dve", lambda e, t=t: e.tensor_tensor(
                    s_l3q[:, t:t + 1], s_l3q[:, t:t + 1], s_l3q[:, 8 + t:9 + t],
                    ALU.add),
                  ["l3qtmp", f"s_l3q#{t}"], [f"s_l3q#{t}"], tag=f"l3qa{dq}_{t}")

    tap("dacc", lambda: d_da.ap(), ["dda0", "dda1"])
    tap("l3s", lambda: s_l3s.ap(), [f"s_l3s#{t}" for t in range(8)])
    tap("l3q", lambda: s_l3q.ap(), [f"s_l3q#{t}" for t in range(8)])
    out_evs = []
    for t in range(8):
        i2 = t % 2
        dar, x1r, fin = s_dar[i2], s_x1r[i2], s_fin[i2]
        darn, x1n, finn = f"s_dar{i2}", f"s_x1r{i2}", f"s_fin{i2}"
        D("sp", lambda e, t=t, dar=dar: e.dma_start(
            out=dar.ap(), in_=d_da[t * 128:(t + 1) * 128, :]),
          ["dda0", "dda1"], [darn], tag=f"dald{t}", fam=darn)
        D("act", lambda e, t=t, x1r=x1r: e.dma_start(
            out=x1r.ap(), in_=d_x1[t * 128:(t + 1) * 128, :]),
          ["dx1_0", "dx1_1"], [x1n], tag=f"x1ld{t}", fam=x1n)
        cb = i2 * 8
        E("dve", lambda e, t=t, cb=cb: e.tensor_scalar(
            s_col[:, cb + 2:cb + 3], s_l3s[:, t:t + 1], 1.0 / DM, None, ALU.mult),
          [f"s_l3s#{t}"], [f"s_col#{cb + 2}"], tag=f"l3m_{t}")
        E("dve", lambda e, cb=cb: e.tensor_tensor(
            s_col[:, cb + 4:cb + 5], s_col[:, cb + 2:cb + 3],
            s_col[:, cb + 2:cb + 3], ALU.mult),
          [f"s_col#{cb + 2}"], [f"s_col#{cb + 4}"], tag=f"l3m2_{t}")
        E("dve", lambda e, t=t, cb=cb: e.tensor_scalar(
            s_col[:, cb + 5:cb + 6], s_l3q[:, t:t + 1], 1.0 / DM, None, ALU.mult),
          [f"s_l3q#{t}"], [f"s_col#{cb + 5}"], tag=f"l3v1_{t}")
        E("dve", lambda e, cb=cb: e.tensor_tensor(
            s_col[:, cb + 5:cb + 6], s_col[:, cb + 5:cb + 6],
            s_col[:, cb + 4:cb + 5], ALU.subtract),
          [f"s_col#{cb + 4}", f"s_col#{cb + 5}"], [f"s_col#{cb + 5}"],
          tag=f"l3v2_{t}")
        E("act", lambda e, cb=cb: e.activation(
            s_col[:, cb + 6:cb + 7], s_col[:, cb + 5:cb + 6], AF.Sqrt,
            bias=s_kc[:, 0:1]),
          [f"s_col#{cb + 5}", "s_kc"], [f"s_col#{cb + 6}"], tag=f"l3sd_{t}")
        E("dve", lambda e, cb=cb: e.reciprocal(
            s_col[:, cb + 7:cb + 8], s_col[:, cb + 6:cb + 7]),
          [f"s_col#{cb + 6}"], [f"s_col#{cb + 7}"], tag=f"l3r_{t}")
        E("dve", lambda e, dar=dar, fin=fin, cb=cb: e.tensor_scalar(
            fin.ap(), dar.ap(), s_col[:, cb + 2:cb + 3], s_col[:, cb + 7:cb + 8],
            ALU.subtract, ALU.mult),
          [darn, f"s_col#{cb + 2}", f"s_col#{cb + 7}"], [finn], tag=f"ln3a_{t}")
        E("dve", lambda e, fin=fin, x1r=x1r: e.tensor_tensor(
            fin.ap(), fin.ap(), x1r.ap(), ALU.add),
          [finn, x1n], [finn], tag=f"res2_{t}")
        ev = D("pool", lambda e, t=t, fin=fin: e.dma_start(
            out=d_out[t * 128:(t + 1) * 128, :], in_=fin.ap()),
          [finn], [f"dout{t}"], tag=f"outst{t}", fam=f"dout{t % 2}")
        out_evs.append(ev)
    bld.op("sp", lambda e: e.nop(), [], [], set(out_evs), tag="final_wait")

    return _finish(nc, bld, tap_t, ar, arena_addr, B)


def _finish(nc, bld, tap_t, ar, arena_addr, B):
    import contextlib
    items = ar.commit(arena_addr)
    by_name = {}
    for (off, size, ph0, ph1, name) in items:
        by_name[name] = (off, size, ph0, ph1)
    owners = {}
    for bn, buf in B.items():
        owner = bn.split("#")[0]
        if owner in by_name:
            owners.setdefault(owner, []).append(buf)
    names = list(by_name)
    for i in range(len(names)):
        for j in range(len(names)):
            if i == j:
                continue
            a, b = names[i], names[j]
            (ao, asz, ap0, ap1) = by_name[a]
            (bo, bsz, bp0, bp1) = by_name[b]
            if ao + asz <= bo or bo + bsz <= ao:
                continue
            if not (ap1 < bp0):
                continue
            last_evs = set()
            for buf in owners.get(a, ()):
                if buf.last_w is not None:
                    last_evs.add(buf.last_w)
                last_evs.update(buf.readers)
            for buf in owners.get(b, ()):
                if buf.first_w_op is not None:
                    eng, idx = buf.first_w_op
                    bld.ops[eng][idx].deps.update(last_evs)

    ctx = contextlib.ExitStack()
    sems = {e: ctx.enter_context(nc.semaphore(f"sem_{e}")) for e in ENGINES}
    dma_sems = {f: ctx.enter_context(nc.semaphore(f"dsem_{f}"))
                for f in bld.nfam}
    cc_sem = ctx.enter_context(nc.semaphore("ccsem"))
    with ctx, nc.Block() as block:
        bld.finalize(block, sems, dma_sems, cc_sem)
    return nc, tap_t


def prep_inputs(inputs):
    _ = inputs.get("L0_attn_mask")
    x = np.asarray(inputs["L0_attn_input_states"], dtype=np.float32)
    wip_t = np.ascontiguousarray(np.asarray(inputs["L0_attn_in_proj_w"], np.float32).T)
    wop_t = np.ascontiguousarray(np.asarray(inputs["L0_attn_out_proj_w"], np.float32).T)
    wu_t = np.ascontiguousarray(np.asarray(inputs["L0_ffn_W_up"], np.float32).T)
    wg_t = np.ascontiguousarray(np.asarray(inputs["L0_ffn_W_gate"], np.float32).T)
    wd_t = np.ascontiguousarray(np.asarray(inputs["L0_ffn_W_down"], np.float32).T)
    cw = np.asarray(inputs["L0_attn_conv_w"], np.float32).reshape(4352, 4)
    cw_pp = np.zeros((128, 136), np.float32)
    for j in range(34):
        cw_pp[:, j * 4:(j + 1) * 4] = cw[j * 128:(j + 1) * 128, :]
    cb = np.asarray(inputs["L0_attn_conv_b"], np.float32)
    cb_pp = np.ascontiguousarray(cb.reshape(34, 128).T)
    alog = np.asarray(inputs["L0_attn_A_log"], np.float32).reshape(64, 1)
    dtb = np.asarray(inputs["L0_attn_dt_bias"], np.float32).reshape(64, 1)
    drow = np.asarray(inputs["L0_attn_D"], np.float32).reshape(1, 64)
    nw = np.ascontiguousarray(
        np.asarray(inputs["L0_attn_norm_w"], np.float32).reshape(32, 128).T)

    in_maps = []
    for r in range(8):
        seq, half = r // 2, r % 2
        xs = np.zeros((NROWS, DM), np.float32)
        tok0 = half * NT
        xs[128:, :] = x[seq, tok0:tok0 + NT, :]
        if half == 1:
            xs[125:128, :] = x[seq, NT - 3:NT, :]
        e0m = np.zeros((1, 8), np.float32)
        if half == 1:
            e0m[0, r - 1] = 1.0
        in_maps.append({
            "x": np.ascontiguousarray(xs),
            "wip_t": wip_t, "wop_t": wop_t,
            "wu_t": wu_t, "wg_t": wg_t, "wd_t": wd_t,
            "wu_s": np.ascontiguousarray(wu_t[r * 256:(r + 1) * 256, :]),
            "wg_s": np.ascontiguousarray(wg_t[r * 256:(r + 1) * 256, :]),
            "wd_s": np.ascontiguousarray(wd_t[r * 1024:(r + 1) * 1024, :]),
            "conv_w": cw_pp, "conv_b": cb_pp,
            "alog": alog, "dtb": dtb, "d_row": drow, "normw": nw,
            "e0m": e0m,
        })
    return in_maps


def kernel(**inputs):
    in_maps = prep_inputs(inputs)
    nc, _ = build_program()
    res = run_bass_kernel_spmd(nc, in_maps, core_ids=list(range(8)))
    out = np.zeros((4, 2048, 2048), np.float32)
    for r in range(8):
        seq, half = r // 2, r % 2
        out[seq, half * NT:(half + 1) * NT, :] = res.results[r]["out"]
    return out
